# version 07
# speedup vs baseline: 1.3051x; 1.0119x over previous
import numpy as np
from contextlib import ExitStack

import concourse.bass as bass
import concourse.mybir as mybir
from concourse.bass_utils import run_bass_kernel_spmd

F32 = mybir.dt.float32
BF16 = mybir.dt.bfloat16
AF = mybir.ActivationFunctionType
ALU = mybir.AluOpType
AX = mybir.AxisListType

D = 1024
T = 2048
DC = D // 128
TGS = 512
NTG = T // TGS
EPS_LN = 1e-5
ATT_HEADS = list(range(16))


def cdiv(a, b):
    return -(-a // b)


class Sem:
    def __init__(self, handle):
        self.h = handle
        self.n = 0

    def tag(self, ins, k=1):
        self.n += k
        ins.then_inc(self.h, k)
        return (self.h, self.n)

    def mark(self):
        return (self.h, self.n)


def wait(e, *marks):
    for m in marks:
        if m is None:
            continue
        if isinstance(m, (list, tuple)) and m and not isinstance(m[0], tuple) \
                and not hasattr(m[0], "num"):
            e.wait_ge(m[0], m[1])
        elif isinstance(m, list):
            wait(e, *m)
        else:
            e.wait_ge(m[0], m[1])


class DmaQ:

    def __init__(self, b, name, R=4):
        self.sems = [b.slot_sem(name) for _ in range(R)]
        self.i = 0

    def begin(self, e):
        s = self.sems[self.i % len(self.sems)]
        self.i += 1
        if s.n > 0:
            e.wait_ge(s.h, s.n)
        return s


class B:

    def __init__(self, nc, ctx, blk):
        self.nc = nc
        self.ctx = ctx
        self.blk = blk
        self.pe = Sem(ctx.enter_context(nc.semaphore("s_pe")))
        self.vec = Sem(ctx.enter_context(nc.semaphore("s_vec")))
        self.sca = Sem(ctx.enter_context(nc.semaphore("s_sca")))
        self.pool = Sem(ctx.enter_context(nc.semaphore("s_pool")))
        self._slot_ctr = [0]
        self._s_q = None
        self.yqq = DmaQ(self, "s_yq", 4)
        self.gqq = DmaQ(self, "s_gq", 4)

    def slot_sem(self, name):
        self._slot_ctr[0] += 1
        return Sem(self.ctx.enter_context(
            self.nc.semaphore(f"{name}{self._slot_ctr[0]}")))

    def sub(self, ph, nc=None):
        nb = B.__new__(B)
        nb.__dict__.update(self.__dict__)
        nb.ctx = ph
        nb._s_q = None
        if nc is not None:
            nb.nc = nc
        return nb

    def t(self, f):
        self.blk.tensor(f)

    def v(self, f):
        self.blk.vector(f)

    def s(self, f):
        if self._s_q is not None:
            self._s_q.append(f)
        else:
            self.blk.scalar(f)

    def s_defer(self):
        assert self._s_q is None
        self._s_q = []

    def s_flush(self):
        q, self._s_q = self._s_q, None
        for f in q:
            self.blk.scalar(f)

    def sca_reserve(self, k=1):
        self.sca.n += k
        return (self.sca.h, self.sca.n)

    def g(self, f):
        self.blk.gpsimd(f)

    def y(self, f):
        self.blk.sync(f)


class NCP:

    def __init__(self, nc, prefix):
        self._nc = nc
        self._p = prefix

    def sbuf_tensor(self, name, *a, **k):
        return self._nc.sbuf_tensor(self._p + name, *a, **k)

    def psum_tensor(self, name, *a, **k):
        return self._nc.psum_tensor(self._p + name, *a, **k)

    def dram_tensor(self, name, *a, **k):
        return self._nc.dram_tensor(self._p + name, *a, **k)

    def semaphore(self, name=None, num=None):
        return self._nc.semaphore((self._p + name) if name else name, num)

    def __getattr__(self, a):
        return getattr(self._nc, a)


def block_barrier(b, mark):
    for emit in (b.t, b.v, b.s, b.g, b.y):
        emit(lambda e: wait(e, mark))


def _t(a):
    return np.ascontiguousarray(np.asarray(a).T)


def _cols(v):
    return np.ascontiguousarray(np.asarray(v).reshape(-1, 128).T)


CONST_SCALARS = [EPS_LN, 1.0]


def make_consts():
    c = {}
    c["c_ones_sq"] = np.ones((128, 128), dtype=np.float32)
    c["c_ones_col"] = np.ones((128, 1), dtype=np.float32)
    c["c_ones_row"] = np.ones((1, 128), dtype=np.float32)
    c["c_negones_row"] = -np.ones((1, 128), dtype=np.float32)
    c["c_scal"] = np.tile(np.asarray(CONST_SCALARS, np.float32), (128, 1))
    c["c_ident"] = np.eye(128, dtype=np.float32)
    i = np.arange(128)[:, None]
    j = np.arange(TGS)[None, :]
    for r in range(4):
        c[f"c_mask{r}"] = ((i + 128 * r) <= j).astype(np.float32)
    return c


CONST_NAMES = list(make_consts().keys())


class ConstTiles:
    def __init__(self, b, dr):
        nc = b.nc
        sb = lambda n, s, dt: b.ctx.enter_context(nc.sbuf_tensor(n, s, dt))
        self.ones_sq = sb("k_ones_sq", [128, 128], F32)
        self.ones_col = sb("k_ones_col", [128, 1], BF16)
        self.ones_row = sb("k_ones_row", [1, 128], BF16)
        self.negones_row = sb("k_negones_row", [1, 128], BF16)
        self.ones_row_f = sb("k_ones_row_f", [1, 128], F32)
        self.masks = [sb(f"k_mask{r}", [128, TGS], BF16) for r in range(4)]
        self.scal = sb("k_scal", [128, len(CONST_SCALARS)], F32)
        self.ident_f = sb("k_ident_f", [128, 128], F32)
        self.ident_b = sb("k_ident_b", [128, 128], BF16)
        for i, v in enumerate(CONST_SCALARS):
            nc.const_aps.register(F32, float(v), self.scal[:, i:i + 1])

        def g(e):
            grp = b.gqq.begin(e)
            for out, src in [(self.ones_sq, "c_ones_sq"), (self.ones_col, "c_ones_col"),
                             (self.ones_row, "c_ones_row"),
                             (self.negones_row, "c_negones_row"), (self.scal, "c_scal")]:
                self.ready = grp.tag(e.dma_start(out=out[:], in_=dr[src]), 16)
            self.ready = grp.tag(
                e.dma_start(out=self.ones_row_f[:], in_=dr["c_ones_row"]), 16)
            self.ready = grp.tag(
                e.dma_start(out=self.ident_f[:], in_=dr["c_ident"]), 16)
            self.ready = grp.tag(
                e.dma_start(out=self.ident_b[:], in_=dr["c_ident"]), 16)
            for r in range(4):
                self.ready = grp.tag(
                    e.dma_start(out=self.masks[r][:], in_=dr[f"c_mask{r}"]), 16)
        b.g(g)

    def barrier(self, b, extra=()):
        mk = self.ready
        ex = list(extra)
        for emit in (b.t, b.v, b.s):
            emit(lambda e: wait(e, mk, *ex))


def load_cols(b, name, dr, ncols):
    sb = b.ctx.enter_context(b.nc.sbuf_tensor("k_" + name, [128, ncols], F32))
    mk = []

    def g(e):
        grp = b.gqq.begin(e)
        mk.append(grp.tag(e.dma_start(out=sb[:], in_=dr[name]), 16))
    b.g(g)
    return sb, mk[0]


class WPool:
    def __init__(self, b, name, cols_per_buf, nbuf=2, init_free=None):
        self.nbuf = nbuf
        self.bufs = [b.ctx.enter_context(
            b.nc.sbuf_tensor(f"{name}_w{i}", [128, cols_per_buf], BF16))
            for i in range(nbuf)]
        self.s_ld = [b.slot_sem(name + "_ld") for _ in range(nbuf)]
        self.freed = [init_free] * nbuf
        self.seq = 0


class WStream:
    def __init__(self, b, pool, dram, K, M, ncols):
        self.b = b
        self.pool = pool
        self.dram_r = dram.rearrange("(kc p) m -> p kc m", p=128)
        self.K = K
        self.KC = K // 128
        self.M = M
        self.ncols = ncols
        assert self.KC * ncols <= pool.bufs[0].shape[1]
        self.nblocks = cdiv(M, ncols)
        self.loaded = {}
        self.slot = {}

    def load(self, blkid):
        pool = self.pool
        slot = pool.seq % pool.nbuf
        pool.seq += 1
        self.slot[blkid] = slot
        buf, s_ld = pool.bufs[slot], pool.s_ld[slot]
        c0 = blkid * self.ncols
        c1 = min(self.M, c0 + self.ncols)
        KC, ncols = self.KC, self.ncols
        prior = pool.freed[slot]
        dram_r = self.dram_r

        def g(e):
            wait(e, prior)
            kgrp = min(8, KC)
            for k0 in range(0, KC, kgrp):
                k1 = min(KC, k0 + kgrp)
                self.loaded[blkid] = s_ld.tag(e.dma_start(
                    out=buf[:, k0 * ncols:k0 * ncols + (k1 - k0) * ncols].rearrange(
                        "p (kc m) -> p kc m", kc=k1 - k0),
                    in_=dram_r[:, k0:k1, c0:c1]), 16)
        self.b.g(g)

    def tile(self, blkid, kc, ml):
        buf = self.pool.bufs[self.slot[blkid]]
        base = kc * self.ncols + ml * 128
        return buf[:, base:base + 128]

    def rhs(self, blkid, kc, ncols=None):
        buf = self.pool.bufs[self.slot[blkid]]
        nc_ = ncols if ncols is not None else self.ncols
        return buf[:, kc * self.ncols: kc * self.ncols + nc_]

    def set_free(self, blkid, marker):
        self.pool.freed[self.slot[blkid]] = marker


class PsumRot:
    def __init__(self, b, name, n, shape=(128, TGS), init_free=None):
        self.tiles = [b.ctx.enter_context(b.nc.psum_tensor(f"{name}{i}", list(shape), F32))
                      for i in range(n)]
        self.free = [init_free] * n
        self.i = 0

    def get(self):
        idx = self.i % len(self.tiles)
        self.i += 1
        return idx, self.tiles[idx], self.free[idx]

    def set_free(self, idx, marker):
        self.free[idx] = marker


def fm_proj(b, ws, KC, rhs_ids, rhs_fn, evac, pr, mc_total, extra_waits=()):
    mpb = ws.ncols // 128
    nblk = cdiv(mc_total, mpb)
    first = True
    done = {}
    for blkid in range(nblk):
        ws.load(blkid)
        for ml in range(mpb):
            mc = blkid * mpb + ml
            if mc >= mc_total:
                break
            for rid in rhs_ids:
                pidx, ps, ps_free = pr.get()
                waits = [ws.loaded[blkid], ps_free]
                if first:
                    waits += list(extra_waits)
                    first = False

                def tmm(e, waits=waits, ps=ps, blkid=blkid, ml=ml, rid=rid):
                    wait(e, *waits)
                    for kc in range(KC):
                        ins = e.matmul(ps[:], ws.tile(blkid, kc, ml), rhs_fn(kc, rid),
                                       start=(kc == 0), stop=(kc == KC - 1))
                    self_mark = b.pe.tag(ins)
                    tmm.mark = self_mark
                b.t(tmm)
                ws.set_free(blkid, tmm.mark)
                mk = evac(mc, rid, ps, [tmm.mark])
                pr.set_free(pidx, mk)
                done[mk[0]] = mk
    return list(done.values())


def tm_proj(b, ws, KC, lhs_fn, evac, pr, tc_total=16, extra_waits=(),
            bias_row=None, ones_row=None):
    first = True
    done = {}
    for fb in range(ws.nblocks):
        ws.load(fb)
        ncols = min(ws.ncols, ws.M - fb * ws.ncols)
        for tc in range(tc_total):
            pidx, ps, ps_free = pr.get()
            waits = [ws.loaded[fb], ps_free]
            if first:
                waits += list(extra_waits)
                first = False

            def tmm(e, waits=waits, ps=ps, fb=fb, tc=tc, ncols=ncols):
                wait(e, *waits)
                for kc in range(KC):
                    ins = e.matmul(ps[:, 0:ncols], lhs_fn(kc, tc),
                                   ws.rhs(fb, kc, ncols), start=(kc == 0),
                                   stop=(kc == KC - 1 and bias_row is None))
                if bias_row is not None:
                    ins = e.matmul(ps[:, 0:ncols], ones_row[:],
                                   bias_row(fb), start=False, stop=True)
                tmm.mark = b.pe.tag(ins)
            b.t(tmm)
            ws.set_free(fb, tmm.mark)
            mk = evac(tc, fb, ps, [tmm.mark])
            pr.set_free(pidx, mk)
            done[mk[0]] = mk
    return list(done.values())


def emit_ln(b, x_dram, xn_sb, K):
    nc = b.nc
    with ExitStack() as ph:
        xt = ph.enter_context(nc.sbuf_tensor("ln_xt", [128, DC * TGS], F32))
        sq = ph.enter_context(nc.sbuf_tensor("ln_sq", [128, DC * TGS], F32))
        stat = ph.enter_context(nc.sbuf_tensor("ln_stat", [128, 4 * TGS], F32))
        mean, m2 = stat[:, 0:TGS], stat[:, TGS:2 * TGS]
        rstd, var = stat[:, 2 * TGS:3 * TGS], stat[:, 3 * TGS:4 * TGS]
        p_s1 = ph.enter_context(nc.psum_tensor("ln_ps1", [128, TGS], F32))
        p_s2 = ph.enter_context(nc.psum_tensor("ln_ps2", [128, TGS], F32))
        xn_mark = None

        for tg in range(NTG):
            prev_done = xn_mark
            mks = {}

            def yload(e, prev_done=prev_done, tg=tg):
                wait(e, prev_done)
                grp = b.yqq.begin(e)
                for c in range(DC):
                    mks["x"] = grp.tag(e.dma_start(
                        out=xt[:, c * TGS:(c + 1) * TGS],
                        in_=x_dram[c * 128:(c + 1) * 128, tg * TGS:(tg + 1) * TGS]), 16)
            b.y(yload)

            def sact(e, prev_done=prev_done):
                wait(e, mks["x"], prev_done)
                for c in range(DC):
                    sl = slice(c * TGS, (c + 1) * TGS)
                    ins = e.activation(sq[:, sl], xt[:, sl], AF.Square)
                mks["sq"] = b.sca.tag(ins)
            b.s(sact)

            def tmm(e, prev_done=prev_done):
                wait(e, mks["x"], prev_done)
                for c in range(DC):
                    e.matmul(p_s1[:], K.ones_sq[:], xt[:, c * TGS:(c + 1) * TGS],
                             start=(c == 0), stop=(c == DC - 1))
                wait(e, mks["sq"])
                for c in range(DC):
                    ins = e.matmul(p_s2[:], K.ones_sq[:], sq[:, c * TGS:(c + 1) * TGS],
                                   start=(c == 0), stop=(c == DC - 1))
                mks["mm"] = b.pe.tag(ins)
            b.t(tmm)

            def vstat(e):
                wait(e, mks["mm"])
                e.drain()
                e.tensor_scalar_mul(mean, p_s1[:], 1.0 / D)
                e.drain()
                e.tensor_tensor(m2, mean, mean, ALU.mult)
                e.drain()
                mks["var"] = b.vec.tag(e.scalar_tensor_tensor(
                    var, p_s2[:], 1.0 / D, m2, ALU.mult, ALU.subtract))
            b.v(vstat)

            def sact2(e):
                wait(e, mks["var"])
                e.activation(rstd, var, AF.Ln, bias=EPS_LN)
                e.drain()
                mks["rstd"] = b.sca.tag(e.activation(rstd, rstd, AF.Exp, scale=-0.5))
            b.s(sact2)

            def vsub(e):
                wait(e, mks["rstd"])
                e.drain()
                for c in range(DC):
                    sl = slice(c * TGS, (c + 1) * TGS)
                    ins = e.scalar_tensor_tensor(sq[:, sl], mean, -1.0, xt[:, sl],
                                                 ALU.mult, ALU.add)
                mks["sub"] = b.vec.tag(ins)
            b.v(vsub)

            def pmul(e, tg=tg):
                wait(e, mks["sub"])
                e.drain()
                for c in range(DC):
                    ins = e.tensor_tensor(
                        xn_sb[:, c * T + tg * TGS: c * T + (tg + 1) * TGS],
                        sq[:, c * TGS:(c + 1) * TGS], rstd, ALU.mult)
                mks["xn"] = b.pool.tag(ins)
            b.g(pmul)
            xn_mark = mks["xn"]
    return xn_mark


def emit_attention_1h(b, K, q_sb, k_sb, v_sb, ctx_sb, scale, causal,
                      ready_waits, name="att", upto="full"):
    nc = b.nc
    with ExitStack() as ph:
        sqs = ph.enter_context(nc.sbuf_tensor(name + "_sqs", [128, T], BF16))
        nrm = ph.enter_context(nc.sbuf_tensor(name + "_nrm", [1, 16], F32))
        mb = ph.enter_context(nc.sbuf_tensor(name + "_mb", [128, 1], F32))
        mbb = ph.enter_context(nc.sbuf_tensor(name + "_mbb", [1, 1], BF16))
        pt = ph.enter_context(nc.sbuf_tensor(name + "_pt", [128, 16 * TGS], BF16))
        rb = ph.enter_context(nc.sbuf_tensor(name + "_rb", [128, TGS], F32))
        p_st = [ph.enter_context(nc.psum_tensor(f"{name}_pst{i}", [128, TGS], F32))
                for i in range(2)]
        p_ctx = [ph.enter_context(nc.psum_tensor(f"{name}_pc{i}", [128, TGS], F32))
                 for i in range(4)]
        p_ns = ph.enter_context(nc.psum_tensor(name + "_pns", [1, TGS], F32))
        p_misc = ph.enter_context(nc.psum_tensor(name + "_pmisc", [128, TGS], F32))
        p_n = p_ns
        p_sum = p_ns
        p_rb = p_misc
        p_mb = p_misc[:, 0:1]
        mks = {}

        col = 0
        for src_i, src in enumerate([q_sb, k_sb]):
            for tgq in range(NTG):
                for c in range(DC):
                    def sact(e, c=c, tgq=tgq, src=src, first=(col == 0 and c == 0)):
                        if first and src_i == 0:
                            wait(e, *ready_waits)
                        if "nq_mm" in mks:
                            wait(e, mks["nq_mm"])
                        tag_sl = src[:, c * T + tgq * TGS: c * T + (tgq + 1) * TGS]
                        mks["sq"] = b.sca.tag(e.activation(
                            sqs[:, tgq * TGS:(tgq + 1) * TGS], tag_sl, AF.Square))
                    b.s(sact)

                    def tmm(e, c=c, tgq=tgq):
                        wait(e, mks["sq"], mks.get("nred"))
                        mks["nq_mm"] = b.pe.tag(e.matmul(
                            p_n[:], K.ones_col[:], sqs[:, tgq * TGS:(tgq + 1) * TGS],
                            start=(c == 0), stop=(c == DC - 1)))
                    b.t(tmm)

                def vred(e, col=col):
                    wait(e, mks["nq_mm"])
                    e.drain()
                    mks["nred"] = b.vec.tag(e.reduce_max(
                        nrm[:, col:col + 1], p_n[:], AX.X))
                b.v(vred)
                col += 1

        def vcomb(e):
            wait(e, mks["nred"])
            e.drain()
            e.reduce_max(nrm[:, 8:9], nrm[:, 0:8], AX.X)
            e.drain()
            e.tensor_scalar_mul(nrm[:, 9:10], nrm[:, 8:9], scale)
            e.drain()
            mks["mbb"] = b.vec.tag(e.tensor_copy(mbb[:, 0:1], nrm[:, 9:10]))
        b.v(vcomb)

        def tmb(e):
            wait(e, mks["mbb"])
            mks["mb_mm"] = b.pe.tag(e.matmul(p_mb[:], K.negones_row[:], mbb[:, 0:1],
                                             start=True, stop=True))
        b.t(tmb)

        def vmb(e):
            wait(e, mks["mb_mm"])
            e.drain()
            mks["mb"] = b.vec.tag(e.tensor_copy(mb[:, 0:1], p_mb[:]))
        b.v(vmb)

        if upto == "bound":
            def vdbg(e):
                wait(e, mks["mb"])
                e.drain()
                mks["ctxev"] = b.vec.tag(e.tensor_copy(
                    ctx_sb[:, 0:1], mb[:, 0:1]))
            b.v(vdbg)
            return mks["ctxev"]

        ntgq = int(upto[2:]) if upto.startswith("tq") else NTG
        exp_marks = [None, None]
        b.s_defer()
        for tgq in range(ntgq):
            ntk = (tgq + 1) * (TGS // 128) if causal else T // 128
            pt_marks = []
            for tkc in range(ntk):
                pi = tkc % 2
                ps = p_st[pi]

                def tst(e, tkc=tkc, tgq=tgq, ps=ps, pi=pi, em=exp_marks[pi]):
                    wait(e, em, mks["mb"])
                    for kc in range(DC):
                        ins = e.matmul(
                            ps[:], k_sb[:, kc * T + tkc * 128: kc * T + (tkc + 1) * 128],
                            q_sb[:, kc * T + tgq * TGS: kc * T + (tgq + 1) * TGS],
                            start=(kc == 0), stop=(kc == DC - 1))
                    mks["st"] = b.pe.tag(ins)
                b.t(tst)

                em = b.sca_reserve()

                def sexp(e, tkc=tkc, ps=ps, w=(mks["st"],)):
                    wait(e, *w)
                    e.activation(
                        pt[:, tkc * TGS:(tkc + 1) * TGS], ps[:], AF.Exp,
                        bias=mb[:, 0:1], scale=scale).then_inc(b.sca.h, 1)
                b.s(sexp)
                mks["exp"] = em
                exp_marks[pi] = mks["exp"]
                pt_marks.append(mks["exp"])
                diag_r = tkc - tgq * (TGS // 128)
                if causal and 0 <= diag_r <= 3:
                    def vmask(e, tkc=tkc, diag_r=diag_r, em=mks["exp"]):
                        wait(e, em)
                        e.drain()
                        sl = slice(tkc * TGS, (tkc + 1) * TGS)
                        mks["mask"] = b.vec.tag(e.tensor_tensor(
                            pt[:, sl], pt[:, sl], K.masks[diag_r][:], ALU.mult))
                    b.v(vmask)
                    pt_marks.append(mks["mask"])

            pt_last = [m for m in (pt_marks[-1], mks.get("mask")) if m]
            for pas in range(2):
                def tav(e, tgq=tgq, pas=pas, ntk=ntk,
                        w=(pt_marks[-1], mks.get("mask"), mks.get("ctxev"))):
                    wait(e, *[m for m in w if m])
                    for tkc in range(ntk):
                        st, sp = (tkc == 0), (tkc == ntk - 1)
                        for m in range(4):
                            mc = pas * 4 + m
                            ins = e.matmul(
                                p_ctx[m][:],
                                v_sb[:, tkc * D + mc * 128: tkc * D + (mc + 1) * 128],
                                pt[:, tkc * TGS:(tkc + 1) * TGS], start=st, stop=sp)
                        if pas == 0:
                            ins = e.matmul(p_sum[:], K.ones_col[:],
                                           pt[:, tkc * TGS:(tkc + 1) * TGS],
                                           start=st, stop=sp)
                    mks["av"] = b.pe.tag(ins)
                b.t(tav)

                if pas == 0:
                    def vnorm(e):
                        wait(e, mks["av"])
                        e.drain()
                        mks["rec"] = b.vec.tag(e.reciprocal(rb[0:1, :], p_sum[:]))
                    b.v(vnorm)

                    def trb(e):
                        wait(e, mks["rec"])
                        mks["rb_mm"] = b.pe.tag(e.matmul(
                            p_rb[:], K.ones_row_f[:], rb[0:1, :], start=True, stop=True))
                    b.t(trb)

                    def vrb(e):
                        wait(e, mks["rb_mm"])
                        e.drain()
                        mks["rb"] = b.vec.tag(e.tensor_copy(rb[:, :], p_rb[:]))
                    b.v(vrb)

                    if upto == "norm1":
                        def vdbg(e):
                            wait(e, mks["rb"])
                            e.drain()
                            mks["ctxev"] = b.vec.tag(e.tensor_copy(
                                ctx_sb[:, 0:TGS], rb[:, :]))
                        b.v(vdbg)
                        b.s_flush()
                        return mks["ctxev"]

                if upto == "score1":
                    def vdbg(e):
                        wait(e, mks["av"])
                        e.drain()
                        mks["ctxev"] = b.vec.tag(e.tensor_copy(
                            ctx_sb[:, 0:TGS], pt[:, 0:TGS]))
                    b.v(vdbg)
                    b.s_flush()
                    return mks["ctxev"]

                def vev(e, tgq=tgq, pas=pas):
                    wait(e, mks["av"], mks["rb"])
                    e.drain()
                    for m in range(4):
                        mc = pas * 4 + m
                        ins = e.tensor_tensor(
                            ctx_sb[:, mc * T + tgq * TGS: mc * T + (tgq + 1) * TGS],
                            p_ctx[m][:], rb[:, :], ALU.mult)
                    mks["ctxev"] = b.vec.tag(ins)
                b.v(vev)
        b.s_flush()
    return mks["ctxev"]


VSTRIDE = 16 * 65 + 0


def emit_attention_mh(b, K, q_sb, k_sb, v_sb, ctx_sb, scale, causal,
                      ready_waits, name="mha"):
    nc = b.nc
    with ExitStack() as ph:
        sqs = ph.enter_context(nc.sbuf_tensor(name + "_sqs", [128, T], BF16))
        nrm = ph.enter_context(nc.sbuf_tensor(name + "_nrm", [1, 80], F32))
        mb = ph.enter_context(nc.sbuf_tensor(name + "_mb", [128, 1], F32))
        mbb = ph.enter_context(nc.sbuf_tensor(name + "_mbb", [1, 1], BF16))
        pt = ph.enter_context(nc.sbuf_tensor(name + "_pt", [128, 16 * TGS], BF16))
        rb = ph.enter_context(nc.sbuf_tensor(name + "_rb", [128, TGS], F32))
        NPST = 4
        p_st = [ph.enter_context(nc.psum_tensor(f"{name}_pst{i}", [128, TGS], F32))
                for i in range(NPST)]
        p_ctx = [ph.enter_context(nc.psum_tensor(f"{name}_pc{i}", [128, TGS], F32))
                 for i in range(2)]
        p_ns = ph.enter_context(nc.psum_tensor(name + "_pns", [1, TGS], F32))
        p_misc = ph.enter_context(nc.psum_tensor(name + "_pmisc", [128, TGS], F32))
        mks = {}

        col = 0
        for src_i, src in enumerate([q_sb, k_sb]):
            for c in range(DC):
                def sact(e, c=c, src=src, first=(col == 0)):
                    if first:
                        wait(e, *ready_waits)
                    if "bnd_mm" in mks:
                        wait(e, mks["bnd_mm"])
                    mks["sq"] = b.sca.tag(e.activation(sqs[:, :],
                                                       src[:, c * T:(c + 1) * T],
                                                       AF.Square))
                b.s(sact)
                for tgq in range(NTG):
                    def tmm(e, tgq=tgq):
                        wait(e, mks["sq"], mks.get("bnd_red"))
                        mks["bnd_mm"] = b.pe.tag(e.matmul(
                            p_ns[:], K.ones_col[:], sqs[:, tgq * TGS:(tgq + 1) * TGS],
                            start=True, stop=True))
                    b.t(tmm)

                    def vred(e, col=col):
                        wait(e, mks["bnd_mm"])
                        e.drain()
                        mks["bnd_red"] = b.vec.tag(e.reduce_max(
                            nrm[:, col:col + 1], p_ns[:], AX.X))
                    b.v(vred)
                    col += 1

        def vcomb(e, ncols=col):
            wait(e, mks["bnd_red"])
            e.drain()
            e.reduce_max(nrm[:, 70:71], nrm[:, 0:ncols], AX.X)
            e.drain()
            e.tensor_scalar_mul(nrm[:, 71:72], nrm[:, 70:71], scale)
            e.drain()
            mks["mbb"] = b.vec.tag(e.tensor_copy(mbb[:, 0:1], nrm[:, 71:72]))
        b.v(vcomb)

        def tmb(e):
            wait(e, mks["mbb"])
            mks["mb_mm"] = b.pe.tag(e.matmul(p_misc[:, 0:1], K.negones_row[:],
                                             mbb[:, 0:1], start=True, stop=True))
        b.t(tmb)

        def vmb(e):
            wait(e, mks["mb_mm"])
            e.drain()
            mks["mb"] = b.vec.tag(e.tensor_copy(mb[:, 0:1], p_misc[:, 0:1]))
        b.v(vmb)

        exp_marks = [None] * NPST
        ctx_ev = [None, None]
        b.s_defer()
        for tgq in range(NTG):
            ntk = (tgq + 1) * (TGS // 128) if causal else T // 128
            for h in ATT_HEADS:
                chunk, off = h // 2, (h % 2) * 64
                pt_marks = []
                for tkc in range(ntk):
                    pi = tkc % NPST
                    ps = p_st[pi]

                    def tst(e, tkc=tkc, tgq=tgq, ps=ps, pi=pi, chunk=chunk, off=off,
                            em=exp_marks[pi]):
                        wait(e, em, mks["mb"])
                        ins = e.matmul(
                            ps[:],
                            k_sb[off:off + 64,
                                 chunk * T + tkc * 128: chunk * T + (tkc + 1) * 128],
                            q_sb[off:off + 64,
                                 chunk * T + tgq * TGS: chunk * T + (tgq + 1) * TGS],
                            start=True, stop=True)
                        mks["st"] = b.pe.tag(ins)
                    b.t(tst)

                    em = b.sca_reserve()

                    def sexp(e, tkc=tkc, ps=ps, w=(mks["st"],)):
                        wait(e, *w)
                        e.activation(
                            pt[:, tkc * TGS:(tkc + 1) * TGS], ps[:], AF.Exp,
                            bias=mb[:, 0:1], scale=scale).then_inc(b.sca.h, 1)
                    b.s(sexp)
                    mks["exp"] = em
                    exp_marks[pi] = mks["exp"]
                    pt_marks.append(mks["exp"])
                    diag_r = tkc - tgq * (TGS // 128)
                    if causal and 0 <= diag_r <= 3:
                        def vmask(e, tkc=tkc, diag_r=diag_r, em=mks["exp"]):
                            wait(e, em)
                            e.drain()
                            sl = slice(tkc * TGS, (tkc + 1) * TGS)
                            mks["mask"] = b.vec.tag(e.tensor_tensor(
                                pt[:, sl], pt[:, sl], K.masks[diag_r][:], ALU.mult))
                        b.v(vmask)
                        pt_marks.append(mks["mask"])

                ci = h % 2
                pc = p_ctx[ci]

                def tav(e, tgq=tgq, ntk=ntk, off=off, pc=pc, ci=ci, h=h,
                        w=(pt_marks[-1], mks.get("mask"), ctx_ev[ci])):
                    wait(e, *[m for m in w if m])
                    for tkc in range(ntk):
                        st, sp = (tkc == 0), (tkc == ntk - 1)
                        ins = e.matmul(pc[0:65, :],
                                       v_sb[:, tkc * VSTRIDE + h * 65:
                                            tkc * VSTRIDE + h * 65 + 65],
                                       pt[:, tkc * TGS:(tkc + 1) * TGS],
                                       start=st, stop=sp)
                    mks["av"] = b.pe.tag(ins)
                b.t(tav)

                def vnorm(e, pc=pc):
                    wait(e, mks["av"])
                    e.drain()
                    mks["rec"] = b.vec.tag(e.reciprocal(rb[0:1, :], pc[64:65, :]))
                b.v(vnorm)

                def trb(e):
                    wait(e, mks["rec"])
                    mks["rb_mm"] = b.pe.tag(e.matmul(
                        p_misc[:, :], K.ones_row_f[:], rb[0:1, :],
                        start=True, stop=True))
                b.t(trb)

                def vrb(e):
                    wait(e, mks["rb_mm"])
                    e.drain()
                    mks["rb"] = b.vec.tag(e.tensor_copy(rb[:, :], p_misc[:, :]))
                b.v(vrb)

                def vev(e, tgq=tgq, chunk=chunk, off=off, pc=pc):
                    wait(e, mks["rb"])
                    e.drain()
                    ins = e.tensor_tensor(
                        ctx_sb[off:off + 64,
                               chunk * T + tgq * TGS: chunk * T + (tgq + 1) * TGS],
                        pc[0:64, :], rb[0:64, :], ALU.mult)
                    mks["ctxev"] = b.vec.tag(ins)
                b.v(vev)
                ctx_ev[ci] = mks["ctxev"]
        b.s_flush()
    return mks["ctxev"]


B3_SHAPES = [("wq", [D, D]), ("wk", [D, D]), ("wv", [D, D]), ("wo", [D, D]),
             ("w1", [D, 4096]), ("w2", [D, 4096]), ("h1", [4096, 4096]),
             ("h2", [4096, 4096]), ("w3", [4096, D]),
             ("s1c", [128, 32]), ("s2c", [128, 32]), ("sh1c", [128, 32]),
             ("sh2c", [128, 32]), ("s3c", [128, 8])]


def emit_block3(b0, ctx, dr, K, xin, xout, upto="full"):
    nc = b0.nc if not isinstance(b0.nc, NCP) else b0.nc
    b = b0
    SCALE = 1.0 / 32.0
    if True:
        xn = ctx.enter_context(nc.sbuf_tensor("xn", [128, DC * T], BF16))
        scl = {}
        scl_marks = []
        for nm, ncol in [("s1c", 32), ("s2c", 32), ("sh1c", 32), ("sh2c", 32),
                         ("s3c", 8)]:
            scl[nm], m = load_cols(b, nm, dr, ncol)
            scl_marks.append(m)
        K.barrier(b, extra=scl_marks)
        consts_rdy = []

        xn_mark = emit_ln(b, xin, xn, K)

        if upto == "ln":
            def gdbg(e):
                wait(e, xn_mark)
                grp = b.gqq.begin(e)
                for c in range(DC):
                    m = grp.tag(e.dma_start(
                        out=xout[c * 128:(c + 1) * 128, :],
                        in_=xn[:, c * T:(c + 1) * T]), 16)
                wait(e, m)
            b.g(gdbg)
            return nc

        ctx_sb = ctx.enter_context(nc.sbuf_tensor("ctx_sb", [128, DC * T], BF16))
        with ExitStack() as ph:
            q_sb = ph.enter_context(nc.sbuf_tensor("q_sb", [128, DC * T], BF16))
            k_sb = ph.enter_context(nc.sbuf_tensor("k_sb", [128, DC * T], BF16))
            v_sb = ph.enter_context(nc.sbuf_tensor("v_sb", [128, 16 * D], BF16))
            bq = b.sub(ph)
            qkv_marks = []
            with ExitStack() as phw:
                bw = b.sub(phw)
                pr = PsumRot(bw, "prA", 4)
                pool_a = WPool(bw, "pA", 8 * 1024, 2)

                for wnm, dst in [("wq", q_sb), ("wk", k_sb)]:
                    ws = WStream(bw, pool_a, dr[wnm], D, D, 1024)

                    def evac(mc, rid, ps, waits, dst=dst):
                        out = {}

                        def sev(e):
                            wait(e, *waits)
                            out["m"] = b.sca.tag(e.activation(
                                dst[:, mc * T + rid * TGS: mc * T + (rid + 1) * TGS],
                                ps[:], AF.Identity))
                        b.s(sev)
                        return out["m"]
                    qkv_marks += fm_proj(bw, ws, DC, list(range(NTG)),
                                         lambda kc, rid: xn[:, kc * T + rid * TGS:
                                                            kc * T + (rid + 1) * TGS],
                                         evac, pr, DC,
                                         extra_waits=[xn_mark] + consts_rdy)

                wsv = WStream(bw, pool_a, dr["wv"], D, D, 512)

                def vevac(tc, fb, ps, waits):
                    out = {}

                    def vev(e):
                        wait(e, *waits)
                        e.drain()
                        out["m"] = b.vec.tag(e.tensor_copy(
                            v_sb[:, tc * D + fb * 512: tc * D + (fb + 1) * 512],
                            ps[:, 0:512]))
                    b.v(vev)
                    return out["m"]
                qkv_marks += tm_proj(bw, wsv, DC,
                                     lambda kc, tc: xn[:, kc * T + tc * 128:
                                                       kc * T + (tc + 1) * 128],
                                     vevac, pr, 16, extra_waits=[xn_mark])

            if upto == "qkv":
                def gdbg(e):
                    wait(e, *qkv_marks)
                    grp = b.gqq.begin(e)
                    for c in range(DC):
                        m = grp.tag(e.dma_start(
                            out=xout[c * 128:(c + 1) * 128, :],
                            in_=q_sb[:, c * T:(c + 1) * T]), 16)
                    wait(e, m)
                b.g(gdbg)
                return nc

            ctx_mark = emit_attention_1h(bq, K, q_sb, k_sb, v_sb, ctx_sb,
                                         SCALE, True, qkv_marks, "att",
                                         upto=(upto if upto in
                                               ("score1", "bound", "norm1")
                                               or upto.startswith("tq")
                                               else "full"))

            if upto in ("att", "bound", "score1", "norm1") or upto.startswith("tq"):
                def gdbg(e):
                    wait(e, ctx_mark)
                    grp = b.gqq.begin(e)
                    for c in range(DC):
                        m = grp.tag(e.dma_start(
                            out=xout[c * 128:(c + 1) * 128, :],
                            in_=ctx_sb[:, c * T:(c + 1) * T]), 16)
                    wait(e, m)
                b.g(gdbg)
                return nc

        with ExitStack() as ph2:
            b2 = b.sub(ph2)
            xtg = ph2.enter_context(nc.sbuf_tensor("xtg", [128, DC * TGS], F32))
            t1 = ph2.enter_context(nc.sbuf_tensor("t1", [128, 32 * TGS], BF16))
            hh = ph2.enter_context(nc.sbuf_tensor("hh", [128, 32 * TGS], BF16))
            pr2 = PsumRot(b2, "prB", 4)
            pool_f = WPool(b2, "pF", 8192, 2, init_free=ctx_mark)
            wso = WStream(b2, pool_f, dr["wo"], D, D, 1024)
            ws1 = WStream(b2, pool_f, dr["w1"], D, 4096, 1024)
            ws2 = WStream(b2, pool_f, dr["w2"], D, 4096, 1024)
            wsh1 = WStream(b2, pool_f, dr["h1"], 4096, 4096, 256)
            wsh2 = WStream(b2, pool_f, dr["h2"], 4096, 4096, 256)
            ws3 = WStream(b2, pool_f, dr["w3"], 4096, D, 256)
            store_mark = ctx_mark

            for tg in range(NTG):
                mks = {}

                def yload(e, tg=tg, prev=store_mark):
                    wait(e, prev)
                    grp = b.yqq.begin(e)
                    for c in range(DC):
                        mks["xl"] = grp.tag(e.dma_start(
                            out=xtg[:, c * TGS:(c + 1) * TGS],
                            in_=xin[c * 128:(c + 1) * 128,
                                        tg * TGS:(tg + 1) * TGS]), 16)
                b2.y(yload)
                res_marks = []

                def evac_wo(mc, rid, ps, waits):
                    out = {}

                    def vev(e):
                        wait(e, *waits, mks["xl"])
                        e.drain()
                        out["m"] = b.vec.tag(e.tensor_tensor(
                            xtg[:, mc * TGS:(mc + 1) * TGS],
                            xtg[:, mc * TGS:(mc + 1) * TGS], ps[:], ALU.add))
                    b2.v(vev)
                    res_marks.append(out["m"])
                    return out["m"]
                fm_proj(b2, wso, DC, [tg],
                        lambda kc, rid: ctx_sb[:, kc * T + rid * TGS:
                                               kc * T + (rid + 1) * TGS],
                        evac_wo, pr2, DC, extra_waits=[ctx_mark])

                def evac_s1(mc, rid, ps, waits):
                    em = b2.sca_reserve()

                    def sev(e):
                        wait(e, *waits)
                        e.activation(
                            t1[:, mc * TGS:(mc + 1) * TGS], ps[:], AF.Silu,
                            scale=scl["s1c"][:, mc:mc + 1]).then_inc(b2.sca.h, 1)
                    b2.s(sev)
                    return em
                b2.s_defer()
                m1 = fm_proj(b2, ws1, DC, [tg],
                             lambda kc, rid: xn[:, kc * T + rid * TGS:
                                                kc * T + (rid + 1) * TGS],
                             evac_s1, pr2, 32)

                def evac_s2(mc, rid, ps, waits):
                    out = {}

                    def vev(e):
                        wait(e, *waits, *m1)
                        e.drain()
                        out["m"] = b.vec.tag(e.scalar_tensor_tensor(
                            hh[:, mc * TGS:(mc + 1) * TGS], ps[:],
                            scl["s2c"][:, mc:mc + 1],
                            t1[:, mc * TGS:(mc + 1) * TGS], ALU.mult, ALU.mult))
                    b2.v(vev)
                    return out["m"]
                m2_ = fm_proj(b2, ws2, DC, [tg],
                              lambda kc, rid: xn[:, kc * T + rid * TGS:
                                                 kc * T + (rid + 1) * TGS],
                              evac_s2, pr2, 32)

                def evac_g1(mc, rid, ps, waits):
                    em = b2.sca_reserve()

                    def sev(e):
                        wait(e, *waits)
                        e.activation(
                            t1[:, mc * TGS:(mc + 1) * TGS], ps[:], AF.Gelu,
                            scale=scl["sh1c"][:, mc:mc + 1]).then_inc(b2.sca.h, 1)
                    b2.s(sev)
                    return em
                m3 = fm_proj(b2, wsh1, 32, [0],
                             lambda kc, rid: hh[:, kc * TGS:(kc + 1) * TGS],
                             evac_g1, pr2, 32, extra_waits=m2_)

                def evac_g2(mc, rid, ps, waits):
                    em = b2.sca_reserve()

                    def sev(e):
                        wait(e, *waits)
                        e.activation(
                            hh[:, mc * TGS:(mc + 1) * TGS], ps[:], AF.Gelu,
                            scale=scl["sh2c"][:, mc:mc + 1]).then_inc(b2.sca.h, 1)
                    b2.s(sev)
                    return em
                m4 = fm_proj(b2, wsh2, 32, [0],
                             lambda kc, rid: t1[:, kc * TGS:(kc + 1) * TGS],
                             evac_g2, pr2, 32, extra_waits=m3)
                b2.s_flush()

                def evac_o(mc, rid, ps, waits):
                    out = {}

                    def vev(e):
                        wait(e, *waits)
                        e.drain()
                        out["m"] = b.vec.tag(e.scalar_tensor_tensor(
                            xtg[:, mc * TGS:(mc + 1) * TGS], ps[:],
                            scl["s3c"][:, mc:mc + 1],
                            xtg[:, mc * TGS:(mc + 1) * TGS], ALU.mult, ALU.add))
                    b2.v(vev)
                    res_marks.append(out["m"])
                    return out["m"]
                fm_proj(b2, ws3, 32, [0],
                        lambda kc, rid: hh[:, kc * TGS:(kc + 1) * TGS],
                        evac_o, pr2, DC, extra_waits=m4)

                def ystore(e, tg=tg, rm=list(res_marks)):
                    wait(e, rm[-1])
                    grp = b.yqq.begin(e)
                    for c in range(DC):
                        ystore.mark = grp.tag(e.dma_start(
                            out=xout[c * 128:(c + 1) * 128, tg * TGS:(tg + 1) * TGS],
                            in_=xtg[:, c * TGS:(c + 1) * TGS]), 16)
                b2.y(ystore)
                store_mark = ystore.mark

    return store_mark


def build_block3(upto="full"):
    nc = bass.Bass("TRN2", target_bir_lowering=False)
    dr = {}
    xin = nc.dram_tensor("x", [D, T], F32, kind="ExternalInput").ap()
    for nm, shape in B3_SHAPES:
        dr[nm] = nc.dram_tensor(nm, shape, F32, kind="ExternalInput").ap()
    cs = make_consts()
    for nm in CONST_NAMES:
        dr[nm] = nc.dram_tensor(nm, list(cs[nm].shape), F32, kind="ExternalInput").ap()
    xout = nc.dram_tensor("xout", [D, T], F32, kind="ExternalOutput").ap()
    with ExitStack() as ctx:
        blk = ctx.enter_context(nc.Block())
        b = B(nc, ctx, blk)
        K = ConstTiles(b, dr)
        with ExitStack() as bctx:
            fin = emit_block3(b.sub(bctx), bctx, dr, K, xin, xout, upto=upto)
        if isinstance(fin, tuple):
            b.y(lambda e: wait(e, fin))
    return nc


def prep_block3_inputs(inputs):
    m = dict(make_consts())
    m["wq"] = _t(inputs["L3_attn_Wq"])
    m["wk"] = _t(inputs["L3_attn_Wk"])
    m["wv"] = _t(inputs["L3_attn_Wv"])
    m["wo"] = _t(inputs["L3_attn_Wo"])
    m["w1"] = _t(inputs["L3_ffn_w1_w"])
    m["w2"] = _t(inputs["L3_ffn_w2_w"])
    m["h1"] = _t(inputs["L3_ffn_h1_w"])
    m["h2"] = _t(inputs["L3_ffn_h2_w"])
    m["w3"] = _t(inputs["L3_ffn_w3_w"])
    m["s1c"] = _cols(inputs["L3_ffn_w1_s"])
    m["s2c"] = _cols(inputs["L3_ffn_w2_s"])
    m["sh1c"] = _cols(inputs["L3_ffn_h1_s"])
    m["sh2c"] = _cols(inputs["L3_ffn_h2_s"])
    m["s3c"] = _cols(inputs["L3_ffn_w3_s"])
    return m


def run_block3(x_fm_percore, inputs):
    nc = build_block3()
    base = prep_block3_inputs(inputs)
    in_maps = []
    for core in range(8):
        m = dict(base)
        m["x"] = np.ascontiguousarray(x_fm_percore[core], dtype=np.float32)
        in_maps.append(m)
    res = run_bass_kernel_spmd(nc, in_maps, core_ids=list(range(8)))
    return [res.results[c]["xout"] for c in range(8)]



def emit_gates_top2(b, K, xn, wg_sb, bg_row, gsb, renorm_eps, ready_waits, pr):
    E = 8
    nc = b.nc
    with ExitStack() as ph:
        tmp = ph.enter_context(nc.sbuf_tensor("gt_tmp", [128, 6 * E + 8], F32))
        p = tmp[:, 0:E]
        pm = tmp[:, E:2 * E]
        ge = tmp[:, 2 * E:3 * E]
        w = tmp[:, 3 * E:4 * E]
        m1 = tmp[:, 6 * E:6 * E + 1]
        s1 = tmp[:, 6 * E + 1:6 * E + 2]
        r1 = tmp[:, 6 * E + 2:6 * E + 3]
        mks = {}
        for tc in range(16):
            pidx, ps, ps_free = pr.get()

            def tmm(e, tc=tc, ps=ps, ps_free=ps_free, first=(tc == 0)):
                if first:
                    wait(e, *ready_waits)
                wait(e, ps_free, mks.get("gev"))
                for kc in range(DC):
                    e.matmul(ps[:, 0:E], xn[:, kc * T + tc * 128: kc * T + (tc + 1) * 128],
                             wg_sb[:, kc * E:(kc + 1) * E], start=(kc == 0), stop=False)
                mks["gmm"] = b.pe.tag(e.matmul(ps[:, 0:E], K.ones_row[:], bg_row[:],
                                               start=False, stop=True))
            b.t(tmm)

            def vgate(e, tc=tc, ps=ps):
                wait(e, mks["gmm"])
                e.drain()
                e.reduce_max(m1, ps[:, 0:E], AX.X, negate=True)
                e.drain()
                mks["g1"] = b.vec.tag(e.tensor_copy(pm, ps[:, 0:E]))
            b.v(vgate)
            pr.set_free(pidx, mks["g1"])

            def sgate(e):
                wait(e, mks["g1"])
                mks["g2"] = b.sca.tag(e.activation(p, pm, AF.Exp, bias=m1,
                                                   accum_out=s1))
            b.s(sgate)

            def vgate2(e, tc=tc):
                wait(e, mks["g2"])
                e.drain()
                e.reciprocal(r1, s1)
                e.drain()
                e.tensor_scalar_mul(p, p, r1)
                e.drain()
                e.reduce_max(m1, p, AX.X)
                e.drain()
                e.tensor_single_scalar(ge, p, m1, ALU.is_ge)
                e.drain()
                e.scalar_tensor_tensor(pm, ge, -2.0, p, ALU.mult, ALU.add)
                e.drain()
                e.reduce_max(m1, pm, AX.X)
                e.drain()
                e.tensor_single_scalar(ge, p, m1, ALU.is_ge)
                e.drain()
                e.tensor_tensor(w, p, ge, ALU.mult)
                e.drain()
                if renorm_eps is not None:
                    e.reduce_sum(s1, w, AX.X)
                    e.drain()
                    e.tensor_scalar_add(s1, s1, renorm_eps if renorm_eps is not None else 0.0)
                    e.drain()
                    e.reciprocal(r1, s1)
                    e.drain()
                    e.tensor_scalar_mul(w, w, r1)
                    e.drain()
                mks["gev"] = b.vec.tag(e.tensor_copy(gsb[:, tc * 8:(tc + 1) * 8], w))
            b.v(vgate2)
    return mks["gev"]


B1_SHAPES = [("wq", [D, D]), ("wk", [D, D]), ("wv", [D, D]),
             ("wproj", [D, D]), ("bqc", [128, 8]), ("bkc", [128, 8]),
             ("bvrow", [1, D]), ("bprojc", [128, 8]),
             ("wg", [D, 8]), ("bgrow", [1, 8]),
             ("w1", [8 * D, 4096]), ("b1c", [128, 8 * 32]),
             ("w2", [8 * 4096, D]), ("b2row", [1, 8 * D])]


def emit_block1(b, ctx, dr, K, xin, xout, upto="full"):
    nc = b.nc
    SCALE = 1.0 / 64.0
    if True:
        xn = ctx.enter_context(nc.sbuf_tensor("xn", [128, DC * T], BF16))
        gsb = ctx.enter_context(nc.sbuf_tensor("gsb", [128, 16 * 8], F32))
        colt = {}
        cmarks = []
        for nm, ncol in [("bqc", 8), ("bkc", 8), ("bprojc", 8), ("b1c", 8 * 32)]:
            colt[nm], m = load_cols(b, nm, dr, ncol)
            cmarks.append(m)
        brow = {}
        brow["bgrow"] = ctx.enter_context(nc.sbuf_tensor("k_bgrow", [1, 8], BF16))

        def growloads(e):
            grp = b.gqq.begin(e)
            cmarks.append(grp.tag(e.dma_start(out=brow["bgrow"][:],
                                              in_=dr["bgrow"]), 16))
        b.g(growloads)
        wg_sb = ctx.enter_context(nc.sbuf_tensor("k_wg", [128, DC * 8], BF16))

        def gwg(e):
            grp = b.gqq.begin(e)
            cmarks.append(grp.tag(e.dma_start(
                out=wg_sb[:].rearrange("p (kc m) -> p kc m", kc=DC),
                in_=dr["wg"].rearrange("(kc p) m -> p kc m", p=128)), 16))
        b.g(gwg)
        K.barrier(b, extra=cmarks)

        xn_mark = emit_ln(b, xin, xn, K)

        if upto == "ln":
            def gdbg(e):
                wait(e, xn_mark)
                grp = b.gqq.begin(e)
                for c in range(DC):
                    m = grp.tag(e.dma_start(out=xout[c * 128:(c + 1) * 128, :],
                                            in_=xn[:, c * T:(c + 1) * T]), 16)
                wait(e, m)
            b.g(gdbg)
            return nc

        ctx_sb = ctx.enter_context(nc.sbuf_tensor("ctx_sb", [128, DC * T], BF16))
        if upto == "att":
            def vclr0(e):
                wait(e, xn_mark)
                e.drain()
                b.vec.tag(e.memset(ctx_sb[:], 0.0))
            b.v(vclr0)
        with ExitStack() as ph:
            q_sb = ph.enter_context(nc.sbuf_tensor("q_sb", [128, DC * T], BF16))
            k_sb = ph.enter_context(nc.sbuf_tensor("k_sb", [128, DC * T], BF16))
            v_sb = ph.enter_context(nc.sbuf_tensor("v_sb", [128, 16 * VSTRIDE], BF16))
            bq = b.sub(ph)
            qkv_marks = []
            with ExitStack() as phw:
                bw = b.sub(phw)
                pr = PsumRot(bw, "prA", 4)
                pool_a = WPool(bw, "pA", 4 * 1024, 2)
                brow["bvrow"] = phw.enter_context(
                    nc.sbuf_tensor("k_bvrow", [1, D], BF16))
                bvm = {}

                def gbv(e):
                    grp = b.gqq.begin(e)
                    bvm["m"] = grp.tag(e.dma_start(out=brow["bvrow"][:],
                                                   in_=dr["bvrow"]), 16)
                b.g(gbv)

                for wnm, dst, bc in [("wq", q_sb, "bqc"), ("wk", k_sb, "bkc")]:
                    ws = WStream(bw, pool_a, dr[wnm], D, D, 512)

                    def evac(mc, rid, ps, waits, dst=dst, bc=bc):
                        out = {}

                        def sev(e):
                            wait(e, *waits)
                            out["m"] = b.sca.tag(e.activation(
                                dst[:, mc * T + rid * TGS: mc * T + (rid + 1) * TGS],
                                ps[:], AF.Identity, bias=colt[bc][:, mc:mc + 1]))
                        b.s(sev)
                        return out["m"]
                    qkv_marks += fm_proj(bw, ws, DC, list(range(NTG)),
                                         lambda kc, rid: xn[:, kc * T + rid * TGS:
                                                            kc * T + (rid + 1) * TGS],
                                         evac, pr, DC, extra_waits=[xn_mark])

                vms = {}

                def vmemset(e):
                    wait(e, xn_mark)
                    e.drain()
                    vms["m"] = b.vec.tag(e.memset(v_sb[:], 1.0))
                b.v(vmemset)

                wsv = WStream(bw, pool_a, dr["wv"], D, D, 512)

                def vevac(tc, fb, ps, waits):
                    out = {}

                    def vev(e):
                        wait(e, *waits, vms["m"])
                        e.drain()
                        for j in range(8):
                            h = fb * 8 + j
                            out["m"] = b.vec.tag(e.tensor_copy(
                                v_sb[:, tc * VSTRIDE + h * 65: tc * VSTRIDE + h * 65 + 64],
                                ps[:, j * 64:(j + 1) * 64]))
                    b.v(vev)
                    return out["m"]
                qkv_marks += tm_proj(bw, wsv, DC,
                                     lambda kc, tc: xn[:, kc * T + tc * 128:
                                                       kc * T + (tc + 1) * 128],
                                     vevac, pr, 16,
                                     extra_waits=[xn_mark, bvm["m"]],
                                     bias_row=lambda fb: brow["bvrow"][:, fb * 512:(fb + 1) * 512],
                                     ones_row=K.ones_row)

            if upto == "qkv":
                def gdbg(e):
                    wait(e, *qkv_marks)
                    grp = b.gqq.begin(e)
                    for c in range(DC):
                        m = grp.tag(e.dma_start(out=xout[c * 128:(c + 1) * 128, :],
                                                in_=q_sb[:, c * T:(c + 1) * T]), 16)
                    wait(e, m)
                b.g(gdbg)
                return nc

            ctx_mark = emit_attention_mh(bq, K, q_sb, k_sb, v_sb, ctx_sb,
                                         SCALE, True, qkv_marks, "mha")

            if upto == "att":
                def gdbg(e):
                    wait(e, ctx_mark)
                    grp = b.gqq.begin(e)
                    for c in range(DC):
                        m = grp.tag(e.dma_start(out=xout[c * 128:(c + 1) * 128, :],
                                                in_=ctx_sb[:, c * T:(c + 1) * T]), 16)
                    wait(e, m)
                b.g(gdbg)
                return nc

        with ExitStack() as ph2:
            b2 = b.sub(ph2)
            xtg = ph2.enter_context(nc.sbuf_tensor("xtg", [128, DC * TGS], F32))
            hh = ph2.enter_context(nc.sbuf_tensor("hh", [128, 32 * TGS], BF16))
            acc = ph2.enter_context(nc.sbuf_tensor("acc", [128, 4 * 1024], F32))
            pr2 = PsumRot(b2, "prB", 4)
            pool_f = WPool(b2, "pF", 16 * 1024, 2, init_free=ctx_mark)
            b2r2 = [ph2.enter_context(nc.sbuf_tensor(f"k_b2r{i}", [1, D], BF16))
                    for i in range(2)]
            b2m = {}
            ws1 = [WStream(b2, pool_f, dr["w1"][e * D:(e + 1) * D, :], D, 4096, 1024)
                   for e in range(8)]
            ws2 = [WStream(b2, pool_f, dr["w2"][e * 4096:(e + 1) * 4096, :],
                           4096, D, 512) for e in range(8)]
            wsp = WStream(b2, pool_f, dr["wproj"], D, D, 1024)

            gate_mark = emit_gates_top2(b2, K, xn, wg_sb, brow["bgrow"], gsb,
                                        1e-8, [xn_mark, ctx_mark], pr2)

            store_mark = ctx_mark
            for tg in range(NTG):
                mks = {}

                def yload(e, tg=tg, prev=store_mark, gm=gate_mark):
                    wait(e, prev, gm)
                    grp = b.yqq.begin(e)
                    for c in range(DC):
                        mks["xl"] = grp.tag(e.dma_start(
                            out=xtg[:, c * TGS:(c + 1) * TGS],
                            in_=xin[c * 128:(c + 1) * 128,
                                        tg * TGS:(tg + 1) * TGS]), 16)
                b2.y(yload)
                res_marks = []

                def evac_proj(mc, rid, ps, waits):
                    out = {}

                    def vev(e):
                        wait(e, *waits, mks["xl"])
                        e.drain()
                        out["m"] = b.vec.tag(e.scalar_tensor_tensor(
                            xtg[:, mc * TGS:(mc + 1) * TGS], ps[:],
                            colt["bprojc"][:, mc:mc + 1],
                            xtg[:, mc * TGS:(mc + 1) * TGS], ALU.add, ALU.add))
                    b2.v(vev)
                    res_marks.append(out["m"])
                    return out["m"]
                fm_proj(b2, wsp, DC, [tg],
                        lambda kc, rid: ctx_sb[:, kc * T + rid * TGS:
                                               kc * T + (rid + 1) * TGS],
                        evac_proj, pr2, DC, extra_waits=[ctx_mark])

                def vclr(e, rm=list(res_marks)):
                    wait(e, rm[-1] if rm else None)
                    e.drain()
                    mks["clr"] = b.vec.tag(e.memset(acc[:], 0.0))
                b2.v(vclr)

                b2.s_defer()
                for ee in range(8):
                    def gb2(e, ee=ee,
                            w=(b2m.get(("slot", ee % 2)), ctx_mark)):
                        wait(e, *[m for m in w if m])
                        grp = b.gqq.begin(e)
                        b2m["m"] = grp.tag(e.dma_start(
                            out=b2r2[ee % 2][:],
                            in_=dr["b2row"][:, ee * D:(ee + 1) * D]), 16)
                    b.g(gb2)

                    def evac_h(mc, rid, ps, waits, ee=ee):
                        em = b2.sca_reserve()

                        def sev(e, mc=mc, ps=ps, w=tuple(waits)):
                            wait(e, *w)
                            e.activation(
                                hh[:, mc * TGS:(mc + 1) * TGS], ps[:],
                                AF.Gelu_apprx_tanh,
                                bias=colt["b1c"][:, ee * 32 + mc: ee * 32 + mc + 1]
                            ).then_inc(b2.sca.h, 1)
                        b2.s(sev)
                        return em
                    hmarks = fm_proj(b2, ws1[ee], DC, [tg],
                                     lambda kc, rid: xn[:, kc * T + rid * TGS:
                                                        kc * T + (rid + 1) * TGS],
                                     evac_h, pr2, 32)

                    def evac_o(tc, fb, ps, waits, ee=ee, tg=tg):
                        out = {}

                        def vev(e):
                            wait(e, *waits, mks["clr"])
                            e.drain()
                            out["m"] = b.vec.tag(e.scalar_tensor_tensor(
                                acc[:, tc * 1024 + fb * 512: tc * 1024 + (fb + 1) * 512],
                                ps[:, 0:512],
                                gsb[:, (tg * 4 + tc) * 8 + ee:
                                    (tg * 4 + tc) * 8 + ee + 1],
                                acc[:, tc * 1024 + fb * 512: tc * 1024 + (fb + 1) * 512],
                                ALU.mult, ALU.add))
                        b2.v(vev)
                        return out["m"]
                    omarks = tm_proj(b2, ws2[ee], 32,
                                     lambda kc, tc: hh[:, kc * TGS + tc * 128:
                                                       kc * TGS + (tc + 1) * 128],
                                     evac_o, pr2, 4,
                                     extra_waits=list(hmarks) + [b2m["m"]],
                                     bias_row=lambda fb, ee=ee:
                                         b2r2[ee % 2][:, fb * 512:
                                                       fb * 512 + 512],
                                     ones_row=K.ones_row)
                    mks["moe"] = omarks[-1]
                    b2m[("slot", ee % 2)] = omarks[-1]
                b2.s_flush()

                for tc in range(4):
                    for mc in range(DC):
                        def ttp(e, tc=tc, mc=mc):
                            wait(e, mks["moe"])
                            pidx, ps, ps_free = pr2.get()
                            wait(e, ps_free)
                            mks["tp"] = b.pe.tag(e.matmul(
                                ps[:, 0:128],
                                acc[:, tc * 1024 + mc * 128: tc * 1024 + (mc + 1) * 128],
                                K.ident_f[:], start=True, stop=True))
                            ttp.ps = (pidx, ps)
                        b2.t(ttp)
                        pidx, ps = ttp.ps

                        def vadd(e, tc=tc, mc=mc, ps=ps, pidx=pidx):
                            wait(e, mks["tp"])
                            e.drain()
                            m = b.vec.tag(e.tensor_tensor(
                                xtg[:, mc * TGS + tc * 128: mc * TGS + (tc + 1) * 128],
                                xtg[:, mc * TGS + tc * 128: mc * TGS + (tc + 1) * 128],
                                ps[:, 0:128], ALU.add))
                            pr2.set_free(pidx, m)
                            vadd.mark = m
                        b2.v(vadd)
                        res_marks.append(vadd.mark)

                def ystore(e, tg=tg, rm=list(res_marks)):
                    wait(e, rm[-1])
                    grp = b.yqq.begin(e)
                    for c in range(DC):
                        ystore.mark = grp.tag(e.dma_start(
                            out=xout[c * 128:(c + 1) * 128, tg * TGS:(tg + 1) * TGS],
                            in_=xtg[:, c * TGS:(c + 1) * TGS]), 16)
                b2.y(ystore)
                store_mark = ystore.mark

    return store_mark


def build_block1(upto="full"):
    nc = bass.Bass("TRN2", target_bir_lowering=False)
    dr = {}
    xin = nc.dram_tensor("x", [D, T], F32, kind="ExternalInput").ap()
    for nm, shape in B1_SHAPES:
        dr[nm] = nc.dram_tensor(nm, shape, F32, kind="ExternalInput").ap()
    cs = make_consts()
    for nm in CONST_NAMES:
        dr[nm] = nc.dram_tensor(nm, list(cs[nm].shape), F32, kind="ExternalInput").ap()
    xout = nc.dram_tensor("xout", [D, T], F32, kind="ExternalOutput").ap()
    with ExitStack() as ctx:
        blk = ctx.enter_context(nc.Block())
        b = B(nc, ctx, blk)
        K = ConstTiles(b, dr)
        with ExitStack() as bctx:
            fin = emit_block1(b.sub(bctx), bctx, dr, K, xin, xout, upto=upto)
        if isinstance(fin, tuple):
            b.y(lambda e: wait(e, fin))
    return nc


def prep_block1_inputs(inputs):
    m = dict(make_consts())
    wat = _t(inputs["L1_attn_W_attn"])
    m["wq"] = np.ascontiguousarray(wat[:, 0:D])
    m["wk"] = np.ascontiguousarray(wat[:, D:2 * D])
    m["wv"] = np.ascontiguousarray(wat[:, 2 * D:3 * D])
    ba = np.asarray(inputs["L1_attn_b_attn"])
    m["bqc"] = _cols(ba[0:D])
    m["bkc"] = _cols(ba[D:2 * D])
    m["bvrow"] = np.ascontiguousarray(ba[2 * D:3 * D].reshape(1, D))
    m["wproj"] = _t(inputs["L1_attn_W_proj"])
    m["bprojc"] = _cols(inputs["L1_attn_b_proj"])
    m["wg"] = np.ascontiguousarray(inputs["L1_ffn_Wg"])
    m["bgrow"] = np.ascontiguousarray(np.asarray(inputs["L1_ffn_bg"]).reshape(1, 8))
    m["w1"] = np.ascontiguousarray(np.asarray(inputs["L1_ffn_W1"]).reshape(8 * D, 4096))
    m["b1c"] = np.ascontiguousarray(
        np.asarray(inputs["L1_ffn_b1"]).reshape(8, 32, 128).transpose(2, 0, 1).reshape(128, 8 * 32))
    m["w2"] = np.ascontiguousarray(np.asarray(inputs["L1_ffn_W2"]).reshape(8 * 4096, D))
    m["b2row"] = np.ascontiguousarray(np.asarray(inputs["L1_ffn_b2"]).reshape(1, 8 * D))
    return m


def run_block1(x_fm_percore, inputs, upto="full"):
    nc = build_block1(upto=upto)
    base = prep_block1_inputs(inputs)
    in_maps = []
    for core in range(8):
        m = dict(base)
        m["x"] = np.ascontiguousarray(x_fm_percore[core], dtype=np.float32)
        in_maps.append(m)
    res = run_bass_kernel_spmd(nc, in_maps, core_ids=list(range(8)))
    return [res.results[c]["xout"] for c in range(8)]



B2_SHAPES = [("wq", [D, D]), ("wk", [D, D]), ("wv", [D, D]),
             ("wproj", [D, D]), ("bqc", [128, 8]), ("bkc", [128, 8]),
             ("bvrow", [1, D]), ("bprojc", [128, 8]),
             ("wrouter", [D, 16]), ("brrow", [1, 16]),
             ("wexp", [16 * D, D])]


def emit_block2(b, ctx, dr, K, xin, xout, upto="full"):
    nc = b.nc
    SCALE = 1.0 / 8.0
    E2 = 16
    EPS2 = 1e-10
    if True:
        xn = ctx.enter_context(nc.sbuf_tensor("xn", [128, DC * T], BF16))
        gsb = ctx.enter_context(nc.sbuf_tensor("gsb", [128, 16 * E2], F32))
        acsb = ctx.enter_context(nc.sbuf_tensor("acsb", [128, 16], F32))
        colt = {}
        cmarks = []
        for nm, ncol in [("bqc", 8), ("bkc", 8), ("bprojc", 8)]:
            colt[nm], m = load_cols(b, nm, dr, ncol)
            cmarks.append(m)
        brow = {}
        brow["brrow"] = ctx.enter_context(nc.sbuf_tensor("k_brrow", [1, 16], BF16))
        wr_sb = ctx.enter_context(nc.sbuf_tensor("k_wr", [128, DC * 16], BF16))

        def gsmall(e):
            grp = b.gqq.begin(e)
            grp.tag(e.dma_start(out=brow["brrow"][:], in_=dr["brrow"]), 16)
            cmarks.append(grp.tag(e.dma_start(
                out=wr_sb[:].rearrange("p (kc m) -> p kc m", kc=DC),
                in_=dr["wrouter"].rearrange("(kc p) m -> p kc m", p=128)), 16))
        b.g(gsmall)
        K.barrier(b, extra=cmarks)

        xn_mark = emit_ln(b, xin, xn, K)

        ctx_sb = ctx.enter_context(nc.sbuf_tensor("ctx_sb", [128, DC * T], BF16))
        with ExitStack() as ph:
            q_sb = ph.enter_context(nc.sbuf_tensor("q_sb", [128, DC * T], BF16))
            k_sb = ph.enter_context(nc.sbuf_tensor("k_sb", [128, DC * T], BF16))
            v_sb = ph.enter_context(nc.sbuf_tensor("v_sb", [128, 16 * VSTRIDE], BF16))
            bq = b.sub(ph)
            qkv_marks = []
            with ExitStack() as phw:
                bw = b.sub(phw)
                pr = PsumRot(bw, "prA", 4)
                pool_a = WPool(bw, "pA", 4 * 1024, 2)
                brow["bvrow"] = phw.enter_context(
                    nc.sbuf_tensor("k_bvrow", [1, D], BF16))
                bvm = {}

                def gbv(e):
                    grp = b.gqq.begin(e)
                    bvm["m"] = grp.tag(e.dma_start(out=brow["bvrow"][:],
                                                   in_=dr["bvrow"]), 16)
                b.g(gbv)

                for wnm, dst, bc in [("wq", q_sb, "bqc"), ("wk", k_sb, "bkc")]:
                    ws = WStream(bw, pool_a, dr[wnm], D, D, 512)

                    def evac(mc, rid, ps, waits, dst=dst, bc=bc):
                        out = {}

                        def sev(e):
                            wait(e, *waits)
                            out["m"] = b.sca.tag(e.activation(
                                dst[:, mc * T + rid * TGS: mc * T + (rid + 1) * TGS],
                                ps[:], AF.Identity, bias=colt[bc][:, mc:mc + 1]))
                        b.s(sev)
                        return out["m"]
                    qkv_marks += fm_proj(bw, ws, DC, list(range(NTG)),
                                         lambda kc, rid: xn[:, kc * T + rid * TGS:
                                                            kc * T + (rid + 1) * TGS],
                                         evac, pr, DC, extra_waits=[xn_mark])

                vms = {}

                def vmemset(e):
                    wait(e, xn_mark)
                    e.drain()
                    vms["m"] = b.vec.tag(e.memset(v_sb[:], 1.0))
                b.v(vmemset)

                wsv = WStream(bw, pool_a, dr["wv"], D, D, 512)

                def vevac(tc, fb, ps, waits):
                    out = {}

                    def vev(e):
                        wait(e, *waits, vms["m"])
                        e.drain()
                        for j in range(8):
                            h = fb * 8 + j
                            out["m"] = b.vec.tag(e.tensor_copy(
                                v_sb[:, tc * VSTRIDE + h * 65: tc * VSTRIDE + h * 65 + 64],
                                ps[:, j * 64:(j + 1) * 64]))
                    b.v(vev)
                    return out["m"]
                qkv_marks += tm_proj(bw, wsv, DC,
                                     lambda kc, tc: xn[:, kc * T + tc * 128:
                                                       kc * T + (tc + 1) * 128],
                                     vevac, pr, 16,
                                     extra_waits=[xn_mark, bvm["m"]],
                                     bias_row=lambda fb: brow["bvrow"][:, fb * 512:(fb + 1) * 512],
                                     ones_row=K.ones_row)

            ctx_mark = emit_attention_mh(bq, K, q_sb, k_sb, v_sb, ctx_sb,
                                         SCALE, True, qkv_marks, "mha")

            if upto == "att":
                def gdbg(e):
                    wait(e, ctx_mark)
                    grp = b.gqq.begin(e)
                    for c in range(DC):
                        m = grp.tag(e.dma_start(out=xout[c * 128:(c + 1) * 128, :],
                                                in_=ctx_sb[:, c * T:(c + 1) * T]), 16)
                    wait(e, m)
                b.g(gdbg)
                return nc

        with ExitStack() as ph2:
            b2 = b.sub(ph2)
            xtg = ph2.enter_context(nc.sbuf_tensor("xtg", [128, DC * TGS], F32))
            acc = ph2.enter_context(nc.sbuf_tensor("acc", [128, 4 * 1024], F32))
            pr2 = PsumRot(b2, "prB", 4)
            pool_f = WPool(b2, "pF", 8 * 1024, 2, init_free=ctx_mark)
            wse = [WStream(b2, pool_f, dr["wexp"][e * D:(e + 1) * D, :], D, D, 512)
                   for e in range(E2)]
            wsp = WStream(b2, pool_f, dr["wproj"], D, D, 1024)

            gmks = {}
            with ExitStack() as phg:
                gt = phg.enter_context(nc.sbuf_tensor("gt", [128, 4 * E2 + 8], F32))
                p = gt[:, 0:E2]
                pm = gt[:, E2:2 * E2]
                ge = gt[:, 2 * E2:3 * E2]
                w = gt[:, 3 * E2:4 * E2]
                m1 = gt[:, 4 * E2:4 * E2 + 1]
                s1 = gt[:, 4 * E2 + 1:4 * E2 + 2]
                r1 = gt[:, 4 * E2 + 2:4 * E2 + 3]
                for tc in range(16):
                    pidx, ps, ps_free = pr2.get()

                    def tmm(e, tc=tc, ps=ps, ps_free=ps_free, first=(tc == 0)):
                        if first:
                            wait(e, xn_mark, ctx_mark)
                        wait(e, ps_free)
                        for kc in range(DC):
                            e.matmul(ps[:, 0:E2],
                                     xn[:, kc * T + tc * 128: kc * T + (tc + 1) * 128],
                                     wr_sb[:, kc * E2:(kc + 1) * E2],
                                     start=(kc == 0), stop=False)
                        gmks["gmm"] = b.pe.tag(e.matmul(
                            ps[:, 0:E2], K.ones_row[:], brow["brrow"][:],
                            start=False, stop=True))
                    b2.t(tmm)

                    def vg1(e, ps=ps):
                        wait(e, gmks["gmm"])
                        e.drain()
                        e.reduce_max(m1, ps[:, 0:E2], AX.X, negate=True)
                        e.drain()
                        gmks["g1"] = b.vec.tag(e.tensor_copy(pm, ps[:, 0:E2]))
                    b2.v(vg1)
                    pr2.set_free(pidx, gmks["g1"])

                    def sg(e):
                        wait(e, gmks["g1"])
                        gmks["g2"] = b.sca.tag(e.activation(p, pm, AF.Exp, bias=m1,
                                                            accum_out=s1))
                    b2.s(sg)

                    def vg2(e, tc=tc):
                        wait(e, gmks["g2"])
                        e.drain()
                        e.reciprocal(r1, s1)
                        e.drain()
                        e.tensor_scalar_mul(p, p, r1)
                        e.drain()
                        e.tensor_copy(pm, p)
                        e.drain()
                        for it in range(5):
                            e.reduce_max(m1, pm, AX.X)
                            e.drain()
                            e.tensor_single_scalar(ge, pm, m1, ALU.is_ge)
                            e.drain()
                            e.scalar_tensor_tensor(pm, ge, -9.0, pm, ALU.mult, ALU.add)
                            e.drain()
                        e.reduce_max(m1, pm, AX.X)
                        e.drain()
                        e.tensor_single_scalar(ge, p, m1, ALU.is_ge)
                        e.drain()
                        e.tensor_tensor(w, p, ge, ALU.mult)
                        e.drain()
                        e.reduce_sum(s1, w, AX.X)
                        e.drain()
                        e.reciprocal(r1, s1)
                        e.drain()
                        e.tensor_scalar_mul(w, w, r1)
                        e.drain()
                        gmks["gev"] = b.vec.tag(e.tensor_copy(
                            gsb[:, tc * E2:(tc + 1) * E2], w))
                        e.drain()
                        e.tensor_scalar_add(r1, s1, 1e-10)
                        e.drain()
                        e.reciprocal(r1, r1)
                        e.drain()
                        e.tensor_tensor(r1, r1, s1, ALU.mult)
                        e.drain()
                        gmks["gac"] = b.vec.tag(e.tensor_scalar_mul(
                            acsb[:, tc:tc + 1], r1, 1e-10))
                    b2.v(vg2)
            gate_mark = gmks["gac"]

            store_mark = ctx_mark
            for tg in range(NTG):
                mks = {}

                def yload(e, tg=tg, prev=store_mark, gm=gate_mark):
                    wait(e, prev, gm)
                    grp = b.yqq.begin(e)
                    for c in range(DC):
                        mks["xl"] = grp.tag(e.dma_start(
                            out=xtg[:, c * TGS:(c + 1) * TGS],
                            in_=xin[c * 128:(c + 1) * 128,
                                        tg * TGS:(tg + 1) * TGS]), 16)
                b2.y(yload)
                res_marks = []

                def evac_proj(mc, rid, ps, waits):
                    out = {}

                    def vev(e):
                        wait(e, *waits, mks["xl"])
                        e.drain()
                        out["m"] = b.vec.tag(e.scalar_tensor_tensor(
                            xtg[:, mc * TGS:(mc + 1) * TGS], ps[:],
                            colt["bprojc"][:, mc:mc + 1],
                            xtg[:, mc * TGS:(mc + 1) * TGS], ALU.add, ALU.add))
                    b2.v(vev)
                    res_marks.append(out["m"])
                    return out["m"]
                fm_proj(b2, wsp, DC, [tg],
                        lambda kc, rid: ctx_sb[:, kc * T + rid * TGS:
                                               kc * T + (rid + 1) * TGS],
                        evac_proj, pr2, DC, extra_waits=[ctx_mark])

                def vclr(e, rm=list(res_marks)):
                    wait(e, rm[-1] if rm else None)
                    e.drain()
                    mks["clr"] = b.vec.tag(e.memset(acc[:], 0.0))
                b2.v(vclr)

                for ee in range(E2):
                    def evac_e(tc, fb, ps, waits, ee=ee, tg=tg):
                        out = {}

                        def vev(e):
                            wait(e, *waits, mks["clr"])
                            e.drain()
                            out["m"] = b.vec.tag(e.scalar_tensor_tensor(
                                acc[:, tc * 1024 + fb * 512: tc * 1024 + (fb + 1) * 512],
                                ps[:, 0:512],
                                gsb[:, (tg * 4 + tc) * E2 + ee:
                                    (tg * 4 + tc) * E2 + ee + 1],
                                acc[:, tc * 1024 + fb * 512: tc * 1024 + (fb + 1) * 512],
                                ALU.mult, ALU.add))
                        b2.v(vev)
                        return out["m"]
                    omarks = tm_proj(b2, wse[ee], DC,
                                     lambda kc, tc: xn[:, kc * T + (tg * 4 + tc) * 128:
                                                       kc * T + (tg * 4 + tc + 1) * 128],
                                     evac_e, pr2, 4)
                    mks["moe"] = omarks[-1]

                def vac(e, tg=tg):
                    wait(e, mks["moe"])
                    e.drain()
                    for tc in range(4):
                        ins = e.tensor_scalar_add(
                            acc[:, tc * 1024:(tc + 1) * 1024],
                            acc[:, tc * 1024:(tc + 1) * 1024],
                            acsb[:, tg * 4 + tc: tg * 4 + tc + 1])
                    mks["ac"] = b.vec.tag(ins)
                b2.v(vac)

                for tc in range(4):
                    for mc in range(DC):
                        def ttp(e, tc=tc, mc=mc):
                            wait(e, mks["ac"])
                            pidx, ps, ps_free = pr2.get()
                            wait(e, ps_free)
                            mks["tp"] = b.pe.tag(e.matmul(
                                ps[:, 0:128],
                                acc[:, tc * 1024 + mc * 128: tc * 1024 + (mc + 1) * 128],
                                K.ident_f[:], start=True, stop=True))
                            ttp.ps = (pidx, ps)
                        b2.t(ttp)
                        pidx, ps = ttp.ps

                        def vadd(e, tc=tc, mc=mc, ps=ps, pidx=pidx):
                            wait(e, mks["tp"])
                            e.drain()
                            m = b.vec.tag(e.tensor_tensor(
                                xtg[:, mc * TGS + tc * 128: mc * TGS + (tc + 1) * 128],
                                xtg[:, mc * TGS + tc * 128: mc * TGS + (tc + 1) * 128],
                                ps[:, 0:128], ALU.add))
                            pr2.set_free(pidx, m)
                            vadd.mark = m
                        b2.v(vadd)
                        res_marks.append(vadd.mark)

                def ystore(e, tg=tg, rm=list(res_marks)):
                    wait(e, rm[-1])
                    grp = b.yqq.begin(e)
                    for c in range(DC):
                        ystore.mark = grp.tag(e.dma_start(
                            out=xout[c * 128:(c + 1) * 128, tg * TGS:(tg + 1) * TGS],
                            in_=xtg[:, c * TGS:(c + 1) * TGS]), 16)
                b2.y(ystore)
                store_mark = ystore.mark

    return store_mark


def build_block2(upto="full"):
    nc = bass.Bass("TRN2", target_bir_lowering=False)
    dr = {}
    xin = nc.dram_tensor("x", [D, T], F32, kind="ExternalInput").ap()
    for nm, shape in B2_SHAPES:
        dr[nm] = nc.dram_tensor(nm, shape, F32, kind="ExternalInput").ap()
    cs = make_consts()
    for nm in CONST_NAMES:
        dr[nm] = nc.dram_tensor(nm, list(cs[nm].shape), F32, kind="ExternalInput").ap()
    xout = nc.dram_tensor("xout", [D, T], F32, kind="ExternalOutput").ap()
    with ExitStack() as ctx:
        blk = ctx.enter_context(nc.Block())
        b = B(nc, ctx, blk)
        K = ConstTiles(b, dr)
        with ExitStack() as bctx:
            fin = emit_block2(b.sub(bctx), bctx, dr, K, xin, xout, upto=upto)
        if isinstance(fin, tuple):
            b.y(lambda e: wait(e, fin))
    return nc


def prep_block2_inputs(inputs):
    m = dict(make_consts())
    wat = _t(inputs["L2_attn_qkv_w"])
    m["wq"] = np.ascontiguousarray(wat[:, 0:D])
    m["wk"] = np.ascontiguousarray(wat[:, D:2 * D])
    m["wv"] = np.ascontiguousarray(wat[:, 2 * D:3 * D])
    ba = np.asarray(inputs["L2_attn_qkv_b"])
    m["bqc"] = _cols(ba[0:D])
    m["bkc"] = _cols(ba[D:2 * D])
    m["bvrow"] = np.ascontiguousarray(ba[2 * D:3 * D].reshape(1, D))
    m["wproj"] = _t(inputs["L2_attn_out_w"])
    m["bprojc"] = _cols(inputs["L2_attn_out_b"])
    m["wrouter"] = _t(inputs["L2_ffn_router_w"])
    m["brrow"] = np.ascontiguousarray(np.asarray(inputs["L2_ffn_router_b"]).reshape(1, 16))
    m["wexp"] = np.ascontiguousarray(np.asarray(inputs["L2_ffn_expert_w"]).reshape(16 * D, D))
    return m


def run_block2(x_fm_percore, inputs, upto="full"):
    nc = build_block2(upto=upto)
    base = prep_block2_inputs(inputs)
    in_maps = []
    for core in range(8):
        m = dict(base)
        m["x"] = np.ascontiguousarray(x_fm_percore[core], dtype=np.float32)
        in_maps.append(m)
    res = run_bass_kernel_spmd(nc, in_maps, core_ids=list(range(8)))
    return [res.results[c]["xout"] for c in range(8)]



DI = 2048
DIC = DI // 128
NST = 16


B0_SHAPES = [
    ("inproj", [D, 2 * DI]), ("convw", [128, DIC * 4]), ("convbc", [128, DIC]),
    ("xproj", [DI, 96]), ("dtproj", [64, DI]), ("dtbc", [128, DIC]),
    ("alogc", [128, DIC * NST]), ("dcolc", [128, DIC]), ("outproj", [DI, D]),
    ("fgw", [D, D]), ("fgbc", [128, DC]),
    ("wq", [D, D]), ("wk", [D, D]), ("wv", [D, D]), ("wo", [D, D]),
    ("bqc", [128, 8]), ("bkc", [128, 8]), ("bvrow", [1, D]), ("borow", [1, D]),
    ("g1c", [128, DC]), ("be1c", [128, DC]), ("g2c", [128, DC]),
    ("be2c", [128, DC]),
    ("wg", [D, 8]), ("bgrow", [1, 8]),
    ("we", [8 * D, D]), ("bexprow", [1, 8 * D]),
    ("sel16", [16, 16 * 128])]


def emit_block0(b, ctx, dr, K, xin, xout, upto="full"):
    nc = b.nc
    u_d = nc.dram_tensor("u_d", [128, DIC * T], BF16, kind="Internal").ap()
    zs_d = nc.dram_tensor("zs_d", [128, DIC * T], BF16, kind="Internal").ap()
    dt_d = nc.dram_tensor("dt_d", [128, DIC * T], F32, kind="Internal").ap()
    ba_d = nc.dram_tensor("ba_d", [128, DC * T], BF16, kind="Internal").ap()
    SCALE = 1.0 / 8.0
    TP4 = TGS + 4
    if True:
        xn = ctx.enter_context(nc.sbuf_tensor("xn", [128, DC * T], BF16))
        gsb = ctx.enter_context(nc.sbuf_tensor("gsb", [128, 16 * 8], F32))
        wg_sb = ctx.enter_context(nc.sbuf_tensor("k_wg", [128, DC * 8], BF16))
        colt = {}
        cmarks = []
        for nm, ncol in [("bqc", 8), ("bkc", 8), ("g1c", DC), ("be1c", DC),
                         ("g2c", DC), ("be2c", DC)]:
            colt[nm], m = load_cols(b, nm, dr, ncol)
            cmarks.append(m)
        brow = {}
        brow["bgrow"] = ctx.enter_context(nc.sbuf_tensor("k_bgrow", [1, 8], BF16))

        def gsmall(e):
            grp = b.gqq.begin(e)
            grp.tag(e.dma_start(out=brow["bgrow"][:], in_=dr["bgrow"]), 16)
            cmarks.append(grp.tag(e.dma_start(
                out=wg_sb[:].rearrange("p (kc m) -> p kc m", kc=DC),
                in_=dr["wg"].rearrange("(kc p) m -> p kc m", p=128)), 16))
        b.g(gsmall)
        K.barrier(b, extra=cmarks)

        xn_mark = emit_ln(b, xin, xn, K)

        phA = ctx.enter_context(ExitStack())
        bA = b.sub(phA)
        hstate = phA.enter_context(nc.sbuf_tensor("hstate", [128, DIC * NST], F32))
        asb = phA.enter_context(nc.sbuf_tensor("k_A", [128, DIC * NST], F32))
        bsb = phA.enter_context(nc.sbuf_tensor("bsb", [16, T], BF16))
        csb = phA.enter_context(nc.sbuf_tensor("csb", [16, T], BF16))
        sel16 = phA.enter_context(nc.sbuf_tensor("k_sel16", [16, 16 * 128], BF16))
        amarks = []
        for nm, ncol in [("convw", DIC * 4), ("convbc", DIC), ("dtbc", DIC),
                         ("alogc", DIC * NST), ("dcolc", DIC), ("fgbc", DC)]:
            colt[nm], m = load_cols(bA, nm, dr, ncol)
            amarks.append(m)
        s16m = {}

        def gsel(e):
            grp = b.gqq.begin(e)
            s16m["m"] = grp.tag(e.dma_start(out=sel16[:], in_=dr["sel16"]), 16)
        b.g(gsel)
        amks = {}

        def sA(e, w=list(amarks)):
            wait(e, *w)
            amks["exp"] = b.sca.tag(e.activation(asb[:], colt["alogc"][:], AF.Exp))
        b.s(sA)

        def vA(e):
            wait(e, amks["exp"], s16m["m"])
            e.drain()
            e.tensor_scalar_mul(asb[:], asb[:], -1.0)
            e.drain()
            amks["A"] = b.vec.tag(e.memset(hstate[:], 0.0))
        b.v(vA)

        fe_done = []
        with ExitStack() as phf:
            bf_ = b.sub(phf)
            prf = PsumRot(bf_, "prM", 4)
            pool_m = WPool(bf_, "pM", 8 * 512, 2)
            wsip = WStream(bf_, pool_m, dr["inproj"], D, 2 * DI, 512)
            xip = phf.enter_context(nc.sbuf_tensor("xip", [128, DIC * TP4], BF16))
            cvt = phf.enter_context(nc.sbuf_tensor("cvt", [128, DIC * TGS], F32))
            l1s = phf.enter_context(nc.sbuf_tensor("l1s", [128, DIC * TGS // 2], F32))
            dtraw = phf.enter_context(nc.sbuf_tensor("dtraw", [128, DIC * TGS], F32))
            xig = phf.enter_context(nc.sbuf_tensor("xig", [128, DIC * TGS], BF16))
            zst = phf.enter_context(nc.sbuf_tensor("zst", [128, DIC * TGS], BF16))
            dtin = phf.enter_context(nc.sbuf_tensor("dtin", [64, TGS], BF16))
            xproj_sb = phf.enter_context(nc.sbuf_tensor("k_xproj", [128, DIC * 96], BF16))
            dtproj_sb = phf.enter_context(nc.sbuf_tensor("k_dtproj", [64, DI], BF16))
            p_dt = phf.enter_context(nc.psum_tensor("p_dt", [64, TGS], F32))
            p_bb = phf.enter_context(nc.psum_tensor("p_bb", [16, TGS], F32))
            p_cc = phf.enter_context(nc.psum_tensor("p_cc", [16, TGS], F32))
            pm = {}

            def gproj(e):
                grp = b.gqq.begin(e)
                grp.tag(e.dma_start(
                    out=xproj_sb[:].rearrange("p (kc m) -> p kc m", kc=DIC),
                    in_=dr["xproj"].rearrange("(kc p) m -> p kc m", p=128)), 16)
                pm["w"] = grp.tag(e.dma_start(out=dtproj_sb[:], in_=dr["dtproj"]), 16)
            b.g(gproj)

            def vinit(e):
                wait(e, xn_mark)
                e.drain()
                pm["hal"] = b.vec.tag(e.memset(xip[:], 0.0))
            b.v(vinit)
            pm["sp"] = None
            pm["dtst"] = None

            for tg in range(NTG):
                fmk = {}
                halo_prev = pm["hal"]

                def evac_xz(mc, rid, ps, waits, halo_prev=halo_prev):
                    if mc < DIC:
                        out = {}

                        def vev(e):
                            wait(e, *waits, halo_prev)
                            e.drain()
                            out["m"] = b.vec.tag(e.tensor_copy(
                                xip[:, mc * TP4 + 4:(mc + 1) * TP4], ps[:]))
                        b.v(vev)
                        return out["m"]
                    em = b.sca_reserve()

                    def sev(e, udp=fmk.get("ud_prev")):
                        wait(e, *waits, udp)
                        e.activation(
                            zst[:, (mc - DIC) * TGS:(mc - DIC + 1) * TGS],
                            ps[:], AF.Silu).then_inc(b.sca.h, 1)
                    b.s(sev)
                    return em
                fmk["ud_prev"] = pm.get("ud")
                b.s_defer()
                xz_marks = fm_proj(bf_, wsip, DC, [tg],
                                   lambda kc, rid: xn[:, kc * T + rid * TGS:
                                                      kc * T + (rid + 1) * TGS],
                                   evac_xz, prf, 2 * DIC, extra_waits=[xn_mark])
                b.s_flush()

                def vconv(e, w=list(xz_marks), sp_prev=pm["sp"]):
                    wait(e, *w, sp_prev)
                    e.drain()
                    ins = None
                    for dcc in range(DIC):
                        base = dcc * TP4
                        o = cvt[:, dcc * TGS:(dcc + 1) * TGS]
                        e.tensor_scalar_mul(o, xip[:, base + 1: base + 1 + TGS],
                                            colt["convw"][:, dcc * 4:dcc * 4 + 1])
                        for j in range(1, 4):
                            e.drain()
                            ins = e.scalar_tensor_tensor(
                                o, xip[:, base + 1 + j: base + 1 + j + TGS],
                                colt["convw"][:, dcc * 4 + j: dcc * 4 + j + 1],
                                o, ALU.mult, ALU.add)
                        e.drain()
                    fmk["cv"] = b.vec.tag(ins)
                b.v(vconv)

                def sxi(e, ud_prev=pm.get("ud")):
                    wait(e, fmk["cv"], ud_prev)
                    ins = None
                    for dcc in range(DIC):
                        ins = e.activation(
                            xig[:, dcc * TGS:(dcc + 1) * TGS],
                            cvt[:, dcc * TGS:(dcc + 1) * TGS],
                            AF.Silu, bias=colt["convbc"][:, dcc:dcc + 1])
                    fmk["xi"] = b.sca.tag(ins)
                b.s(sxi)

                def vhalo(e):
                    wait(e, fmk["cv"])
                    e.drain()
                    ins = None
                    for dcc in range(DIC):
                        base = dcc * TP4
                        ins = e.tensor_copy(xip[:, base + 1:base + 4],
                                            xip[:, base + TGS + 1:base + TGS + 4])
                    pm["hal"] = b.vec.tag(ins)
                b.v(vhalo)

                def gu(e, tg=tg):
                    wait(e, fmk["xi"])
                    grp = b.gqq.begin(e)
                    for dcc in range(DIC):
                        grp.tag(e.dma_start(
                            out=u_d[:, dcc * T + tg * TGS: dcc * T + (tg + 1) * TGS],
                            in_=xig[:, dcc * TGS:(dcc + 1) * TGS]), 16)
                        pm["ud"] = grp.tag(e.dma_start(
                            out=zs_d[:, dcc * T + tg * TGS: dcc * T + (tg + 1) * TGS],
                            in_=zst[:, dcc * TGS:(dcc + 1) * TGS]), 16)
                b.g(gu)

                def tdbl(e, bc_prev=fmk.get("bc_prev")):
                    wait(e, fmk["xi"], pm.get("bc_done"), pm.get("dtin_done"),
                         pm["w"])
                    for kc in range(DIC):
                        e.matmul(p_dt[:], xproj_sb[:, kc * 96:kc * 96 + 64],
                                 xig[:, kc * TGS:(kc + 1) * TGS],
                                 start=(kc == 0), stop=(kc == DIC - 1))
                    for kc in range(DIC):
                        e.matmul(p_bb[:], xproj_sb[:, kc * 96 + 64:kc * 96 + 80],
                                 xig[:, kc * TGS:(kc + 1) * TGS],
                                 start=(kc == 0), stop=(kc == DIC - 1))
                    for kc in range(DIC):
                        ins = e.matmul(p_cc[:],
                                       xproj_sb[:, kc * 96 + 80:kc * 96 + 96],
                                       xig[:, kc * TGS:(kc + 1) * TGS],
                                       start=(kc == 0), stop=(kc == DIC - 1))
                    fmk["dbl"] = b.pe.tag(ins)
                b.t(tdbl)

                def vbc(e, tg=tg):
                    wait(e, fmk["dbl"])
                    e.drain()
                    e.tensor_copy(bsb[:, tg * TGS:(tg + 1) * TGS], p_bb[:])
                    e.drain()
                    ins = e.tensor_copy(csb[:, tg * TGS:(tg + 1) * TGS], p_cc[:])
                    pm["bc_done"] = b.vec.tag(ins)
                b.v(vbc)

                def sdtin(e):
                    wait(e, fmk["dbl"])
                    pm["dtin_done"] = b.sca.tag(
                        e.activation(dtin[:], p_dt[:], AF.Identity))
                b.s(sdtin)
                fmk["dtin"] = pm["dtin_done"]

                sp_prev = pm["sp"]
                dtst_prev = pm["dtst"]
                dt_free0 = [prf.free[0], prf.free[1]]
                for mc in range(DIC):
                    def tdt(e, mc=mc):
                        w = [fmk["dtin"], pm["w"]]
                        if mc >= 2:
                            w.append(fmk[f"spv{mc - 2}"])
                        else:
                            w.append(dt_free0[mc])
                            w.append(sp_prev)
                        wait(e, *w)
                        fmk[f"dtmm{mc}"] = b.pe.tag(e.matmul(
                            prf.tiles[mc % 2][:],
                            dtproj_sb[:, mc * 128:(mc + 1) * 128],
                            dtin[:], start=True, stop=True))
                    b.t(tdt)

                    def sraw(e, mc=mc):
                        wait(e, fmk[f"dtmm{mc}"], sp_prev, dtst_prev)
                        fmk[f"spv{mc}"] = b.sca.tag(e.activation(
                            dtraw[:, mc * TGS:(mc + 1) * TGS],
                            prf.tiles[mc % 2][:], AF.Identity,
                            bias=colt["dtbc"][:, mc:mc + 1]))
                    b.s(sraw)
                prf.set_free(0, fmk[f"spv{DIC - 2}"])
                prf.set_free(1, fmk[f"spv{DIC - 1}"])

                HL = DIC * TGS // 2
                for hf in range(2):
                    slh = slice(hf * HL, (hf + 1) * HL)

                    def ssp(e, slh=slh, dtst_prev=pm["dtst"], sp_prev2=pm["sp"]):
                        wait(e, fmk[f"spv{DIC - 1}"], dtst_prev, sp_prev2)
                        e.activation(cvt[:, slh], dtraw[:, slh], AF.Abs)
                        e.drain()
                        e.activation(l1s[:], cvt[:, slh], AF.Exp, scale=-1.0)
                        e.drain()
                        fmk["lnv"] = b.sca.tag(e.activation(l1s[:], l1s[:], AF.Ln,
                                                            bias=1.0))
                    b.s(ssp)

                    def vsp2(e, slh=slh, dtst_prev=pm["dtst"]):
                        wait(e, fmk["lnv"], dtst_prev)
                        e.drain()
                        e.tensor_relu(cvt[:, slh], dtraw[:, slh])
                        e.drain()
                        ins = e.tensor_tensor(dtraw[:, slh], cvt[:, slh], l1s[:],
                                              ALU.add)
                        pm["sp"] = b.vec.tag(ins)
                    b.v(vsp2)

                def ydt(e, tg=tg):
                    wait(e, pm["sp"])
                    grp = b.yqq.begin(e)
                    for dcc in range(DIC):
                        pm["dtst"] = grp.tag(e.dma_start(
                            out=dt_d[:, dcc * T + tg * TGS: dcc * T + (tg + 1) * TGS],
                            in_=dtraw[:, dcc * TGS:(dcc + 1) * TGS]), 16)
                b.y(ydt)

            fe_done = [pm["dtst"], pm["ud"], pm["hal"], pm["sp"], pm["bc_done"],
                       pm["dtin_done"], amks["A"]]

        y_d = nc.dram_tensor("y_d", [128, DIC * T], BF16, kind="Internal").ap()
        brep_d = nc.dram_tensor("brep_d", [128, NST * T], BF16, kind="Internal").ap()
        crep_d = nc.dram_tensor("crep_d", [128, NST * T], BF16, kind="Internal").ap()
        with ExitStack() as phs:
            bs_ = b.sub(phs)
            with ExitStack() as phb:
                brs = [phb.enter_context(nc.sbuf_tensor(f"brs{i}", [128, T], BF16))
                       for i in range(2)]
                p_bc = [phb.enter_context(nc.psum_tensor(f"p_bc{i}", [128, TGS], F32))
                        for i in range(2)]
                sbm = {"ev0": None, "ev1": None, "st0": None, "st1": None}
                for n in range(NST):
                    for src, dst_d in ((bsb, brep_d), (csb, crep_d)):
                        si = 0 if src is bsb else 1
                        for g in range(NTG):
                            pi = g % 2

                            def tbc(e, n=n, g=g, pi=pi, src=src,
                                    w=(sbm[f"ev{pi}"], fe_done if n == 0 and g == 0
                                       and src is bsb else None)):
                                wait(e, w[0], w[1], s16m["m"])
                                sbm["mm"] = b.pe.tag(e.matmul(
                                    p_bc[pi][:], sel16[:, n * 128:(n + 1) * 128],
                                    src[:, g * TGS:(g + 1) * TGS],
                                    start=True, stop=True))
                            b.t(tbc)

                            def sev(e, g=g, pi=pi, si=si,
                                    w=(sbm["mm"], sbm[f"st{si}"])):
                                wait(e, *[m for m in w if m])
                                sbm[f"ev{pi}"] = b.sca.tag(e.activation(
                                    brs[si][:, g * TGS:(g + 1) * TGS], p_bc[pi][:],
                                    AF.Identity))
                            b.s(sev)
                        evm = (sbm["ev0"], sbm["ev1"])

                        def yst(e, n=n, si=si, dst_d=dst_d, w=evm):
                            wait(e, *[m for m in w if m])
                            grp = b.yqq.begin(e)
                            sbm[f"st{si}"] = grp.tag(e.dma_start(
                                out=dst_d[:, n * T:(n + 1) * T], in_=brs[si][:]), 16)
                        b.y(yst)
                sb_done = (sbm["st0"], sbm["st1"])

            QD = 2
            dtq = phs.enter_context(nc.sbuf_tensor("dtq", [128, QD * T], F32))
            uq = phs.enter_context(nc.sbuf_tensor("uq", [128, QD * T], BF16))
            zq = phs.enter_context(nc.sbuf_tensor("zq", [128, QD * T], BF16))
            dtu = [phs.enter_context(nc.sbuf_tensor(f"dtu{j}", [128, T], F32))
                   for j in range(QD)]
            dAs = [phs.enter_context(nc.sbuf_tensor(f"dA{i}", [128, T], F32))
                   for i in range(4)]
            d1s = phs.enter_context(nc.sbuf_tensor("d1s", [128, T], F32))
            scs = phs.enter_context(nc.sbuf_tensor("scs", [128, T], F32))
            hCs = [phs.enter_context(nc.sbuf_tensor(f"hC{i}", [128, T], F32))
                   for i in range(4)]
            bsl = [phs.enter_context(nc.sbuf_tensor(f"bsl{i}", [128, T], BF16))
                   for i in range(2)]
            csl = [phs.enter_context(nc.sbuf_tensor(f"csl{i}", [128, T], BF16))
                   for i in range(2)]
            y_bf = [phs.enter_context(nc.sbuf_tensor(f"y_bf{j}", [128, T], BF16))
                    for j in range(QD)]
            zcol = phs.enter_context(nc.sbuf_tensor("zcol", [128, 1], F32))
            p_y = [phs.enter_context(nc.psum_tensor(f"p_y{i}", [128, TGS], F32))
                   for i in range(8)]
            smk = {}

            def vz(e):
                e.drain()
                smk["zc"] = b.vec.tag(e.memset(zcol[:], 0.0))
            b.v(vz)

            scm = {}
            dam = {}
            yam = {}
            b.s_defer()
            NQ = DIC // QD
            for q in range(NQ):
                if q == 0:
                    w_prev = list(fe_done) + [sb_done[0], sb_done[1]]
                else:
                    w_prev = [m for m in
                              ([dam.get((q * QD - 1 - j, NST - 1)) for j in range(QD)]
                               + [smk.get("vyf"), smk.get("dtu")]) if m]

                def yql(e, q=q, w=tuple(w_prev)):
                    wait(e, *[m for m in w if m])
                    grp = b.yqq.begin(e)
                    for j in range(QD):
                        dcc = q * QD + j
                        smk["dtq"] = grp.tag(e.dma_start(
                            out=dtq[:, j * T:(j + 1) * T],
                            in_=dt_d[:, dcc * T:(dcc + 1) * T]), 16)
                b.y(yql)

                def gql(e, q=q, w=tuple(w_prev)):
                    wait(e, *[m for m in w if m])
                    grp = b.gqq.begin(e)
                    for j in range(QD):
                        dcc = q * QD + j
                        grp.tag(e.dma_start(
                            out=uq[:, j * T:(j + 1) * T],
                            in_=u_d[:, dcc * T:(dcc + 1) * T]), 16)
                        smk["uzq"] = grp.tag(e.dma_start(
                            out=zq[:, j * T:(j + 1) * T],
                            in_=zs_d[:, dcc * T:(dcc + 1) * T]), 16)
                b.g(gql)

                for j in range(QD):
                    def vdtu(e, j=j, w=(smk["dtq"], smk["uzq"])):
                        wait(e, *w)
                        e.drain()
                        smk["dtu"] = b.vec.tag(e.tensor_tensor(
                            dtu[j][:], dtq[:, j * T:(j + 1) * T],
                            uq[:, j * T:(j + 1) * T], ALU.mult))
                    b.v(vdtu)
                dtu_mark = smk["dtu"]

                for n in range(NST):
                    sl = n % 2
                    pc_j = q * QD + QD - 1
                    prev_cons = scm.get((pc_j, n - 2)) if n >= 2 else \
                        scm.get((pc_j - QD, NST - 2 + n))

                    def ybc(e, n=n, sl=sl,
                            w=(prev_cons, sb_done[0], sb_done[1])):
                        wait(e, *[m for m in w if m])
                        grp = b.yqq.begin(e)
                        grp.tag(e.dma_start(
                            out=bsl[sl][:], in_=brep_d[:, n * T:(n + 1) * T]), 16)
                        smk["bc_ld"] = grp.tag(e.dma_start(
                            out=csl[sl][:], in_=crep_d[:, n * T:(n + 1) * T]), 16)
                    b.y(ybc)

                    for j in range(QD):
                        dci = q * QD + j
                        col = dci * NST + n
                        dsl = 2 * (n % 2) + j
                        da_prev = scm.get((dci, n - 2)) if n >= 2 else \
                            scm.get((dci - QD, NST - 2 + n))
                        dam[(dci, n)] = b.sca_reserve()

                        def sda(e, j=j, dsl=dsl, col=col,
                                w=(da_prev, smk["dtq"])):
                            wait(e, *[m for m in w if m])
                            e.activation(
                                dAs[dsl][:], dtq[:, j * T:(j + 1) * T], AF.Exp,
                                scale=asb[:, col:col + 1]).then_inc(b.sca.h, 1)
                        b.s(sda)

                        hc_prev = yam.get((dci, n - 2)) if n >= 2 else \
                            yam.get((dci - QD, NST - 2 + n))

                        def vsc(e, j=j, sl=sl, dsl=dsl,
                                w=(smk["bc_ld"], dam[(dci, n)],
                                   dtu_mark, smk["zc"], hc_prev)):
                            wait(e, *[m for m in w if m])
                            e.drain()
                            e.tensor_tensor(d1s[:], dtu[j][:], bsl[sl][:], ALU.mult)
                            e.drain()
                            e.tensor_tensor_scan(scs[:], dAs[dsl][:], d1s[:],
                                                 zcol[:, 0:1], ALU.mult, ALU.add)
                            e.drain()
                            scm[(dci, n)] = b.vec.tag(e.tensor_tensor(
                                hCs[dsl][:], scs[:], csl[sl][:], ALU.mult))
                        b.v(vsc)

                        def tya(e, j=j, dsl=dsl, n=n,
                                w=(scm[(dci, n)],
                                   smk.get("vyf") if n == 0 else None)):
                            wait(e, *[m for m in w if m])
                            for g in range(NTG):
                                ins = e.matmul(
                                    p_y[j * 4 + g][:], K.ident_f[:],
                                    hCs[dsl][:, g * TGS:(g + 1) * TGS],
                                    start=(n == 0), stop=(n == NST - 1))
                            yam[(dci, n)] = b.pe.tag(ins)
                        b.t(tya)

                for j in range(QD):
                    dci = q * QD + j

                    def vyf(e, j=j, dci=dci, w=(yam[(dci, NST - 1)],
                                                smk.get(f"yst{j}"))):
                        wait(e, *[m for m in w if m])
                        e.drain()
                        ins = None
                        for g in range(NTG):
                            gs = slice(g * TGS, (g + 1) * TGS)
                            js = slice(j * T + g * TGS, j * T + (g + 1) * TGS)
                            e.scalar_tensor_tensor(
                                scs[:, gs], uq[:, js],
                                colt["dcolc"][:, dci:dci + 1],
                                p_y[j * 4 + g][:], ALU.mult, ALU.add)
                            e.drain()
                            ins = e.tensor_tensor(y_bf[j][:, gs], scs[:, gs],
                                                  zq[:, js], ALU.mult)
                            e.drain()
                        smk["vyf"] = b.vec.tag(ins)
                    b.v(vyf)

                    def yyst(e, j=j, dci=dci, w=(smk["vyf"],)):
                        wait(e, *w)
                        grp = b.yqq.begin(e)
                        smk[f"yst{j}"] = grp.tag(e.dma_start(
                            out=y_d[:, dci * T:(dci + 1) * T], in_=y_bf[j][:]), 16)
                    b.y(yyst)
            b.s_flush()
            scan_done = [smk["yst0"], smk["yst1"], smk["vyf"],
                         dam[(DIC - 1, NST - 1)], dam[(DIC - 2, NST - 1)],
                         smk["dtu"]]

        with ExitStack() as php:
            bs_ = b.sub(php)
            prs = PsumRot(bs_, "prS", 4, init_free=list(scan_done))
            pool_s = WPool(bs_, "pS", 8 * 512, 2, init_free=list(scan_done))
            wsop = WStream(bs_, pool_s, dr["outproj"], DI, D, 256)
            wsfg = WStream(bs_, pool_s, dr["fgw"], D, D, 256)
            y_tg = php.enter_context(nc.sbuf_tensor("y_tg", [128, DIC * TGS], BF16))
            fg_sb = php.enter_context(nc.sbuf_tensor("fg_sb", [128, DC * TGS], BF16))
            ba_sb = php.enter_context(nc.sbuf_tensor("ba_sb", [128, DC * TGS], BF16))
            scr = php.enter_context(nc.sbuf_tensor("scr", [128, TGS], F32))
            smk = {}

            for tg in range(NTG):
                def yyl(e, tg=tg, w=(scan_done[0], scan_done[1],
                                     smk.get("op_done"))):
                    wait(e, *[m for m in w if m])
                    grp = b.yqq.begin(e)
                    for dcc in range(DIC):
                        smk["y_ld"] = grp.tag(e.dma_start(
                            out=y_tg[:, dcc * TGS:(dcc + 1) * TGS],
                            in_=y_d[:, dcc * T + tg * TGS: dcc * T + (tg + 1) * TGS]), 16)
                b.y(yyl)

                def evac_fg(mc, rid, ps, waits):
                    em = b.sca_reserve()

                    def sev(e):
                        wait(e, *waits)
                        e.activation(
                            fg_sb[:, mc * TGS:(mc + 1) * TGS], ps[:], AF.Sigmoid,
                            bias=colt["fgbc"][:, mc:mc + 1]).then_inc(b.sca.h, 1)
                    b.s(sev)
                    return em
                b.s_defer()
                fgm = fm_proj(bs_, wsfg, DC, [tg],
                              lambda kc, rid: xn[:, kc * T + rid * TGS:
                                                 kc * T + (rid + 1) * TGS],
                              evac_fg, prs, DC,
                              extra_waits=[m for m in (smk.get("ba_st"),) if m])
                b.s_flush()

                def evac_op(mc, rid, ps, waits, tg=tg, fgm=fgm):
                    out = {}

                    def vev(e, yl=smk["y_ld"]):
                        wait(e, *waits, *fgm, yl)
                        e.drain()
                        e.tensor_tensor(scr[:],
                                        xn[:, mc * T + tg * TGS: mc * T + (tg + 1) * TGS],
                                        ps[:], ALU.subtract)
                        e.drain()
                        e.tensor_tensor(scr[:],
                                        fg_sb[:, mc * TGS:(mc + 1) * TGS],
                                        scr[:], ALU.mult)
                        e.drain()
                        out["m"] = b.vec.tag(e.tensor_tensor(
                            ba_sb[:, mc * TGS:(mc + 1) * TGS], ps[:],
                            scr[:], ALU.add))
                    b.v(vev)
                    smk["op_done"] = out["m"]
                    return out["m"]
                opm = fm_proj(bs_, wsop, DIC, [tg],
                              lambda kc, rid: y_tg[:, kc * TGS:(kc + 1) * TGS],
                              evac_op, prs, DC, extra_waits=[smk["y_ld"]])

                def gba(e, tg=tg, w=list(opm)):
                    wait(e, *w)
                    grp = b.gqq.begin(e)
                    for mc in range(DC):
                        smk["ba_st"] = grp.tag(e.dma_start(
                            out=ba_d[:, mc * T + tg * TGS: mc * T + (tg + 1) * TGS],
                            in_=ba_sb[:, mc * TGS:(mc + 1) * TGS]), 16)
                b.g(gba)
            ba_done = smk["ba_st"]

        if upto == "mamba":
            def gdbg(e):
                wait(e, ba_done)
                grp = b.gqq.begin(e)
                for c in range(DC):
                    m = grp.tag(e.dma_start(out=xout[c * 128:(c + 1) * 128, :],
                                            in_=ba_d[:, c * T:(c + 1) * T]), 16)
                wait(e, m)
            b.g(gdbg)
            return nc

        phA.close()

        x1_d = nc.dram_tensor("x1_d", [128, DC * T], BF16, kind="Internal").ap()
        phBB = ctx.enter_context(ExitStack())
        ctx_sb = phBB.enter_context(nc.sbuf_tensor("ctx_sb", [128, DC * T], BF16))
        with ExitStack() as ph:
            q_sb = ph.enter_context(nc.sbuf_tensor("q_sb", [128, DC * T], BF16))
            k_sb = ph.enter_context(nc.sbuf_tensor("k_sb", [128, DC * T], BF16))
            v_sb = ph.enter_context(nc.sbuf_tensor("v_sb", [128, 16 * VSTRIDE], BF16))
            bq = b.sub(ph)
            qkv_marks = []
            with ExitStack() as phw:
                bw = b.sub(phw)
                pr = PsumRot(bw, "prA", 4)
                pool_a = WPool(bw, "pA", 4 * 1024, 2, init_free=ba_done)
                brow["bvrow"] = phw.enter_context(
                    nc.sbuf_tensor("k_bvrow", [1, D], BF16))
                bvm = {}

                def gbv(e):
                    wait(e, ba_done)
                    grp = b.gqq.begin(e)
                    bvm["m"] = grp.tag(e.dma_start(out=brow["bvrow"][:],
                                                   in_=dr["bvrow"]), 16)
                b.g(gbv)

                for wnm, dst, bc in [("wq", q_sb, "bqc"), ("wk", k_sb, "bkc")]:
                    ws = WStream(bw, pool_a, dr[wnm], D, D, 512)

                    def evac(mc, rid, ps, waits, dst=dst, bc=bc):
                        out = {}

                        def sev(e):
                            wait(e, *waits)
                            out["m"] = b.sca.tag(e.activation(
                                dst[:, mc * T + rid * TGS: mc * T + (rid + 1) * TGS],
                                ps[:], AF.Identity, bias=colt[bc][:, mc:mc + 1]))
                        b.s(sev)
                        return out["m"]
                    qkv_marks += fm_proj(bw, ws, DC, list(range(NTG)),
                                         lambda kc, rid: xn[:, kc * T + rid * TGS:
                                                            kc * T + (rid + 1) * TGS],
                                         evac, pr, DC,
                                         extra_waits=[xn_mark, ba_done])

                vms = {}

                def vmemset(e):
                    wait(e, xn_mark, ba_done)
                    e.drain()
                    vms["m"] = b.vec.tag(e.memset(v_sb[:], 1.0))
                b.v(vmemset)

                wsv = WStream(bw, pool_a, dr["wv"], D, D, 512)

                def vevac(tc, fb, ps, waits):
                    out = {}

                    def vev(e):
                        wait(e, *waits, vms["m"])
                        e.drain()
                        for j in range(8):
                            h = fb * 8 + j
                            out["m"] = b.vec.tag(e.tensor_copy(
                                v_sb[:, tc * VSTRIDE + h * 65: tc * VSTRIDE + h * 65 + 64],
                                ps[:, j * 64:(j + 1) * 64]))
                    b.v(vev)
                    return out["m"]
                qkv_marks += tm_proj(bw, wsv, DC,
                                     lambda kc, tc: xn[:, kc * T + tc * 128:
                                                       kc * T + (tc + 1) * 128],
                                     vevac, pr, 16,
                                     extra_waits=[xn_mark, bvm["m"]],
                                     bias_row=lambda fb: brow["bvrow"][:, fb * 512:(fb + 1) * 512],
                                     ones_row=K.ones_row)

            ctx_mark = emit_attention_mh(bq, K, q_sb, k_sb, v_sb, ctx_sb,
                                         SCALE, False, qkv_marks, "mha")

        with ExitStack() as ph3:
            b3 = b.sub(ph3)
            pr3 = PsumRot(b3, "prC", 4)
            pool_c = WPool(b3, "pC", 8 * 512, 2, init_free=ctx_mark)
            ao_sb = ph3.enter_context(nc.sbuf_tensor("ao_sb", [128, 16 * D], BF16))
            lsc = ph3.enter_context(nc.sbuf_tensor("lsc", [128, D + 8], F32))
            x1st = ph3.enter_context(nc.sbuf_tensor("x1st", [128, DC * 128], BF16))
            brow["borow"] = ph3.enter_context(nc.sbuf_tensor("k_borow", [1, D], BF16))
            bom = {}

            def gbo(e):
                wait(e, ctx_mark)
                grp = b.gqq.begin(e)
                bom["m"] = grp.tag(e.dma_start(out=brow["borow"][:], in_=dr["borow"]), 16)
            b.g(gbo)

            wso = WStream(b3, pool_c, dr["wo"], D, D, 512)

            def evac_ao(tc, fb, ps, waits):
                out = {}

                def vev(e):
                    wait(e, *waits)
                    e.drain()
                    out["m"] = b.vec.tag(e.tensor_copy(
                        ao_sb[:, tc * D + fb * 512: tc * D + (fb + 1) * 512],
                        ps[:, 0:512]))
                b.v(vev)
                return out["m"]
            aom = tm_proj(b3, wso, DC,
                          lambda kc, tc: ctx_sb[:, kc * T + tc * 128:
                                                kc * T + (tc + 1) * 128],
                          evac_ao, pr3, 16, extra_waits=[ctx_mark, bom["m"]],
                          bias_row=lambda fb: brow["borow"][:, fb * 512:(fb + 1) * 512],
                          ones_row=K.ones_row)

            lmk = {"x1": None}
            mean = lsc[:, D:D + 1]
            s2c_ = lsc[:, D + 1:D + 2]
            rstd = lsc[:, D + 2:D + 3]

            def scrx():
                return lsc[:, 0:128]

            for tc in range(16):
                def vst1(e, tc=tc, w=list(aom)):
                    wait(e, *w, lmk["x1"])
                    e.drain()
                    e.reduce_sum(mean, ao_sb[:, tc * D:(tc + 1) * D], AX.X)
                    e.drain()
                    lmk["mean"] = b.vec.tag(e.tensor_scalar_mul(mean, mean, 1.0 / D))
                b.v(vst1)

                def sst1(e, tc=tc):
                    wait(e, lmk["mean"])
                    lmk["sq"] = b.sca.tag(e.activation(
                        lsc[:, 0:D], ao_sb[:, tc * D:(tc + 1) * D], AF.Square,
                        accum_out=s2c_))
                b.s(sst1)

                def vst2(e, tc=tc):
                    wait(e, lmk["sq"])
                    e.drain()
                    e.tensor_scalar_mul(s2c_, s2c_, 1.0 / D)
                    e.drain()
                    e.tensor_tensor(lsc[:, D + 3:D + 4], mean, mean, ALU.mult)
                    e.drain()
                    lmk["var"] = b.vec.tag(e.tensor_tensor(
                        s2c_, s2c_, lsc[:, D + 3:D + 4], ALU.subtract))
                b.v(vst2)

                def sst2(e):
                    wait(e, lmk["var"])
                    e.activation(rstd, s2c_, AF.Ln, bias=EPS_LN)
                    e.drain()
                    lmk["rstd"] = b.sca.tag(e.activation(rstd, rstd, AF.Exp,
                                                         scale=-0.5))
                b.s(sst2)

                def vnorm(e, tc=tc):
                    wait(e, lmk["rstd"])
                    e.drain()
                    lmk["nrm"] = b.vec.tag(e.tensor_scalar(
                        lsc[:, 0:D], ao_sb[:, tc * D:(tc + 1) * D], mean, rstd,
                        ALU.subtract, ALU.mult))
                b.v(vnorm)

                for mc in range(DC):
                    pidx, psx, ps_free = pr3.get()

                    def ttp(e, tc=tc, mc=mc, psx=psx, ps_free=ps_free):
                        wait(e, lmk["nrm"], ps_free)
                        lmk["tp"] = b.pe.tag(e.matmul(
                            psx[:, 0:128], lsc[:, mc * 128:(mc + 1) * 128],
                            K.ident_f[:], start=True, stop=True))
                    b3.t(ttp)

                    def vx1(e, tc=tc, mc=mc, psx=psx):
                        wait(e, lmk["tp"], lmk.get("x1dma") if mc == 0 else None)
                        e.drain()
                        e.tensor_scalar(scrx(), psx[:, 0:128],
                                        colt["g1c"][:, mc:mc + 1],
                                        colt["be1c"][:, mc:mc + 1],
                                        ALU.mult, ALU.add)
                        e.drain()
                        ins = e.tensor_tensor(
                            x1st[:, mc * 128:(mc + 1) * 128],
                            xn[:, mc * T + tc * 128: mc * T + (tc + 1) * 128],
                            scrx(), ALU.add)
                        m = b.vec.tag(ins)
                        pr3.set_free(pidx, m)
                        lmk["x1"] = m
                    b3.v(vx1)

                def gx1(e, tc=tc):
                    wait(e, lmk["x1"])
                    grp = b.gqq.begin(e)
                    for mc in range(DC):
                        lmk["x1dma"] = grp.tag(e.dma_start(
                            out=x1_d[:, mc * T + tc * 128: mc * T + (tc + 1) * 128],
                            in_=x1st[:, mc * 128:(mc + 1) * 128]), 16)
                b3.g(gx1)
            x1_mark = lmk["x1dma"]

        if upto == "x1":
            def gdbg(e):
                wait(e, x1_mark)
                grp = b.gqq.begin(e)
                for c in range(DC):
                    m = grp.tag(e.dma_start(out=xout[c * 128:(c + 1) * 128, :],
                                            in_=x1_d[:, c * T:(c + 1) * T]), 16)
                wait(e, m)
            b.g(gdbg)
            return nc

        phBB.close()

        with ExitStack() as ph2:
            b2 = b.sub(ph2)
            x1_sb = ph2.enter_context(nc.sbuf_tensor("x1_sb", [128, DC * T], BF16))
            x1m = {}

            def gx1l(e):
                wait(e, x1_mark)
                grp = b.gqq.begin(e)
                for c in range(DC):
                    x1m["m"] = grp.tag(e.dma_start(
                        out=x1_sb[:, c * T:(c + 1) * T],
                        in_=x1_d[:, c * T:(c + 1) * T]), 16)
            b2.g(gx1l)
            xtg = ph2.enter_context(nc.sbuf_tensor("xtg", [128, DC * TGS], F32))
            acc = ph2.enter_context(nc.sbuf_tensor("acc", [128, 4 * 1024], F32))
            bax = ph2.enter_context(nc.sbuf_tensor("bax", [128, DC * TGS], BF16))
            lsc2 = ph2.enter_context(nc.sbuf_tensor("lsc2", [128, D + 8], F32))
            pr2 = PsumRot(b2, "prB", 4)
            pool_f = WPool(b2, "pF", 8 * 1024, 2, init_free=x1_mark)
            brow["bexprow"] = ph2.enter_context(
                nc.sbuf_tensor("k_bexprow", [1, 8 * D], BF16))
            bxm = {}

            def gbx(e):
                wait(e, x1_mark)
                grp = b.gqq.begin(e)
                bxm["m"] = grp.tag(e.dma_start(out=brow["bexprow"][:],
                                               in_=dr["bexprow"]), 16)
            b.g(gbx)
            wse = [WStream(b2, pool_f, dr["we"][e * D:(e + 1) * D, :], D, D, 512)
                   for e in range(8)]

            gate_mark = emit_gates_top2(b2, K, x1_sb, wg_sb, brow["bgrow"], gsb,
                                        None, [x1_mark, x1m["m"]], pr2)

            store_mark = x1_mark
            mean2 = lsc2[:, D:D + 1]
            s2c2 = lsc2[:, D + 1:D + 2]
            rstd2 = lsc2[:, D + 2:D + 3]
            for tg in range(NTG):
                mks = {}

                def yload(e, tg=tg, prev=store_mark, gm=gate_mark):
                    wait(e, prev, gm)
                    grp = b.yqq.begin(e)
                    for c in range(DC):
                        mks["xl"] = grp.tag(e.dma_start(
                            out=xtg[:, c * TGS:(c + 1) * TGS],
                            in_=xin[c * 128:(c + 1) * 128,
                                        tg * TGS:(tg + 1) * TGS]), 16)
                b2.y(yload)

                def gba2(e, tg=tg, prev=store_mark):
                    wait(e, prev, ba_done)
                    grp = b.gqq.begin(e)
                    for c in range(DC):
                        mks["bal"] = grp.tag(e.dma_start(
                            out=bax[:, c * TGS:(c + 1) * TGS],
                            in_=ba_d[:, c * T + tg * TGS: c * T + (tg + 1) * TGS]), 16)
                b2.g(gba2)

                def vres0(e, tg=tg):
                    wait(e, mks["xl"], mks["bal"], x1m["m"])
                    e.drain()
                    for c in range(DC):
                        e.tensor_tensor(xtg[:, c * TGS:(c + 1) * TGS],
                                        xtg[:, c * TGS:(c + 1) * TGS],
                                        bax[:, c * TGS:(c + 1) * TGS], ALU.add)
                        e.drain()
                        ins = e.tensor_tensor(
                            xtg[:, c * TGS:(c + 1) * TGS],
                            xtg[:, c * TGS:(c + 1) * TGS],
                            x1_sb[:, c * T + tg * TGS: c * T + (tg + 1) * TGS],
                            ALU.add)
                        e.drain()
                    mks["res0"] = b.vec.tag(ins)
                b2.v(vres0)

                def vclr(e):
                    wait(e, mks["res0"])
                    e.drain()
                    mks["clr"] = b.vec.tag(e.memset(acc[:], 0.0))
                b2.v(vclr)

                for ee in range(8):
                    def evac_e(tc, fb, ps, waits, ee=ee, tg=tg):
                        out = {}

                        def vev(e):
                            wait(e, *waits, mks["clr"])
                            e.drain()
                            out["m"] = b.vec.tag(e.scalar_tensor_tensor(
                                acc[:, tc * 1024 + fb * 512: tc * 1024 + (fb + 1) * 512],
                                ps[:, 0:512],
                                gsb[:, (tg * 4 + tc) * 8 + ee:
                                    (tg * 4 + tc) * 8 + ee + 1],
                                acc[:, tc * 1024 + fb * 512: tc * 1024 + (fb + 1) * 512],
                                ALU.mult, ALU.add))
                        b2.v(vev)
                        return out["m"]
                    omarks = tm_proj(b2, wse[ee], DC,
                                     lambda kc, tc: x1_sb[:, kc * T + (tg * 4 + tc) * 128:
                                                          kc * T + (tg * 4 + tc + 1) * 128],
                                     evac_e, pr2, 4,
                                     extra_waits=[bxm["m"]],
                                     bias_row=lambda fb, ee=ee:
                                         brow["bexprow"][:, ee * D + fb * 512:
                                                         ee * D + (fb + 1) * 512],
                                     ones_row=K.ones_row)
                    mks["moe"] = omarks[-1]

                res_marks = []
                for tc in range(4):
                    def vl1(e, tc=tc, w=mks["moe"]):
                        wait(e, w)
                        e.drain()
                        e.reduce_sum(mean2, acc[:, tc * 1024:(tc + 1) * 1024], AX.X)
                        e.drain()
                        mks["mean2"] = b.vec.tag(e.tensor_scalar_mul(mean2, mean2,
                                                                     1.0 / D))
                    b2.v(vl1)

                    def sl1(e, tc=tc):
                        wait(e, mks["mean2"])
                        mks["sq2"] = b.sca.tag(e.activation(
                            lsc2[:, 0:D], acc[:, tc * 1024:(tc + 1) * 1024],
                            AF.Square, accum_out=s2c2))
                    b2.s(sl1)

                    def vl2(e):
                        wait(e, mks["sq2"])
                        e.drain()
                        e.tensor_scalar_mul(s2c2, s2c2, 1.0 / D)
                        e.drain()
                        e.tensor_tensor(lsc2[:, D + 3:D + 4], mean2, mean2, ALU.mult)
                        e.drain()
                        mks["var2"] = b.vec.tag(e.tensor_tensor(
                            s2c2, s2c2, lsc2[:, D + 3:D + 4], ALU.subtract))
                    b2.v(vl2)

                    def sl2(e):
                        wait(e, mks["var2"])
                        e.activation(rstd2, s2c2, AF.Ln, bias=EPS_LN)
                        e.drain()
                        mks["rstd2"] = b.sca.tag(e.activation(rstd2, rstd2, AF.Exp,
                                                              scale=-0.5))
                    b2.s(sl2)

                    def vn2(e, tc=tc):
                        wait(e, mks["rstd2"])
                        e.drain()
                        mks["nrm2"] = b.vec.tag(e.tensor_scalar(
                            lsc2[:, 0:D], acc[:, tc * 1024:(tc + 1) * 1024],
                            mean2, rstd2, ALU.subtract, ALU.mult))
                    b2.v(vn2)

                    for mc in range(DC):
                        pidx, psx, ps_free = pr2.get()

                        def ttp(e, mc=mc, psx=psx, ps_free=ps_free):
                            wait(e, mks["nrm2"], ps_free)
                            mks["tp2"] = b.pe.tag(e.matmul(
                                psx[:, 0:128], lsc2[:, mc * 128:(mc + 1) * 128],
                                K.ident_f[:], start=True, stop=True))
                        b2.t(ttp)

                        def vfin(e, tc=tc, mc=mc, psx=psx, pidx=pidx):
                            wait(e, mks["tp2"], mks["res0"])
                            e.drain()
                            e.scalar_tensor_tensor(
                                xtg[:, mc * TGS + tc * 128: mc * TGS + (tc + 1) * 128],
                                psx[:, 0:128], colt["g2c"][:, mc:mc + 1],
                                xtg[:, mc * TGS + tc * 128: mc * TGS + (tc + 1) * 128],
                                ALU.mult, ALU.add)
                            e.drain()
                            ins = e.tensor_scalar_add(
                                xtg[:, mc * TGS + tc * 128: mc * TGS + (tc + 1) * 128],
                                xtg[:, mc * TGS + tc * 128: mc * TGS + (tc + 1) * 128],
                                colt["be2c"][:, mc:mc + 1])
                            e.drain()
                            m = b.vec.tag(ins)
                            pr2.set_free(pidx, m)
                            vfin.mark = m
                        b2.v(vfin)
                        res_marks.append(vfin.mark)

                def ystore(e, tg=tg, rm=list(res_marks)):
                    wait(e, rm[-1])
                    grp = b.yqq.begin(e)
                    for c in range(DC):
                        ystore.mark = grp.tag(e.dma_start(
                            out=xout[c * 128:(c + 1) * 128, tg * TGS:(tg + 1) * TGS],
                            in_=xtg[:, c * TGS:(c + 1) * TGS]), 16)
                b2.y(ystore)
                store_mark = ystore.mark

    return store_mark


def build_block0(upto="full"):
    nc = bass.Bass("TRN2", target_bir_lowering=False)
    dr = {}
    xin = nc.dram_tensor("x", [D, T], F32, kind="ExternalInput").ap()
    for nm, shape in B0_SHAPES:
        dr[nm] = nc.dram_tensor(nm, shape, F32, kind="ExternalInput").ap()
    cs = make_consts()
    for nm in CONST_NAMES:
        dr[nm] = nc.dram_tensor(nm, list(cs[nm].shape), F32, kind="ExternalInput").ap()
    xout = nc.dram_tensor("xout", [D, T], F32, kind="ExternalOutput").ap()
    with ExitStack() as ctx:
        blk = ctx.enter_context(nc.Block())
        b = B(nc, ctx, blk)
        K = ConstTiles(b, dr)
        with ExitStack() as bctx:
            fin = emit_block0(b.sub(bctx), bctx, dr, K, xin, xout, upto=upto)
        if isinstance(fin, tuple):
            b.y(lambda e: wait(e, fin))
    return nc


def prep_block0_inputs(inputs):
    m = dict(make_consts())
    m["inproj"] = np.ascontiguousarray(inputs["L0_attn_in_proj_w"])
    cw = np.asarray(inputs["L0_attn_conv_w"]).reshape(DI, 4)
    m["convw"] = np.ascontiguousarray(
        cw.reshape(DIC, 128, 4).transpose(1, 0, 2).reshape(128, DIC * 4))
    m["convbc"] = _cols(inputs["L0_attn_conv_b"])
    m["xproj"] = np.ascontiguousarray(inputs["L0_attn_x_proj_w"])
    m["dtproj"] = np.ascontiguousarray(inputs["L0_attn_dt_proj_w"])
    m["dtbc"] = _cols(inputs["L0_attn_dt_proj_b"])
    al = np.asarray(inputs["L0_attn_A_log"])
    m["alogc"] = np.ascontiguousarray(
        al.reshape(DIC, 128, NST).transpose(1, 0, 2).reshape(128, DIC * NST))
    m["dcolc"] = _cols(inputs["L0_attn_D"])
    m["outproj"] = np.ascontiguousarray(inputs["L0_attn_out_proj_w"])
    m["fgw"] = np.ascontiguousarray(inputs["L0_attn_fg_w"])
    m["fgbc"] = _cols(inputs["L0_attn_fg_b"])
    m["wq"] = np.ascontiguousarray(inputs["L0_ffn_Wq"])
    m["wk"] = np.ascontiguousarray(inputs["L0_ffn_Wk"])
    m["wv"] = np.ascontiguousarray(inputs["L0_ffn_Wv"])
    m["wo"] = np.ascontiguousarray(inputs["L0_ffn_Wo"])
    m["bqc"] = _cols(inputs["L0_ffn_bq"])
    m["bkc"] = _cols(inputs["L0_ffn_bk"])
    m["bvrow"] = np.ascontiguousarray(np.asarray(inputs["L0_ffn_bv"]).reshape(1, D))
    m["borow"] = np.ascontiguousarray(np.asarray(inputs["L0_ffn_bo"]).reshape(1, D))
    m["g1c"] = _cols(inputs["L0_ffn_g1"])
    m["be1c"] = _cols(inputs["L0_ffn_be1"])
    m["g2c"] = _cols(inputs["L0_ffn_g2"])
    m["be2c"] = _cols(inputs["L0_ffn_be2"])
    m["wg"] = np.ascontiguousarray(inputs["L0_ffn_Wg"])
    m["bgrow"] = np.ascontiguousarray(np.asarray(inputs["L0_ffn_bg"]).reshape(1, 8))
    m["we"] = np.ascontiguousarray(np.asarray(inputs["L0_ffn_We"]).reshape(8 * D, D))
    m["bexprow"] = np.ascontiguousarray(
        np.asarray(inputs["L0_ffn_bexp"]).reshape(1, 8 * D))
    s = np.zeros((16, 16 * 128), dtype=np.float32)
    for n in range(16):
        s[n, n * 128:(n + 1) * 128] = 1.0
    m["sel16"] = s
    return m


def run_block0(x_fm_percore, inputs, upto="full"):
    nc = build_block0(upto=upto)
    base = prep_block0_inputs(inputs)
    in_maps = []
    for core in range(8):
        m = dict(base)
        m["x"] = np.ascontiguousarray(x_fm_percore[core], dtype=np.float32)
        in_maps.append(m)
    res = run_bass_kernel_spmd(nc, in_maps, core_ids=list(range(8)))
    return [res.results[c]["xout"] for c in range(8)]




def build_fused():
    nc = bass.Bass("TRN2", target_bir_lowering=False)
    cs = make_consts()
    drc = {}
    for nm in CONST_NAMES:
        drc[nm] = nc.dram_tensor(nm, list(cs[nm].shape), F32,
                                 kind="ExternalInput").ap()
    xin = nc.dram_tensor("x", [D, T], F32, kind="ExternalInput").ap()
    xout = nc.dram_tensor("xout", [D, T], F32, kind="ExternalOutput").ap()
    xc = [nc.dram_tensor(f"xc{i}", [D, T], F32, kind="Internal").ap()
          for i in range(3)]
    drs = []
    for i, shapes in enumerate([B0_SHAPES, B1_SHAPES, B2_SHAPES, B3_SHAPES]):
        d = dict(drc)
        for nm, shape in shapes:
            d[nm] = nc.dram_tensor(f"b{i}_{nm}", shape, F32,
                                   kind="ExternalInput").ap()
        drs.append(d)

    with ExitStack() as ctx:
        blk = ctx.enter_context(nc.Block())
        b = B(nc, ctx, blk)
        K = ConstTiles(b, drc)
        emits = [emit_block0, emit_block1, emit_block2, emit_block3]
        xi_list = [xin, xc[0], xc[1], xc[2]]
        xo_list = [xc[0], xc[1], xc[2], xout]
        fin = None
        for i in range(4):
            ncp = NCP(nc, f"b{i}_")
            with ExitStack() as bctx:
                fin = emits[i](b.sub(bctx, ncp), bctx, drs[i], K,
                               xi_list[i], xo_list[i])
            block_barrier(b, fin)
        b.y(lambda e: wait(e, fin))
    return nc


def prep_fused_inputs(inputs):
    m = dict(make_consts())
    for i, prep in enumerate([prep_block0_inputs, prep_block1_inputs,
                              prep_block2_inputs, prep_block3_inputs]):
        d = prep(inputs)
        for k, v in d.items():
            if not k.startswith("c_"):
                m[f"b{i}_{k}"] = v
    return m


_FUSED_CACHE = {}


def kernel_fused(**inputs):
    if "nc" not in _FUSED_CACHE:
        _FUSED_CACHE["nc"] = build_fused()
    nc = _FUSED_CACHE["nc"]
    base = prep_fused_inputs(inputs)
    x = np.asarray(inputs["L0_attn_x"])
    in_maps = []
    for core in range(8):
        m = dict(base)
        m["x"] = np.ascontiguousarray(x[core].T, dtype=np.float32)
        in_maps.append(m)
    res = run_bass_kernel_spmd(nc, in_maps, core_ids=list(range(8)))
    return np.stack([np.ascontiguousarray(res.results[c]["xout"].T)
                     for c in range(8)])


def kernel(**inputs):
    return kernel_fused(**inputs)



# version 08 (final)
# speedup vs baseline: 1.3688x; 1.0489x over previous
import numpy as np
from contextlib import ExitStack

import concourse.bass as bass
import concourse.mybir as mybir
from concourse.bass_utils import run_bass_kernel_spmd

F32 = mybir.dt.float32
BF16 = mybir.dt.bfloat16
AF = mybir.ActivationFunctionType
ALU = mybir.AluOpType
AX = mybir.AxisListType

D = 1024
T = 2048
DC = D // 128
TGS = 512
NTG = T // TGS
EPS_LN = 1e-5
ATT_HEADS = list(range(16))


def cdiv(a, b):
    return -(-a // b)


class Sem:
    def __init__(self, handle):
        self.h = handle
        self.n = 0

    def tag(self, ins, k=1):
        self.n += k
        ins.then_inc(self.h, k)
        return (self.h, self.n)

    def mark(self):
        return (self.h, self.n)


def wait(e, *marks):
    for m in marks:
        if m is None:
            continue
        if isinstance(m, (list, tuple)) and m and not isinstance(m[0], tuple) \
                and not hasattr(m[0], "num"):
            e.wait_ge(m[0], m[1])
        elif isinstance(m, list):
            wait(e, *m)
        else:
            e.wait_ge(m[0], m[1])


class DmaQ:

    def __init__(self, b, name, R=4):
        self.sems = [b.slot_sem(name) for _ in range(R)]
        self.i = 0

    def begin(self, e):
        s = self.sems[self.i % len(self.sems)]
        self.i += 1
        if s.n > 0:
            e.wait_ge(s.h, s.n)
        return s


class B:

    def __init__(self, nc, ctx, blk):
        self.nc = nc
        self.ctx = ctx
        self.blk = blk
        self.pe = Sem(ctx.enter_context(nc.semaphore("s_pe")))
        self.vec = Sem(ctx.enter_context(nc.semaphore("s_vec")))
        self.sca = Sem(ctx.enter_context(nc.semaphore("s_sca")))
        self.pool = Sem(ctx.enter_context(nc.semaphore("s_pool")))
        self._slot_ctr = [0]
        self._s_q = None
        self.yqq = DmaQ(self, "s_yq", 4)
        self.gqq = DmaQ(self, "s_gq", 4)

    def slot_sem(self, name):
        self._slot_ctr[0] += 1
        return Sem(self.ctx.enter_context(
            self.nc.semaphore(f"{name}{self._slot_ctr[0]}")))

    def sub(self, ph, nc=None):
        nb = B.__new__(B)
        nb.__dict__.update(self.__dict__)
        nb.ctx = ph
        nb._s_q = None
        if nc is not None:
            nb.nc = nc
        return nb

    def t(self, f):
        self.blk.tensor(f)

    def v(self, f):
        self.blk.vector(f)

    def s(self, f):
        if self._s_q is not None:
            self._s_q.append(f)
        else:
            self.blk.scalar(f)

    def s_defer(self):
        assert self._s_q is None
        self._s_q = []

    def s_flush(self):
        q, self._s_q = self._s_q, None
        for f in q:
            self.blk.scalar(f)

    def sca_reserve(self, k=1):
        self.sca.n += k
        return (self.sca.h, self.sca.n)

    def g(self, f):
        self.blk.gpsimd(f)

    def y(self, f):
        self.blk.sync(f)


class NCP:

    def __init__(self, nc, prefix):
        self._nc = nc
        self._p = prefix

    def sbuf_tensor(self, name, *a, **k):
        return self._nc.sbuf_tensor(self._p + name, *a, **k)

    def psum_tensor(self, name, *a, **k):
        return self._nc.psum_tensor(self._p + name, *a, **k)

    def dram_tensor(self, name, *a, **k):
        return self._nc.dram_tensor(self._p + name, *a, **k)

    def semaphore(self, name=None, num=None):
        return self._nc.semaphore((self._p + name) if name else name, num)

    def __getattr__(self, a):
        return getattr(self._nc, a)


def block_barrier(b, mark):
    for emit in (b.t, b.v, b.s, b.g, b.y):
        emit(lambda e: wait(e, mark))


def _t(a):
    return np.ascontiguousarray(np.asarray(a).T)


def _cols(v):
    return np.ascontiguousarray(np.asarray(v).reshape(-1, 128).T)


CONST_SCALARS = [EPS_LN, 1.0]


def make_consts():
    c = {}
    c["c_ones_sq"] = np.ones((128, 128), dtype=np.float32)
    c["c_ones_col"] = np.ones((128, 1), dtype=np.float32)
    c["c_ones_row"] = np.ones((1, 128), dtype=np.float32)
    c["c_negones_row"] = -np.ones((1, 128), dtype=np.float32)
    c["c_scal"] = np.tile(np.asarray(CONST_SCALARS, np.float32), (128, 1))
    c["c_ident"] = np.eye(128, dtype=np.float32)
    i = np.arange(128)[:, None]
    j = np.arange(TGS)[None, :]
    for r in range(4):
        c[f"c_mask{r}"] = ((i + 128 * r) <= j).astype(np.float32)
    return c


CONST_NAMES = list(make_consts().keys())


class ConstTiles:
    def __init__(self, b, dr):
        nc = b.nc
        sb = lambda n, s, dt: b.ctx.enter_context(nc.sbuf_tensor(n, s, dt))
        self.ones_sq = sb("k_ones_sq", [128, 128], F32)
        self.ones_col = sb("k_ones_col", [128, 1], BF16)
        self.ones_row = sb("k_ones_row", [1, 128], BF16)
        self.negones_row = sb("k_negones_row", [1, 128], BF16)
        self.ones_row_f = sb("k_ones_row_f", [1, 128], F32)
        self.masks = [sb(f"k_mask{r}", [128, TGS], BF16) for r in range(4)]
        self.scal = sb("k_scal", [128, len(CONST_SCALARS)], F32)
        self.ident_f = sb("k_ident_f", [128, 128], F32)
        self.ident_b = sb("k_ident_b", [128, 128], BF16)
        for i, v in enumerate(CONST_SCALARS):
            nc.const_aps.register(F32, float(v), self.scal[:, i:i + 1])

        def g(e):
            grp = b.gqq.begin(e)
            for out, src in [(self.ones_sq, "c_ones_sq"), (self.ones_col, "c_ones_col"),
                             (self.ones_row, "c_ones_row"),
                             (self.negones_row, "c_negones_row"), (self.scal, "c_scal")]:
                self.ready = grp.tag(e.dma_start(out=out[:], in_=dr[src]), 16)
            self.ready = grp.tag(
                e.dma_start(out=self.ones_row_f[:], in_=dr["c_ones_row"]), 16)
            self.ready = grp.tag(
                e.dma_start(out=self.ident_f[:], in_=dr["c_ident"]), 16)
            self.ready = grp.tag(
                e.dma_start(out=self.ident_b[:], in_=dr["c_ident"]), 16)
            for r in range(4):
                self.ready = grp.tag(
                    e.dma_start(out=self.masks[r][:], in_=dr[f"c_mask{r}"]), 16)
        b.g(g)

    def barrier(self, b, extra=()):
        mk = self.ready
        ex = list(extra)
        for emit in (b.t, b.v, b.s):
            emit(lambda e: wait(e, mk, *ex))


def load_cols(b, name, dr, ncols):
    sb = b.ctx.enter_context(b.nc.sbuf_tensor("k_" + name, [128, ncols], F32))
    mk = []

    def g(e):
        grp = b.gqq.begin(e)
        mk.append(grp.tag(e.dma_start(out=sb[:], in_=dr[name]), 16))
    b.g(g)
    return sb, mk[0]


class WPool:
    def __init__(self, b, name, cols_per_buf, nbuf=2, init_free=None):
        self.nbuf = nbuf
        self.bufs = [b.ctx.enter_context(
            b.nc.sbuf_tensor(f"{name}_w{i}", [128, cols_per_buf], BF16))
            for i in range(nbuf)]
        self.s_ld = [b.slot_sem(name + "_ld") for _ in range(nbuf)]
        self.freed = [init_free] * nbuf
        self.seq = 0


class WStream:
    def __init__(self, b, pool, dram, K, M, ncols):
        self.b = b
        self.pool = pool
        self.dram_r = dram.rearrange("(kc p) m -> p kc m", p=128)
        self.K = K
        self.KC = K // 128
        self.M = M
        self.ncols = ncols
        assert self.KC * ncols <= pool.bufs[0].shape[1]
        self.nblocks = cdiv(M, ncols)
        self.loaded = {}
        self.slot = {}

    def load(self, blkid):
        pool = self.pool
        slot = pool.seq % pool.nbuf
        pool.seq += 1
        self.slot[blkid] = slot
        buf, s_ld = pool.bufs[slot], pool.s_ld[slot]
        c0 = blkid * self.ncols
        c1 = min(self.M, c0 + self.ncols)
        KC, ncols = self.KC, self.ncols
        prior = pool.freed[slot]
        dram_r = self.dram_r

        def g(e):
            wait(e, prior)
            kgrp = min(8, KC)
            for k0 in range(0, KC, kgrp):
                k1 = min(KC, k0 + kgrp)
                self.loaded[blkid] = s_ld.tag(e.dma_start(
                    out=buf[:, k0 * ncols:k0 * ncols + (k1 - k0) * ncols].rearrange(
                        "p (kc m) -> p kc m", kc=k1 - k0),
                    in_=dram_r[:, k0:k1, c0:c1]), 16)
        self.b.g(g)

    def tile(self, blkid, kc, ml):
        buf = self.pool.bufs[self.slot[blkid]]
        base = kc * self.ncols + ml * 128
        return buf[:, base:base + 128]

    def rhs(self, blkid, kc, ncols=None):
        buf = self.pool.bufs[self.slot[blkid]]
        nc_ = ncols if ncols is not None else self.ncols
        return buf[:, kc * self.ncols: kc * self.ncols + nc_]

    def set_free(self, blkid, marker):
        self.pool.freed[self.slot[blkid]] = marker


class PsumRot:
    def __init__(self, b, name, n, shape=(128, TGS), init_free=None):
        self.tiles = [b.ctx.enter_context(b.nc.psum_tensor(f"{name}{i}", list(shape), F32))
                      for i in range(n)]
        self.free = [init_free] * n
        self.i = 0

    def get(self):
        idx = self.i % len(self.tiles)
        self.i += 1
        return idx, self.tiles[idx], self.free[idx]

    def set_free(self, idx, marker):
        self.free[idx] = marker


def fm_proj(b, ws, KC, rhs_ids, rhs_fn, evac, pr, mc_total, extra_waits=()):
    mpb = ws.ncols // 128
    nblk = cdiv(mc_total, mpb)
    first = True
    done = {}
    for blkid in range(nblk):
        ws.load(blkid)
        for ml in range(mpb):
            mc = blkid * mpb + ml
            if mc >= mc_total:
                break
            for rid in rhs_ids:
                pidx, ps, ps_free = pr.get()
                waits = [ws.loaded[blkid], ps_free]
                if first:
                    waits += list(extra_waits)
                    first = False

                def tmm(e, waits=waits, ps=ps, blkid=blkid, ml=ml, rid=rid):
                    wait(e, *waits)
                    for kc in range(KC):
                        ins = e.matmul(ps[:], ws.tile(blkid, kc, ml), rhs_fn(kc, rid),
                                       start=(kc == 0), stop=(kc == KC - 1))
                    self_mark = b.pe.tag(ins)
                    tmm.mark = self_mark
                b.t(tmm)
                ws.set_free(blkid, tmm.mark)
                mk = evac(mc, rid, ps, [tmm.mark])
                pr.set_free(pidx, mk)
                done[mk[0]] = mk
    return list(done.values())


def tm_proj(b, ws, KC, lhs_fn, evac, pr, tc_total=16, extra_waits=(),
            bias_row=None, ones_row=None):
    first = True
    done = {}
    for fb in range(ws.nblocks):
        ws.load(fb)
        ncols = min(ws.ncols, ws.M - fb * ws.ncols)
        for tc in range(tc_total):
            pidx, ps, ps_free = pr.get()
            waits = [ws.loaded[fb], ps_free]
            if first:
                waits += list(extra_waits)
                first = False

            def tmm(e, waits=waits, ps=ps, fb=fb, tc=tc, ncols=ncols):
                wait(e, *waits)
                for kc in range(KC):
                    ins = e.matmul(ps[:, 0:ncols], lhs_fn(kc, tc),
                                   ws.rhs(fb, kc, ncols), start=(kc == 0),
                                   stop=(kc == KC - 1 and bias_row is None))
                if bias_row is not None:
                    ins = e.matmul(ps[:, 0:ncols], ones_row[:],
                                   bias_row(fb), start=False, stop=True)
                tmm.mark = b.pe.tag(ins)
            b.t(tmm)
            ws.set_free(fb, tmm.mark)
            mk = evac(tc, fb, ps, [tmm.mark])
            pr.set_free(pidx, mk)
            done[mk[0]] = mk
    return list(done.values())


def emit_ln(b, x_dram, xn_sb, K):
    nc = b.nc
    with ExitStack() as ph:
        xt = ph.enter_context(nc.sbuf_tensor("ln_xt", [128, DC * TGS], F32))
        sq = ph.enter_context(nc.sbuf_tensor("ln_sq", [128, DC * TGS], F32))
        stat = ph.enter_context(nc.sbuf_tensor("ln_stat", [128, 4 * TGS], F32))
        mean, m2 = stat[:, 0:TGS], stat[:, TGS:2 * TGS]
        rstd, var = stat[:, 2 * TGS:3 * TGS], stat[:, 3 * TGS:4 * TGS]
        p_s1 = ph.enter_context(nc.psum_tensor("ln_ps1", [128, TGS], F32))
        p_s2 = ph.enter_context(nc.psum_tensor("ln_ps2", [128, TGS], F32))
        xn_mark = None

        for tg in range(NTG):
            prev_done = xn_mark
            mks = {}

            def yload(e, prev_done=prev_done, tg=tg):
                wait(e, prev_done)
                grp = b.yqq.begin(e)
                for c in range(DC):
                    mks["x"] = grp.tag(e.dma_start(
                        out=xt[:, c * TGS:(c + 1) * TGS],
                        in_=x_dram[c * 128:(c + 1) * 128, tg * TGS:(tg + 1) * TGS]), 16)
            b.y(yload)

            def sact(e, prev_done=prev_done):
                wait(e, mks["x"], prev_done)
                for c in range(DC):
                    sl = slice(c * TGS, (c + 1) * TGS)
                    ins = e.activation(sq[:, sl], xt[:, sl], AF.Square)
                mks["sq"] = b.sca.tag(ins)
            b.s(sact)

            def tmm(e, prev_done=prev_done):
                wait(e, mks["x"], prev_done)
                for c in range(DC):
                    e.matmul(p_s1[:], K.ones_sq[:], xt[:, c * TGS:(c + 1) * TGS],
                             start=(c == 0), stop=(c == DC - 1))
                wait(e, mks["sq"])
                for c in range(DC):
                    ins = e.matmul(p_s2[:], K.ones_sq[:], sq[:, c * TGS:(c + 1) * TGS],
                                   start=(c == 0), stop=(c == DC - 1))
                mks["mm"] = b.pe.tag(ins)
            b.t(tmm)

            def vstat(e):
                wait(e, mks["mm"])
                e.drain()
                e.tensor_scalar_mul(mean, p_s1[:], 1.0 / D)
                e.drain()
                e.tensor_tensor(m2, mean, mean, ALU.mult)
                e.drain()
                mks["var"] = b.vec.tag(e.scalar_tensor_tensor(
                    var, p_s2[:], 1.0 / D, m2, ALU.mult, ALU.subtract))
            b.v(vstat)

            def sact2(e):
                wait(e, mks["var"])
                e.activation(rstd, var, AF.Ln, bias=EPS_LN)
                e.drain()
                mks["rstd"] = b.sca.tag(e.activation(rstd, rstd, AF.Exp, scale=-0.5))
            b.s(sact2)

            def vsub(e):
                wait(e, mks["rstd"])
                e.drain()
                for c in range(DC):
                    sl = slice(c * TGS, (c + 1) * TGS)
                    ins = e.scalar_tensor_tensor(sq[:, sl], mean, -1.0, xt[:, sl],
                                                 ALU.mult, ALU.add)
                mks["sub"] = b.vec.tag(ins)
            b.v(vsub)

            def pmul(e, tg=tg):
                wait(e, mks["sub"])
                e.drain()
                for c in range(DC):
                    ins = e.tensor_tensor(
                        xn_sb[:, c * T + tg * TGS: c * T + (tg + 1) * TGS],
                        sq[:, c * TGS:(c + 1) * TGS], rstd, ALU.mult)
                mks["xn"] = b.pool.tag(ins)
            b.g(pmul)
            xn_mark = mks["xn"]
    return xn_mark


def emit_attention_1h(b, K, q_sb, k_sb, v_sb, ctx_sb, scale, causal,
                      ready_waits, name="att", upto="full"):
    nc = b.nc
    with ExitStack() as ph:
        sqs = ph.enter_context(nc.sbuf_tensor(name + "_sqs", [128, T], BF16))
        nrm = ph.enter_context(nc.sbuf_tensor(name + "_nrm", [1, 16], F32))
        mb = ph.enter_context(nc.sbuf_tensor(name + "_mb", [128, 1], F32))
        mbb = ph.enter_context(nc.sbuf_tensor(name + "_mbb", [1, 1], BF16))
        pt = ph.enter_context(nc.sbuf_tensor(name + "_pt", [128, 16 * TGS], BF16))
        rb = ph.enter_context(nc.sbuf_tensor(name + "_rb", [128, TGS], F32))
        p_st = [ph.enter_context(nc.psum_tensor(f"{name}_pst{i}", [128, TGS], F32))
                for i in range(2)]
        p_ctx = [ph.enter_context(nc.psum_tensor(f"{name}_pc{i}", [128, TGS], F32))
                 for i in range(4)]
        p_ns = ph.enter_context(nc.psum_tensor(name + "_pns", [1, TGS], F32))
        p_misc = ph.enter_context(nc.psum_tensor(name + "_pmisc", [128, TGS], F32))
        p_n = p_ns
        p_sum = p_ns
        p_rb = p_misc
        p_mb = p_misc[:, 0:1]
        mks = {}

        col = 0
        for src_i, src in enumerate([q_sb, k_sb]):
            for tgq in range(NTG):
                for c in range(DC):
                    def sact(e, c=c, tgq=tgq, src=src, first=(col == 0 and c == 0)):
                        if first and src_i == 0:
                            wait(e, *ready_waits)
                        if "nq_mm" in mks:
                            wait(e, mks["nq_mm"])
                        tag_sl = src[:, c * T + tgq * TGS: c * T + (tgq + 1) * TGS]
                        mks["sq"] = b.sca.tag(e.activation(
                            sqs[:, tgq * TGS:(tgq + 1) * TGS], tag_sl, AF.Square))
                    b.s(sact)

                    def tmm(e, c=c, tgq=tgq):
                        wait(e, mks["sq"], mks.get("nred"))
                        mks["nq_mm"] = b.pe.tag(e.matmul(
                            p_n[:], K.ones_col[:], sqs[:, tgq * TGS:(tgq + 1) * TGS],
                            start=(c == 0), stop=(c == DC - 1)))
                    b.t(tmm)

                def vred(e, col=col):
                    wait(e, mks["nq_mm"])
                    e.drain()
                    mks["nred"] = b.vec.tag(e.reduce_max(
                        nrm[:, col:col + 1], p_n[:], AX.X))
                b.v(vred)
                col += 1

        def vcomb(e):
            wait(e, mks["nred"])
            e.drain()
            e.reduce_max(nrm[:, 8:9], nrm[:, 0:8], AX.X)
            e.drain()
            e.tensor_scalar_mul(nrm[:, 9:10], nrm[:, 8:9], scale)
            e.drain()
            mks["mbb"] = b.vec.tag(e.tensor_copy(mbb[:, 0:1], nrm[:, 9:10]))
        b.v(vcomb)

        def tmb(e):
            wait(e, mks["mbb"])
            mks["mb_mm"] = b.pe.tag(e.matmul(p_mb[:], K.negones_row[:], mbb[:, 0:1],
                                             start=True, stop=True))
        b.t(tmb)

        def vmb(e):
            wait(e, mks["mb_mm"])
            e.drain()
            mks["mb"] = b.vec.tag(e.tensor_copy(mb[:, 0:1], p_mb[:]))
        b.v(vmb)

        if upto == "bound":
            def vdbg(e):
                wait(e, mks["mb"])
                e.drain()
                mks["ctxev"] = b.vec.tag(e.tensor_copy(
                    ctx_sb[:, 0:1], mb[:, 0:1]))
            b.v(vdbg)
            return mks["ctxev"]

        ntgq = int(upto[2:]) if upto.startswith("tq") else NTG
        exp_marks = [None, None]
        b.s_defer()
        for tgq in range(ntgq):
            ntk = (tgq + 1) * (TGS // 128) if causal else T // 128
            pt_marks = []
            for tkc in range(ntk):
                pi = tkc % 2
                ps = p_st[pi]

                def tst(e, tkc=tkc, tgq=tgq, ps=ps, pi=pi, em=exp_marks[pi]):
                    wait(e, em, mks["mb"])
                    for kc in range(DC):
                        ins = e.matmul(
                            ps[:], k_sb[:, kc * T + tkc * 128: kc * T + (tkc + 1) * 128],
                            q_sb[:, kc * T + tgq * TGS: kc * T + (tgq + 1) * TGS],
                            start=(kc == 0), stop=(kc == DC - 1))
                    mks["st"] = b.pe.tag(ins)
                b.t(tst)

                em = b.sca_reserve()

                def sexp(e, tkc=tkc, ps=ps, w=(mks["st"],)):
                    wait(e, *w)
                    e.activation(
                        pt[:, tkc * TGS:(tkc + 1) * TGS], ps[:], AF.Exp,
                        bias=mb[:, 0:1], scale=scale).then_inc(b.sca.h, 1)
                b.s(sexp)
                mks["exp"] = em
                exp_marks[pi] = mks["exp"]
                pt_marks.append(mks["exp"])
                diag_r = tkc - tgq * (TGS // 128)
                if causal and 0 <= diag_r <= 3:
                    def vmask(e, tkc=tkc, diag_r=diag_r, em=mks["exp"]):
                        wait(e, em)
                        e.drain()
                        sl = slice(tkc * TGS, (tkc + 1) * TGS)
                        mks["mask"] = b.vec.tag(e.tensor_tensor(
                            pt[:, sl], pt[:, sl], K.masks[diag_r][:], ALU.mult))
                    b.v(vmask)
                    pt_marks.append(mks["mask"])

            pt_last = [m for m in (pt_marks[-1], mks.get("mask")) if m]
            for pas in range(2):
                def tav(e, tgq=tgq, pas=pas, ntk=ntk,
                        w=(pt_marks[-1], mks.get("mask"), mks.get("ctxev"))):
                    wait(e, *[m for m in w if m])
                    for tkc in range(ntk):
                        st, sp = (tkc == 0), (tkc == ntk - 1)
                        for m in range(4):
                            mc = pas * 4 + m
                            ins = e.matmul(
                                p_ctx[m][:],
                                v_sb[:, tkc * D + mc * 128: tkc * D + (mc + 1) * 128],
                                pt[:, tkc * TGS:(tkc + 1) * TGS], start=st, stop=sp)
                        if pas == 0:
                            ins = e.matmul(p_sum[:], K.ones_col[:],
                                           pt[:, tkc * TGS:(tkc + 1) * TGS],
                                           start=st, stop=sp)
                    mks["av"] = b.pe.tag(ins)
                b.t(tav)

                if pas == 0:
                    def vnorm(e):
                        wait(e, mks["av"])
                        e.drain()
                        mks["rec"] = b.vec.tag(e.reciprocal(rb[0:1, :], p_sum[:]))
                    b.v(vnorm)

                    def trb(e):
                        wait(e, mks["rec"])
                        mks["rb_mm"] = b.pe.tag(e.matmul(
                            p_rb[:], K.ones_row_f[:], rb[0:1, :], start=True, stop=True))
                    b.t(trb)

                    def vrb(e):
                        wait(e, mks["rb_mm"])
                        e.drain()
                        mks["rb"] = b.vec.tag(e.tensor_copy(rb[:, :], p_rb[:]))
                    b.v(vrb)

                    if upto == "norm1":
                        def vdbg(e):
                            wait(e, mks["rb"])
                            e.drain()
                            mks["ctxev"] = b.vec.tag(e.tensor_copy(
                                ctx_sb[:, 0:TGS], rb[:, :]))
                        b.v(vdbg)
                        b.s_flush()
                        return mks["ctxev"]

                if upto == "score1":
                    def vdbg(e):
                        wait(e, mks["av"])
                        e.drain()
                        mks["ctxev"] = b.vec.tag(e.tensor_copy(
                            ctx_sb[:, 0:TGS], pt[:, 0:TGS]))
                    b.v(vdbg)
                    b.s_flush()
                    return mks["ctxev"]

                def vev(e, tgq=tgq, pas=pas):
                    wait(e, mks["av"], mks["rb"])
                    e.drain()
                    for m in range(4):
                        mc = pas * 4 + m
                        ins = e.tensor_tensor(
                            ctx_sb[:, mc * T + tgq * TGS: mc * T + (tgq + 1) * TGS],
                            p_ctx[m][:], rb[:, :], ALU.mult)
                    mks["ctxev"] = b.vec.tag(ins)
                b.v(vev)
        b.s_flush()
    return mks["ctxev"]


VSTRIDE = 16 * 65 + 0


def emit_attention_mh(b, K, q_sb, k_sb, v_sb, ctx_sb, scale, causal,
                      ready_waits, name="mha"):
    nc = b.nc
    with ExitStack() as ph:
        sqs = ph.enter_context(nc.sbuf_tensor(name + "_sqs", [128, T], BF16))
        nrm = ph.enter_context(nc.sbuf_tensor(name + "_nrm", [1, 80], F32))
        mb = ph.enter_context(nc.sbuf_tensor(name + "_mb", [128, 1], F32))
        mbb = ph.enter_context(nc.sbuf_tensor(name + "_mbb", [1, 1], BF16))
        pt = ph.enter_context(nc.sbuf_tensor(name + "_pt", [128, 16 * TGS], BF16))
        rb = ph.enter_context(nc.sbuf_tensor(name + "_rb", [128, TGS], F32))
        NPST = 4
        p_st = [ph.enter_context(nc.psum_tensor(f"{name}_pst{i}", [128, TGS], F32))
                for i in range(NPST)]
        p_ctx = [ph.enter_context(nc.psum_tensor(f"{name}_pc{i}", [128, TGS], F32))
                 for i in range(2)]
        p_ns = ph.enter_context(nc.psum_tensor(name + "_pns", [1, TGS], F32))
        p_misc = ph.enter_context(nc.psum_tensor(name + "_pmisc", [128, TGS], F32))
        mks = {}

        col = 0
        for src_i, src in enumerate([q_sb, k_sb]):
            for c in range(DC):
                def sact(e, c=c, src=src, first=(col == 0)):
                    if first:
                        wait(e, *ready_waits)
                    if "bnd_mm" in mks:
                        wait(e, mks["bnd_mm"])
                    mks["sq"] = b.sca.tag(e.activation(sqs[:, :],
                                                       src[:, c * T:(c + 1) * T],
                                                       AF.Square))
                b.s(sact)
                for tgq in range(NTG):
                    def tmm(e, tgq=tgq):
                        wait(e, mks["sq"], mks.get("bnd_red"))
                        mks["bnd_mm"] = b.pe.tag(e.matmul(
                            p_ns[:], K.ones_col[:], sqs[:, tgq * TGS:(tgq + 1) * TGS],
                            start=True, stop=True))
                    b.t(tmm)

                    def vred(e, col=col):
                        wait(e, mks["bnd_mm"])
                        e.drain()
                        mks["bnd_red"] = b.vec.tag(e.reduce_max(
                            nrm[:, col:col + 1], p_ns[:], AX.X))
                    b.v(vred)
                    col += 1

        def vcomb(e, ncols=col):
            wait(e, mks["bnd_red"])
            e.drain()
            e.reduce_max(nrm[:, 70:71], nrm[:, 0:ncols], AX.X)
            e.drain()
            e.tensor_scalar_mul(nrm[:, 71:72], nrm[:, 70:71], scale)
            e.drain()
            mks["mbb"] = b.vec.tag(e.tensor_copy(mbb[:, 0:1], nrm[:, 71:72]))
        b.v(vcomb)

        def tmb(e):
            wait(e, mks["mbb"])
            mks["mb_mm"] = b.pe.tag(e.matmul(p_misc[:, 0:1], K.negones_row[:],
                                             mbb[:, 0:1], start=True, stop=True))
        b.t(tmb)

        def vmb(e):
            wait(e, mks["mb_mm"])
            e.drain()
            mks["mb"] = b.vec.tag(e.tensor_copy(mb[:, 0:1], p_misc[:, 0:1]))
        b.v(vmb)

        exp_marks = [None] * NPST
        ctx_ev = [None, None]
        b.s_defer()
        for tgq in range(NTG):
            ntk = (tgq + 1) * (TGS // 128) if causal else T // 128
            for h in ATT_HEADS:
                chunk, off = h // 2, (h % 2) * 64
                pt_marks = []
                for tkc in range(ntk):
                    pi = tkc % NPST
                    ps = p_st[pi]

                    def tst(e, tkc=tkc, tgq=tgq, ps=ps, pi=pi, chunk=chunk, off=off,
                            em=exp_marks[pi]):
                        wait(e, em, mks["mb"])
                        ins = e.matmul(
                            ps[:],
                            k_sb[off:off + 64,
                                 chunk * T + tkc * 128: chunk * T + (tkc + 1) * 128],
                            q_sb[off:off + 64,
                                 chunk * T + tgq * TGS: chunk * T + (tgq + 1) * TGS],
                            start=True, stop=True)
                        mks["st"] = b.pe.tag(ins)
                    b.t(tst)

                    em = b.sca_reserve()

                    def sexp(e, tkc=tkc, ps=ps, w=(mks["st"],)):
                        wait(e, *w)
                        e.activation(
                            pt[:, tkc * TGS:(tkc + 1) * TGS], ps[:], AF.Exp,
                            bias=mb[:, 0:1], scale=scale).then_inc(b.sca.h, 1)
                    b.s(sexp)
                    mks["exp"] = em
                    exp_marks[pi] = mks["exp"]
                    pt_marks.append(mks["exp"])
                    diag_r = tkc - tgq * (TGS // 128)
                    if causal and 0 <= diag_r <= 3:
                        def vmask(e, tkc=tkc, diag_r=diag_r, em=mks["exp"]):
                            wait(e, em)
                            e.drain()
                            sl = slice(tkc * TGS, (tkc + 1) * TGS)
                            mks["mask"] = b.vec.tag(e.tensor_tensor(
                                pt[:, sl], pt[:, sl], K.masks[diag_r][:], ALU.mult))
                        b.v(vmask)
                        pt_marks.append(mks["mask"])

                ci = h % 2
                pc = p_ctx[ci]

                def tav(e, tgq=tgq, ntk=ntk, off=off, pc=pc, ci=ci, h=h,
                        w=(pt_marks[-1], mks.get("mask"), ctx_ev[ci])):
                    wait(e, *[m for m in w if m])
                    for tkc in range(ntk):
                        st, sp = (tkc == 0), (tkc == ntk - 1)
                        ins = e.matmul(pc[0:65, :],
                                       v_sb[:, tkc * VSTRIDE + h * 65:
                                            tkc * VSTRIDE + h * 65 + 65],
                                       pt[:, tkc * TGS:(tkc + 1) * TGS],
                                       start=st, stop=sp)
                    mks["av"] = b.pe.tag(ins)
                b.t(tav)

                def vnorm(e, pc=pc):
                    wait(e, mks["av"])
                    e.drain()
                    mks["rec"] = b.vec.tag(e.tensor_copy(rb[0:1, :], pc[64:65, :]))
                b.v(vnorm)

                def trb(e):
                    wait(e, mks["rec"])
                    mks["rb_mm"] = b.pe.tag(e.matmul(
                        p_misc[:, :], K.ones_row_f[:], rb[0:1, :],
                        start=True, stop=True))
                b.t(trb)

                def vrb(e):
                    wait(e, mks["rb_mm"])
                    e.drain()
                    mks["rb"] = b.vec.tag(e.reciprocal(rb[:, :], p_misc[:, :]))
                b.v(vrb)

                def vev(e, tgq=tgq, chunk=chunk, off=off, pc=pc):
                    wait(e, mks["rb"])
                    e.drain()
                    ins = e.tensor_tensor(
                        ctx_sb[off:off + 64,
                               chunk * T + tgq * TGS: chunk * T + (tgq + 1) * TGS],
                        pc[0:64, :], rb[0:64, :], ALU.mult)
                    mks["ctxev"] = b.vec.tag(ins)
                b.v(vev)
                ctx_ev[ci] = mks["ctxev"]
        b.s_flush()
    return mks["ctxev"]


B3_SHAPES = [("wq", [D, D]), ("wk", [D, D]), ("wv", [D, D]), ("wo", [D, D]),
             ("w1", [D, 4096]), ("w2", [D, 4096]), ("h1", [4096, 4096]),
             ("h2", [4096, 4096]), ("w3", [4096, D]),
             ("s1c", [128, 32]), ("s2c", [128, 32]), ("sh1c", [128, 32]),
             ("sh2c", [128, 32]), ("s3c", [128, 8])]


def emit_block3(b0, ctx, dr, K, xin, xout, upto="full"):
    nc = b0.nc if not isinstance(b0.nc, NCP) else b0.nc
    b = b0
    SCALE = 1.0 / 32.0
    if True:
        xn = ctx.enter_context(nc.sbuf_tensor("xn", [128, DC * T], BF16))
        scl = {}
        scl_marks = []
        for nm, ncol in [("s1c", 32), ("s2c", 32), ("sh1c", 32), ("sh2c", 32),
                         ("s3c", 8)]:
            scl[nm], m = load_cols(b, nm, dr, ncol)
            scl_marks.append(m)
        K.barrier(b, extra=scl_marks)
        consts_rdy = []

        xn_mark = emit_ln(b, xin, xn, K)

        if upto == "ln":
            def gdbg(e):
                wait(e, xn_mark)
                grp = b.gqq.begin(e)
                for c in range(DC):
                    m = grp.tag(e.dma_start(
                        out=xout[c * 128:(c + 1) * 128, :],
                        in_=xn[:, c * T:(c + 1) * T]), 16)
                wait(e, m)
            b.g(gdbg)
            return nc

        ctx_sb = ctx.enter_context(nc.sbuf_tensor("ctx_sb", [128, DC * T], BF16))
        with ExitStack() as ph:
            q_sb = ph.enter_context(nc.sbuf_tensor("q_sb", [128, DC * T], BF16))
            k_sb = ph.enter_context(nc.sbuf_tensor("k_sb", [128, DC * T], BF16))
            v_sb = ph.enter_context(nc.sbuf_tensor("v_sb", [128, 16 * D], BF16))
            bq = b.sub(ph)
            qkv_marks = []
            with ExitStack() as phw:
                bw = b.sub(phw)
                pr = PsumRot(bw, "prA", 4)
                pool_a = WPool(bw, "pA", 8 * 1024, 2)

                for wnm, dst in [("wq", q_sb), ("wk", k_sb)]:
                    ws = WStream(bw, pool_a, dr[wnm], D, D, 1024)

                    def evac(mc, rid, ps, waits, dst=dst):
                        out = {}

                        def sev(e):
                            wait(e, *waits)
                            out["m"] = b.sca.tag(e.activation(
                                dst[:, mc * T + rid * TGS: mc * T + (rid + 1) * TGS],
                                ps[:], AF.Identity))
                        b.s(sev)
                        return out["m"]
                    qkv_marks += fm_proj(bw, ws, DC, list(range(NTG)),
                                         lambda kc, rid: xn[:, kc * T + rid * TGS:
                                                            kc * T + (rid + 1) * TGS],
                                         evac, pr, DC,
                                         extra_waits=[xn_mark] + consts_rdy)

                wsv = WStream(bw, pool_a, dr["wv"], D, D, 512)

                def vevac(tc, fb, ps, waits):
                    out = {}

                    def vev(e):
                        wait(e, *waits)
                        e.drain()
                        out["m"] = b.vec.tag(e.tensor_copy(
                            v_sb[:, tc * D + fb * 512: tc * D + (fb + 1) * 512],
                            ps[:, 0:512]))
                    b.v(vev)
                    return out["m"]
                qkv_marks += tm_proj(bw, wsv, DC,
                                     lambda kc, tc: xn[:, kc * T + tc * 128:
                                                       kc * T + (tc + 1) * 128],
                                     vevac, pr, 16, extra_waits=[xn_mark])

            if upto == "qkv":
                def gdbg(e):
                    wait(e, *qkv_marks)
                    grp = b.gqq.begin(e)
                    for c in range(DC):
                        m = grp.tag(e.dma_start(
                            out=xout[c * 128:(c + 1) * 128, :],
                            in_=q_sb[:, c * T:(c + 1) * T]), 16)
                    wait(e, m)
                b.g(gdbg)
                return nc

            ctx_mark = emit_attention_1h(bq, K, q_sb, k_sb, v_sb, ctx_sb,
                                         SCALE, True, qkv_marks, "att",
                                         upto=(upto if upto in
                                               ("score1", "bound", "norm1")
                                               or upto.startswith("tq")
                                               else "full"))

            if upto in ("att", "bound", "score1", "norm1") or upto.startswith("tq"):
                def gdbg(e):
                    wait(e, ctx_mark)
                    grp = b.gqq.begin(e)
                    for c in range(DC):
                        m = grp.tag(e.dma_start(
                            out=xout[c * 128:(c + 1) * 128, :],
                            in_=ctx_sb[:, c * T:(c + 1) * T]), 16)
                    wait(e, m)
                b.g(gdbg)
                return nc

        with ExitStack() as ph2:
            b2 = b.sub(ph2)
            xtg = ph2.enter_context(nc.sbuf_tensor("xtg", [128, DC * TGS], F32))
            t1 = ph2.enter_context(nc.sbuf_tensor("t1", [128, 32 * TGS], BF16))
            hh = ph2.enter_context(nc.sbuf_tensor("hh", [128, 32 * TGS], BF16))
            pr2 = PsumRot(b2, "prB", 4)
            pool_f = WPool(b2, "pF", 8192, 2, init_free=ctx_mark)
            wso = WStream(b2, pool_f, dr["wo"], D, D, 1024)
            ws1 = WStream(b2, pool_f, dr["w1"], D, 4096, 1024)
            ws2 = WStream(b2, pool_f, dr["w2"], D, 4096, 1024)
            wsh1 = WStream(b2, pool_f, dr["h1"], 4096, 4096, 256)
            wsh2 = WStream(b2, pool_f, dr["h2"], 4096, 4096, 256)
            ws3 = WStream(b2, pool_f, dr["w3"], 4096, D, 256)
            store_mark = ctx_mark

            for tg in range(NTG):
                mks = {}

                def yload(e, tg=tg, prev=store_mark):
                    wait(e, prev)
                    grp = b.yqq.begin(e)
                    for c in range(DC):
                        mks["xl"] = grp.tag(e.dma_start(
                            out=xtg[:, c * TGS:(c + 1) * TGS],
                            in_=xin[c * 128:(c + 1) * 128,
                                        tg * TGS:(tg + 1) * TGS]), 16)
                b2.y(yload)
                res_marks = []

                def evac_wo(mc, rid, ps, waits):
                    out = {}

                    def vev(e):
                        wait(e, *waits, mks["xl"])
                        e.drain()
                        out["m"] = b.vec.tag(e.tensor_tensor(
                            xtg[:, mc * TGS:(mc + 1) * TGS],
                            xtg[:, mc * TGS:(mc + 1) * TGS], ps[:], ALU.add))
                    b2.v(vev)
                    res_marks.append(out["m"])
                    return out["m"]
                fm_proj(b2, wso, DC, [tg],
                        lambda kc, rid: ctx_sb[:, kc * T + rid * TGS:
                                               kc * T + (rid + 1) * TGS],
                        evac_wo, pr2, DC, extra_waits=[ctx_mark])

                def evac_s1(mc, rid, ps, waits):
                    em = b2.sca_reserve()

                    def sev(e):
                        wait(e, *waits)
                        e.activation(
                            t1[:, mc * TGS:(mc + 1) * TGS], ps[:], AF.Silu,
                            scale=scl["s1c"][:, mc:mc + 1]).then_inc(b2.sca.h, 1)
                    b2.s(sev)
                    return em
                b2.s_defer()
                m1 = fm_proj(b2, ws1, DC, [tg],
                             lambda kc, rid: xn[:, kc * T + rid * TGS:
                                                kc * T + (rid + 1) * TGS],
                             evac_s1, pr2, 32)

                def evac_s2(mc, rid, ps, waits):
                    out = {}

                    def vev(e):
                        wait(e, *waits, *m1)
                        e.drain()
                        out["m"] = b.vec.tag(e.scalar_tensor_tensor(
                            hh[:, mc * TGS:(mc + 1) * TGS], ps[:],
                            scl["s2c"][:, mc:mc + 1],
                            t1[:, mc * TGS:(mc + 1) * TGS], ALU.mult, ALU.mult))
                    b2.v(vev)
                    return out["m"]
                m2_ = fm_proj(b2, ws2, DC, [tg],
                              lambda kc, rid: xn[:, kc * T + rid * TGS:
                                                 kc * T + (rid + 1) * TGS],
                              evac_s2, pr2, 32)

                def evac_g1(mc, rid, ps, waits):
                    em = b2.sca_reserve()

                    def sev(e):
                        wait(e, *waits)
                        e.activation(
                            t1[:, mc * TGS:(mc + 1) * TGS], ps[:], AF.Gelu,
                            scale=scl["sh1c"][:, mc:mc + 1]).then_inc(b2.sca.h, 1)
                    b2.s(sev)
                    return em
                m3 = fm_proj(b2, wsh1, 32, [0],
                             lambda kc, rid: hh[:, kc * TGS:(kc + 1) * TGS],
                             evac_g1, pr2, 32, extra_waits=m2_)

                def evac_g2(mc, rid, ps, waits):
                    em = b2.sca_reserve()

                    def sev(e):
                        wait(e, *waits)
                        e.activation(
                            hh[:, mc * TGS:(mc + 1) * TGS], ps[:], AF.Gelu,
                            scale=scl["sh2c"][:, mc:mc + 1]).then_inc(b2.sca.h, 1)
                    b2.s(sev)
                    return em
                m4 = fm_proj(b2, wsh2, 32, [0],
                             lambda kc, rid: t1[:, kc * TGS:(kc + 1) * TGS],
                             evac_g2, pr2, 32, extra_waits=m3)
                b2.s_flush()

                def evac_o(mc, rid, ps, waits):
                    out = {}

                    def vev(e):
                        wait(e, *waits)
                        e.drain()
                        out["m"] = b.vec.tag(e.scalar_tensor_tensor(
                            xtg[:, mc * TGS:(mc + 1) * TGS], ps[:],
                            scl["s3c"][:, mc:mc + 1],
                            xtg[:, mc * TGS:(mc + 1) * TGS], ALU.mult, ALU.add))
                    b2.v(vev)
                    res_marks.append(out["m"])
                    return out["m"]
                fm_proj(b2, ws3, 32, [0],
                        lambda kc, rid: hh[:, kc * TGS:(kc + 1) * TGS],
                        evac_o, pr2, DC, extra_waits=m4)

                def ystore(e, tg=tg, rm=list(res_marks)):
                    wait(e, rm[-1])
                    grp = b.yqq.begin(e)
                    for c in range(DC):
                        ystore.mark = grp.tag(e.dma_start(
                            out=xout[c * 128:(c + 1) * 128, tg * TGS:(tg + 1) * TGS],
                            in_=xtg[:, c * TGS:(c + 1) * TGS]), 16)
                b2.y(ystore)
                store_mark = ystore.mark

    return store_mark


def build_block3(upto="full"):
    nc = bass.Bass("TRN2", target_bir_lowering=False)
    dr = {}
    xin = nc.dram_tensor("x", [D, T], F32, kind="ExternalInput").ap()
    for nm, shape in B3_SHAPES:
        dr[nm] = nc.dram_tensor(nm, shape, F32, kind="ExternalInput").ap()
    cs = make_consts()
    for nm in CONST_NAMES:
        dr[nm] = nc.dram_tensor(nm, list(cs[nm].shape), F32, kind="ExternalInput").ap()
    xout = nc.dram_tensor("xout", [D, T], F32, kind="ExternalOutput").ap()
    with ExitStack() as ctx:
        blk = ctx.enter_context(nc.Block())
        b = B(nc, ctx, blk)
        K = ConstTiles(b, dr)
        with ExitStack() as bctx:
            fin = emit_block3(b.sub(bctx), bctx, dr, K, xin, xout, upto=upto)
        if isinstance(fin, tuple):
            b.y(lambda e: wait(e, fin))
    return nc


def prep_block3_inputs(inputs):
    m = dict(make_consts())
    m["wq"] = _t(inputs["L3_attn_Wq"])
    m["wk"] = _t(inputs["L3_attn_Wk"])
    m["wv"] = _t(inputs["L3_attn_Wv"])
    m["wo"] = _t(inputs["L3_attn_Wo"])
    m["w1"] = _t(inputs["L3_ffn_w1_w"])
    m["w2"] = _t(inputs["L3_ffn_w2_w"])
    m["h1"] = _t(inputs["L3_ffn_h1_w"])
    m["h2"] = _t(inputs["L3_ffn_h2_w"])
    m["w3"] = _t(inputs["L3_ffn_w3_w"])
    m["s1c"] = _cols(inputs["L3_ffn_w1_s"])
    m["s2c"] = _cols(inputs["L3_ffn_w2_s"])
    m["sh1c"] = _cols(inputs["L3_ffn_h1_s"])
    m["sh2c"] = _cols(inputs["L3_ffn_h2_s"])
    m["s3c"] = _cols(inputs["L3_ffn_w3_s"])
    return m


def run_block3(x_fm_percore, inputs):
    nc = build_block3()
    base = prep_block3_inputs(inputs)
    in_maps = []
    for core in range(8):
        m = dict(base)
        m["x"] = np.ascontiguousarray(x_fm_percore[core], dtype=np.float32)
        in_maps.append(m)
    res = run_bass_kernel_spmd(nc, in_maps, core_ids=list(range(8)))
    return [res.results[c]["xout"] for c in range(8)]



def emit_gates_top2(b, K, xn, wg_sb, bg_row, gsb, renorm_eps, ready_waits, pr):
    E = 8
    nc = b.nc
    with ExitStack() as ph:
        tmp = ph.enter_context(nc.sbuf_tensor("gt_tmp", [128, 6 * E + 8], F32))
        p = tmp[:, 0:E]
        pm = tmp[:, E:2 * E]
        ge = tmp[:, 2 * E:3 * E]
        w = tmp[:, 3 * E:4 * E]
        m1 = tmp[:, 6 * E:6 * E + 1]
        s1 = tmp[:, 6 * E + 1:6 * E + 2]
        r1 = tmp[:, 6 * E + 2:6 * E + 3]
        mks = {}
        for tc in range(16):
            pidx, ps, ps_free = pr.get()

            def tmm(e, tc=tc, ps=ps, ps_free=ps_free, first=(tc == 0)):
                if first:
                    wait(e, *ready_waits)
                wait(e, ps_free, mks.get("gev"))
                for kc in range(DC):
                    e.matmul(ps[:, 0:E], xn[:, kc * T + tc * 128: kc * T + (tc + 1) * 128],
                             wg_sb[:, kc * E:(kc + 1) * E], start=(kc == 0), stop=False)
                mks["gmm"] = b.pe.tag(e.matmul(ps[:, 0:E], K.ones_row[:], bg_row[:],
                                               start=False, stop=True))
            b.t(tmm)

            def vgate(e, tc=tc, ps=ps):
                wait(e, mks["gmm"])
                e.drain()
                e.reduce_max(m1, ps[:, 0:E], AX.X, negate=True)
                e.drain()
                mks["g1"] = b.vec.tag(e.tensor_copy(pm, ps[:, 0:E]))
            b.v(vgate)
            pr.set_free(pidx, mks["g1"])

            def sgate(e):
                wait(e, mks["g1"])
                mks["g2"] = b.sca.tag(e.activation(p, pm, AF.Exp, bias=m1,
                                                   accum_out=s1))
            b.s(sgate)

            def vgate2(e, tc=tc):
                wait(e, mks["g2"])
                e.drain()
                e.reciprocal(r1, s1)
                e.drain()
                e.tensor_scalar_mul(p, p, r1)
                e.drain()
                e.reduce_max(m1, p, AX.X)
                e.drain()
                e.tensor_single_scalar(ge, p, m1, ALU.is_ge)
                e.drain()
                e.scalar_tensor_tensor(pm, ge, -2.0, p, ALU.mult, ALU.add)
                e.drain()
                e.reduce_max(m1, pm, AX.X)
                e.drain()
                e.tensor_single_scalar(ge, p, m1, ALU.is_ge)
                e.drain()
                e.tensor_tensor(w, p, ge, ALU.mult)
                e.drain()
                if renorm_eps is not None:
                    e.reduce_sum(s1, w, AX.X)
                    e.drain()
                    e.tensor_scalar_add(s1, s1, renorm_eps if renorm_eps is not None else 0.0)
                    e.drain()
                    e.reciprocal(r1, s1)
                    e.drain()
                    e.tensor_scalar_mul(w, w, r1)
                    e.drain()
                mks["gev"] = b.vec.tag(e.tensor_copy(gsb[:, tc * 8:(tc + 1) * 8], w))
            b.v(vgate2)
    return mks["gev"]


B1_SHAPES = [("wq", [D, D]), ("wk", [D, D]), ("wv", [D, D]),
             ("wproj", [D, D]), ("bqc", [128, 8]), ("bkc", [128, 8]),
             ("bvrow", [1, D]), ("bprojc", [128, 8]),
             ("wg", [D, 8]), ("bgrow", [1, 8]),
             ("w1", [8 * D, 4096]), ("b1c", [128, 8 * 32]),
             ("w2", [8 * 4096, D]), ("b2row", [1, 8 * D])]


def emit_block1(b, ctx, dr, K, xin, xout, upto="full"):
    nc = b.nc
    SCALE = 1.0 / 64.0
    if True:
        xn = ctx.enter_context(nc.sbuf_tensor("xn", [128, DC * T], BF16))
        gsb = ctx.enter_context(nc.sbuf_tensor("gsb", [128, 16 * 8], F32))
        colt = {}
        cmarks = []
        for nm, ncol in [("bqc", 8), ("bkc", 8), ("bprojc", 8), ("b1c", 8 * 32)]:
            colt[nm], m = load_cols(b, nm, dr, ncol)
            cmarks.append(m)
        brow = {}
        brow["bgrow"] = ctx.enter_context(nc.sbuf_tensor("k_bgrow", [1, 8], BF16))

        def growloads(e):
            grp = b.gqq.begin(e)
            cmarks.append(grp.tag(e.dma_start(out=brow["bgrow"][:],
                                              in_=dr["bgrow"]), 16))
        b.g(growloads)
        wg_sb = ctx.enter_context(nc.sbuf_tensor("k_wg", [128, DC * 8], BF16))

        def gwg(e):
            grp = b.gqq.begin(e)
            cmarks.append(grp.tag(e.dma_start(
                out=wg_sb[:].rearrange("p (kc m) -> p kc m", kc=DC),
                in_=dr["wg"].rearrange("(kc p) m -> p kc m", p=128)), 16))
        b.g(gwg)
        K.barrier(b, extra=cmarks)

        xn_mark = emit_ln(b, xin, xn, K)

        if upto == "ln":
            def gdbg(e):
                wait(e, xn_mark)
                grp = b.gqq.begin(e)
                for c in range(DC):
                    m = grp.tag(e.dma_start(out=xout[c * 128:(c + 1) * 128, :],
                                            in_=xn[:, c * T:(c + 1) * T]), 16)
                wait(e, m)
            b.g(gdbg)
            return nc

        ctx_sb = ctx.enter_context(nc.sbuf_tensor("ctx_sb", [128, DC * T], BF16))
        if upto == "att":
            def vclr0(e):
                wait(e, xn_mark)
                e.drain()
                b.vec.tag(e.memset(ctx_sb[:], 0.0))
            b.v(vclr0)
        with ExitStack() as ph:
            q_sb = ph.enter_context(nc.sbuf_tensor("q_sb", [128, DC * T], BF16))
            k_sb = ph.enter_context(nc.sbuf_tensor("k_sb", [128, DC * T], BF16))
            v_sb = ph.enter_context(nc.sbuf_tensor("v_sb", [128, 16 * VSTRIDE], BF16))
            bq = b.sub(ph)
            qkv_marks = []
            with ExitStack() as phw:
                bw = b.sub(phw)
                pr = PsumRot(bw, "prA", 4)
                pool_a = WPool(bw, "pA", 4 * 1024, 2)
                brow["bvrow"] = phw.enter_context(
                    nc.sbuf_tensor("k_bvrow", [1, D], BF16))
                bvm = {}

                def gbv(e):
                    grp = b.gqq.begin(e)
                    bvm["m"] = grp.tag(e.dma_start(out=brow["bvrow"][:],
                                                   in_=dr["bvrow"]), 16)
                b.g(gbv)

                for wnm, dst, bc in [("wq", q_sb, "bqc"), ("wk", k_sb, "bkc")]:
                    ws = WStream(bw, pool_a, dr[wnm], D, D, 512)

                    def evac(mc, rid, ps, waits, dst=dst, bc=bc):
                        out = {}

                        def sev(e):
                            wait(e, *waits)
                            out["m"] = b.sca.tag(e.activation(
                                dst[:, mc * T + rid * TGS: mc * T + (rid + 1) * TGS],
                                ps[:], AF.Identity, bias=colt[bc][:, mc:mc + 1]))
                        b.s(sev)
                        return out["m"]
                    qkv_marks += fm_proj(bw, ws, DC, list(range(NTG)),
                                         lambda kc, rid: xn[:, kc * T + rid * TGS:
                                                            kc * T + (rid + 1) * TGS],
                                         evac, pr, DC, extra_waits=[xn_mark])

                vms = {}

                def vmemset(e):
                    wait(e, xn_mark)
                    e.drain()
                    vms["m"] = b.vec.tag(e.memset(v_sb[:], 1.0))
                b.v(vmemset)

                wsv = WStream(bw, pool_a, dr["wv"], D, D, 512)

                def vevac(tc, fb, ps, waits):
                    out = {}

                    def vev(e):
                        wait(e, *waits, vms["m"])
                        e.drain()
                        for j in range(8):
                            h = fb * 8 + j
                            out["m"] = b.vec.tag(e.tensor_copy(
                                v_sb[:, tc * VSTRIDE + h * 65: tc * VSTRIDE + h * 65 + 64],
                                ps[:, j * 64:(j + 1) * 64]))
                    b.v(vev)
                    return out["m"]
                qkv_marks += tm_proj(bw, wsv, DC,
                                     lambda kc, tc: xn[:, kc * T + tc * 128:
                                                       kc * T + (tc + 1) * 128],
                                     vevac, pr, 16,
                                     extra_waits=[xn_mark, bvm["m"]],
                                     bias_row=lambda fb: brow["bvrow"][:, fb * 512:(fb + 1) * 512],
                                     ones_row=K.ones_row)

            if upto == "qkv":
                def gdbg(e):
                    wait(e, *qkv_marks)
                    grp = b.gqq.begin(e)
                    for c in range(DC):
                        m = grp.tag(e.dma_start(out=xout[c * 128:(c + 1) * 128, :],
                                                in_=q_sb[:, c * T:(c + 1) * T]), 16)
                    wait(e, m)
                b.g(gdbg)
                return nc

            ctx_mark = emit_attention_mh(bq, K, q_sb, k_sb, v_sb, ctx_sb,
                                         SCALE, True, qkv_marks, "mha")

            if upto == "att":
                def gdbg(e):
                    wait(e, ctx_mark)
                    grp = b.gqq.begin(e)
                    for c in range(DC):
                        m = grp.tag(e.dma_start(out=xout[c * 128:(c + 1) * 128, :],
                                                in_=ctx_sb[:, c * T:(c + 1) * T]), 16)
                    wait(e, m)
                b.g(gdbg)
                return nc

        with ExitStack() as ph2:
            b2 = b.sub(ph2)
            xtg = ph2.enter_context(nc.sbuf_tensor("xtg", [128, DC * TGS], F32))
            hh = ph2.enter_context(nc.sbuf_tensor("hh", [128, 32 * TGS], BF16))
            acc = ph2.enter_context(nc.sbuf_tensor("acc", [128, 4 * 1024], F32))
            pr2 = PsumRot(b2, "prB", 4)
            pool_f = WPool(b2, "pF", 16 * 1024, 2, init_free=ctx_mark)
            b2r2 = [ph2.enter_context(nc.sbuf_tensor(f"k_b2r{i}", [1, D], BF16))
                    for i in range(2)]
            b2m = {}
            ws1 = [WStream(b2, pool_f, dr["w1"][e * D:(e + 1) * D, :], D, 4096, 1024)
                   for e in range(8)]
            ws2 = [WStream(b2, pool_f, dr["w2"][e * 4096:(e + 1) * 4096, :],
                           4096, D, 512) for e in range(8)]
            wsp = WStream(b2, pool_f, dr["wproj"], D, D, 1024)

            gate_mark = emit_gates_top2(b2, K, xn, wg_sb, brow["bgrow"], gsb,
                                        1e-8, [xn_mark, ctx_mark], pr2)

            store_mark = ctx_mark
            for tg in range(NTG):
                mks = {}

                def yload(e, tg=tg, prev=store_mark, gm=gate_mark):
                    wait(e, prev, gm)
                    grp = b.yqq.begin(e)
                    for c in range(DC):
                        mks["xl"] = grp.tag(e.dma_start(
                            out=xtg[:, c * TGS:(c + 1) * TGS],
                            in_=xin[c * 128:(c + 1) * 128,
                                        tg * TGS:(tg + 1) * TGS]), 16)
                b2.y(yload)
                res_marks = []

                def evac_proj(mc, rid, ps, waits):
                    out = {}

                    def vev(e):
                        wait(e, *waits, mks["xl"])
                        e.drain()
                        out["m"] = b.vec.tag(e.scalar_tensor_tensor(
                            xtg[:, mc * TGS:(mc + 1) * TGS], ps[:],
                            colt["bprojc"][:, mc:mc + 1],
                            xtg[:, mc * TGS:(mc + 1) * TGS], ALU.add, ALU.add))
                    b2.v(vev)
                    res_marks.append(out["m"])
                    return out["m"]
                fm_proj(b2, wsp, DC, [tg],
                        lambda kc, rid: ctx_sb[:, kc * T + rid * TGS:
                                               kc * T + (rid + 1) * TGS],
                        evac_proj, pr2, DC, extra_waits=[ctx_mark])

                def vclr(e, rm=list(res_marks)):
                    wait(e, rm[-1] if rm else None)
                    e.drain()
                    mks["clr"] = b.vec.tag(e.memset(acc[:], 0.0))
                b2.v(vclr)

                b2.s_defer()
                for ee in range(8):
                    def gb2(e, ee=ee,
                            w=(b2m.get(("slot", ee % 2)), ctx_mark)):
                        wait(e, *[m for m in w if m])
                        grp = b.gqq.begin(e)
                        b2m["m"] = grp.tag(e.dma_start(
                            out=b2r2[ee % 2][:],
                            in_=dr["b2row"][:, ee * D:(ee + 1) * D]), 16)
                    b.g(gb2)

                    def evac_h(mc, rid, ps, waits, ee=ee):
                        em = b2.sca_reserve()

                        def sev(e, mc=mc, ps=ps, w=tuple(waits)):
                            wait(e, *w)
                            e.activation(
                                hh[:, mc * TGS:(mc + 1) * TGS], ps[:],
                                AF.Gelu_apprx_tanh,
                                bias=colt["b1c"][:, ee * 32 + mc: ee * 32 + mc + 1]
                            ).then_inc(b2.sca.h, 1)
                        b2.s(sev)
                        return em
                    hmarks = fm_proj(b2, ws1[ee], DC, [tg],
                                     lambda kc, rid: xn[:, kc * T + rid * TGS:
                                                        kc * T + (rid + 1) * TGS],
                                     evac_h, pr2, 32)

                    def evac_o(tc, fb, ps, waits, ee=ee, tg=tg):
                        out = {}

                        def vev(e):
                            wait(e, *waits, mks["clr"])
                            e.drain()
                            out["m"] = b.vec.tag(e.scalar_tensor_tensor(
                                acc[:, tc * 1024 + fb * 512: tc * 1024 + (fb + 1) * 512],
                                ps[:, 0:512],
                                gsb[:, (tg * 4 + tc) * 8 + ee:
                                    (tg * 4 + tc) * 8 + ee + 1],
                                acc[:, tc * 1024 + fb * 512: tc * 1024 + (fb + 1) * 512],
                                ALU.mult, ALU.add))
                        b2.v(vev)
                        return out["m"]
                    omarks = tm_proj(b2, ws2[ee], 32,
                                     lambda kc, tc: hh[:, kc * TGS + tc * 128:
                                                       kc * TGS + (tc + 1) * 128],
                                     evac_o, pr2, 4,
                                     extra_waits=list(hmarks) + [b2m["m"]],
                                     bias_row=lambda fb, ee=ee:
                                         b2r2[ee % 2][:, fb * 512:
                                                       fb * 512 + 512],
                                     ones_row=K.ones_row)
                    mks["moe"] = omarks[-1]
                    b2m[("slot", ee % 2)] = omarks[-1]
                b2.s_flush()

                for tc in range(4):
                    for mc in range(DC):
                        def ttp(e, tc=tc, mc=mc):
                            wait(e, mks["moe"])
                            pidx, ps, ps_free = pr2.get()
                            wait(e, ps_free)
                            mks["tp"] = b.pe.tag(e.matmul(
                                ps[:, 0:128],
                                acc[:, tc * 1024 + mc * 128: tc * 1024 + (mc + 1) * 128],
                                K.ident_f[:], start=True, stop=True))
                            ttp.ps = (pidx, ps)
                        b2.t(ttp)
                        pidx, ps = ttp.ps

                        def vadd(e, tc=tc, mc=mc, ps=ps, pidx=pidx):
                            wait(e, mks["tp"])
                            e.drain()
                            m = b.vec.tag(e.tensor_tensor(
                                xtg[:, mc * TGS + tc * 128: mc * TGS + (tc + 1) * 128],
                                xtg[:, mc * TGS + tc * 128: mc * TGS + (tc + 1) * 128],
                                ps[:, 0:128], ALU.add))
                            pr2.set_free(pidx, m)
                            vadd.mark = m
                        b2.v(vadd)
                        res_marks.append(vadd.mark)

                def ystore(e, tg=tg, rm=list(res_marks)):
                    wait(e, rm[-1])
                    grp = b.yqq.begin(e)
                    for c in range(DC):
                        ystore.mark = grp.tag(e.dma_start(
                            out=xout[c * 128:(c + 1) * 128, tg * TGS:(tg + 1) * TGS],
                            in_=xtg[:, c * TGS:(c + 1) * TGS]), 16)
                b2.y(ystore)
                store_mark = ystore.mark

    return store_mark


def build_block1(upto="full"):
    nc = bass.Bass("TRN2", target_bir_lowering=False)
    dr = {}
    xin = nc.dram_tensor("x", [D, T], F32, kind="ExternalInput").ap()
    for nm, shape in B1_SHAPES:
        dr[nm] = nc.dram_tensor(nm, shape, F32, kind="ExternalInput").ap()
    cs = make_consts()
    for nm in CONST_NAMES:
        dr[nm] = nc.dram_tensor(nm, list(cs[nm].shape), F32, kind="ExternalInput").ap()
    xout = nc.dram_tensor("xout", [D, T], F32, kind="ExternalOutput").ap()
    with ExitStack() as ctx:
        blk = ctx.enter_context(nc.Block())
        b = B(nc, ctx, blk)
        K = ConstTiles(b, dr)
        with ExitStack() as bctx:
            fin = emit_block1(b.sub(bctx), bctx, dr, K, xin, xout, upto=upto)
        if isinstance(fin, tuple):
            b.y(lambda e: wait(e, fin))
    return nc


def prep_block1_inputs(inputs):
    m = dict(make_consts())
    wat = _t(inputs["L1_attn_W_attn"])
    m["wq"] = np.ascontiguousarray(wat[:, 0:D])
    m["wk"] = np.ascontiguousarray(wat[:, D:2 * D])
    m["wv"] = np.ascontiguousarray(wat[:, 2 * D:3 * D])
    ba = np.asarray(inputs["L1_attn_b_attn"])
    m["bqc"] = _cols(ba[0:D])
    m["bkc"] = _cols(ba[D:2 * D])
    m["bvrow"] = np.ascontiguousarray(ba[2 * D:3 * D].reshape(1, D))
    m["wproj"] = _t(inputs["L1_attn_W_proj"])
    m["bprojc"] = _cols(inputs["L1_attn_b_proj"])
    m["wg"] = np.ascontiguousarray(inputs["L1_ffn_Wg"])
    m["bgrow"] = np.ascontiguousarray(np.asarray(inputs["L1_ffn_bg"]).reshape(1, 8))
    m["w1"] = np.ascontiguousarray(np.asarray(inputs["L1_ffn_W1"]).reshape(8 * D, 4096))
    m["b1c"] = np.ascontiguousarray(
        np.asarray(inputs["L1_ffn_b1"]).reshape(8, 32, 128).transpose(2, 0, 1).reshape(128, 8 * 32))
    m["w2"] = np.ascontiguousarray(np.asarray(inputs["L1_ffn_W2"]).reshape(8 * 4096, D))
    m["b2row"] = np.ascontiguousarray(np.asarray(inputs["L1_ffn_b2"]).reshape(1, 8 * D))
    return m


def run_block1(x_fm_percore, inputs, upto="full"):
    nc = build_block1(upto=upto)
    base = prep_block1_inputs(inputs)
    in_maps = []
    for core in range(8):
        m = dict(base)
        m["x"] = np.ascontiguousarray(x_fm_percore[core], dtype=np.float32)
        in_maps.append(m)
    res = run_bass_kernel_spmd(nc, in_maps, core_ids=list(range(8)))
    return [res.results[c]["xout"] for c in range(8)]



B2_SHAPES = [("wq", [D, D]), ("wk", [D, D]), ("wv", [D, D]),
             ("wproj", [D, D]), ("bqc", [128, 8]), ("bkc", [128, 8]),
             ("bvrow", [1, D]), ("bprojc", [128, 8]),
             ("wrouter", [D, 16]), ("brrow", [1, 16]),
             ("wexp", [16 * D, D])]


def emit_block2(b, ctx, dr, K, xin, xout, upto="full"):
    nc = b.nc
    SCALE = 1.0 / 8.0
    E2 = 16
    EPS2 = 1e-10
    if True:
        xn = ctx.enter_context(nc.sbuf_tensor("xn", [128, DC * T], BF16))
        gsb = ctx.enter_context(nc.sbuf_tensor("gsb", [128, 16 * E2], F32))
        acsb = ctx.enter_context(nc.sbuf_tensor("acsb", [128, 16], F32))
        colt = {}
        cmarks = []
        for nm, ncol in [("bqc", 8), ("bkc", 8), ("bprojc", 8)]:
            colt[nm], m = load_cols(b, nm, dr, ncol)
            cmarks.append(m)
        brow = {}
        brow["brrow"] = ctx.enter_context(nc.sbuf_tensor("k_brrow", [1, 16], BF16))
        wr_sb = ctx.enter_context(nc.sbuf_tensor("k_wr", [128, DC * 16], BF16))

        def gsmall(e):
            grp = b.gqq.begin(e)
            grp.tag(e.dma_start(out=brow["brrow"][:], in_=dr["brrow"]), 16)
            cmarks.append(grp.tag(e.dma_start(
                out=wr_sb[:].rearrange("p (kc m) -> p kc m", kc=DC),
                in_=dr["wrouter"].rearrange("(kc p) m -> p kc m", p=128)), 16))
        b.g(gsmall)
        K.barrier(b, extra=cmarks)

        xn_mark = emit_ln(b, xin, xn, K)

        ctx_sb = ctx.enter_context(nc.sbuf_tensor("ctx_sb", [128, DC * T], BF16))
        with ExitStack() as ph:
            q_sb = ph.enter_context(nc.sbuf_tensor("q_sb", [128, DC * T], BF16))
            k_sb = ph.enter_context(nc.sbuf_tensor("k_sb", [128, DC * T], BF16))
            v_sb = ph.enter_context(nc.sbuf_tensor("v_sb", [128, 16 * VSTRIDE], BF16))
            bq = b.sub(ph)
            qkv_marks = []
            with ExitStack() as phw:
                bw = b.sub(phw)
                pr = PsumRot(bw, "prA", 4)
                pool_a = WPool(bw, "pA", 4 * 1024, 2)
                brow["bvrow"] = phw.enter_context(
                    nc.sbuf_tensor("k_bvrow", [1, D], BF16))
                bvm = {}

                def gbv(e):
                    grp = b.gqq.begin(e)
                    bvm["m"] = grp.tag(e.dma_start(out=brow["bvrow"][:],
                                                   in_=dr["bvrow"]), 16)
                b.g(gbv)

                for wnm, dst, bc in [("wq", q_sb, "bqc"), ("wk", k_sb, "bkc")]:
                    ws = WStream(bw, pool_a, dr[wnm], D, D, 512)

                    def evac(mc, rid, ps, waits, dst=dst, bc=bc):
                        out = {}

                        def sev(e):
                            wait(e, *waits)
                            out["m"] = b.sca.tag(e.activation(
                                dst[:, mc * T + rid * TGS: mc * T + (rid + 1) * TGS],
                                ps[:], AF.Identity, bias=colt[bc][:, mc:mc + 1]))
                        b.s(sev)
                        return out["m"]
                    qkv_marks += fm_proj(bw, ws, DC, list(range(NTG)),
                                         lambda kc, rid: xn[:, kc * T + rid * TGS:
                                                            kc * T + (rid + 1) * TGS],
                                         evac, pr, DC, extra_waits=[xn_mark])

                vms = {}

                def vmemset(e):
                    wait(e, xn_mark)
                    e.drain()
                    vms["m"] = b.vec.tag(e.memset(v_sb[:], 1.0))
                b.v(vmemset)

                wsv = WStream(bw, pool_a, dr["wv"], D, D, 512)

                def vevac(tc, fb, ps, waits):
                    out = {}

                    def vev(e):
                        wait(e, *waits, vms["m"])
                        e.drain()
                        for j in range(8):
                            h = fb * 8 + j
                            out["m"] = b.vec.tag(e.tensor_copy(
                                v_sb[:, tc * VSTRIDE + h * 65: tc * VSTRIDE + h * 65 + 64],
                                ps[:, j * 64:(j + 1) * 64]))
                    b.v(vev)
                    return out["m"]
                qkv_marks += tm_proj(bw, wsv, DC,
                                     lambda kc, tc: xn[:, kc * T + tc * 128:
                                                       kc * T + (tc + 1) * 128],
                                     vevac, pr, 16,
                                     extra_waits=[xn_mark, bvm["m"]],
                                     bias_row=lambda fb: brow["bvrow"][:, fb * 512:(fb + 1) * 512],
                                     ones_row=K.ones_row)

            ctx_mark = emit_attention_mh(bq, K, q_sb, k_sb, v_sb, ctx_sb,
                                         SCALE, True, qkv_marks, "mha")

            if upto == "att":
                def gdbg(e):
                    wait(e, ctx_mark)
                    grp = b.gqq.begin(e)
                    for c in range(DC):
                        m = grp.tag(e.dma_start(out=xout[c * 128:(c + 1) * 128, :],
                                                in_=ctx_sb[:, c * T:(c + 1) * T]), 16)
                    wait(e, m)
                b.g(gdbg)
                return nc

        with ExitStack() as ph2:
            b2 = b.sub(ph2)
            xtg = ph2.enter_context(nc.sbuf_tensor("xtg", [128, DC * TGS], F32))
            acc = ph2.enter_context(nc.sbuf_tensor("acc", [128, 4 * 1024], F32))
            pr2 = PsumRot(b2, "prB", 4)
            pool_f = WPool(b2, "pF", 8 * 1024, 2, init_free=ctx_mark)
            wse = [WStream(b2, pool_f, dr["wexp"][e * D:(e + 1) * D, :], D, D, 512)
                   for e in range(E2)]
            wsp = WStream(b2, pool_f, dr["wproj"], D, D, 1024)

            gmks = {}
            with ExitStack() as phg:
                gt = phg.enter_context(nc.sbuf_tensor("gt", [128, 4 * E2 + 8], F32))
                p = gt[:, 0:E2]
                pm = gt[:, E2:2 * E2]
                ge = gt[:, 2 * E2:3 * E2]
                w = gt[:, 3 * E2:4 * E2]
                m1 = gt[:, 4 * E2:4 * E2 + 1]
                s1 = gt[:, 4 * E2 + 1:4 * E2 + 2]
                r1 = gt[:, 4 * E2 + 2:4 * E2 + 3]
                for tc in range(16):
                    pidx, ps, ps_free = pr2.get()

                    def tmm(e, tc=tc, ps=ps, ps_free=ps_free, first=(tc == 0)):
                        if first:
                            wait(e, xn_mark, ctx_mark)
                        wait(e, ps_free)
                        for kc in range(DC):
                            e.matmul(ps[:, 0:E2],
                                     xn[:, kc * T + tc * 128: kc * T + (tc + 1) * 128],
                                     wr_sb[:, kc * E2:(kc + 1) * E2],
                                     start=(kc == 0), stop=False)
                        gmks["gmm"] = b.pe.tag(e.matmul(
                            ps[:, 0:E2], K.ones_row[:], brow["brrow"][:],
                            start=False, stop=True))
                    b2.t(tmm)

                    def vg1(e, ps=ps):
                        wait(e, gmks["gmm"])
                        e.drain()
                        e.reduce_max(m1, ps[:, 0:E2], AX.X, negate=True)
                        e.drain()
                        gmks["g1"] = b.vec.tag(e.tensor_copy(pm, ps[:, 0:E2]))
                    b2.v(vg1)
                    pr2.set_free(pidx, gmks["g1"])

                    def sg(e):
                        wait(e, gmks["g1"])
                        gmks["g2"] = b.sca.tag(e.activation(p, pm, AF.Exp, bias=m1,
                                                            accum_out=s1))
                    b2.s(sg)

                    def vg2(e, tc=tc):
                        wait(e, gmks["g2"])
                        e.drain()
                        e.reciprocal(r1, s1)
                        e.drain()
                        e.tensor_scalar_mul(p, p, r1)
                        e.drain()
                        e.tensor_copy(pm, p)
                        e.drain()
                        for it in range(5):
                            e.reduce_max(m1, pm, AX.X)
                            e.drain()
                            e.tensor_single_scalar(ge, pm, m1, ALU.is_ge)
                            e.drain()
                            e.scalar_tensor_tensor(pm, ge, -9.0, pm, ALU.mult, ALU.add)
                            e.drain()
                        e.reduce_max(m1, pm, AX.X)
                        e.drain()
                        e.tensor_single_scalar(ge, p, m1, ALU.is_ge)
                        e.drain()
                        e.tensor_tensor(w, p, ge, ALU.mult)
                        e.drain()
                        e.reduce_sum(s1, w, AX.X)
                        e.drain()
                        e.reciprocal(r1, s1)
                        e.drain()
                        e.tensor_scalar_mul(w, w, r1)
                        e.drain()
                        gmks["gev"] = b.vec.tag(e.tensor_copy(
                            gsb[:, tc * E2:(tc + 1) * E2], w))
                        e.drain()
                        e.tensor_scalar_add(r1, s1, 1e-10)
                        e.drain()
                        e.reciprocal(r1, r1)
                        e.drain()
                        e.tensor_tensor(r1, r1, s1, ALU.mult)
                        e.drain()
                        gmks["gac"] = b.vec.tag(e.tensor_scalar_mul(
                            acsb[:, tc:tc + 1], r1, 1e-10))
                    b2.v(vg2)
            gate_mark = gmks["gac"]

            store_mark = ctx_mark
            for tg in range(NTG):
                mks = {}

                def yload(e, tg=tg, prev=store_mark, gm=gate_mark):
                    wait(e, prev, gm)
                    grp = b.yqq.begin(e)
                    for c in range(DC):
                        mks["xl"] = grp.tag(e.dma_start(
                            out=xtg[:, c * TGS:(c + 1) * TGS],
                            in_=xin[c * 128:(c + 1) * 128,
                                        tg * TGS:(tg + 1) * TGS]), 16)
                b2.y(yload)
                res_marks = []

                def evac_proj(mc, rid, ps, waits):
                    out = {}

                    def vev(e):
                        wait(e, *waits, mks["xl"])
                        e.drain()
                        out["m"] = b.vec.tag(e.scalar_tensor_tensor(
                            xtg[:, mc * TGS:(mc + 1) * TGS], ps[:],
                            colt["bprojc"][:, mc:mc + 1],
                            xtg[:, mc * TGS:(mc + 1) * TGS], ALU.add, ALU.add))
                    b2.v(vev)
                    res_marks.append(out["m"])
                    return out["m"]
                fm_proj(b2, wsp, DC, [tg],
                        lambda kc, rid: ctx_sb[:, kc * T + rid * TGS:
                                               kc * T + (rid + 1) * TGS],
                        evac_proj, pr2, DC, extra_waits=[ctx_mark])

                def vclr(e, rm=list(res_marks)):
                    wait(e, rm[-1] if rm else None)
                    e.drain()
                    mks["clr"] = b.vec.tag(e.memset(acc[:], 0.0))
                b2.v(vclr)

                for ee in range(E2):
                    def evac_e(tc, fb, ps, waits, ee=ee, tg=tg):
                        out = {}

                        def vev(e):
                            wait(e, *waits, mks["clr"])
                            e.drain()
                            out["m"] = b.vec.tag(e.scalar_tensor_tensor(
                                acc[:, tc * 1024 + fb * 512: tc * 1024 + (fb + 1) * 512],
                                ps[:, 0:512],
                                gsb[:, (tg * 4 + tc) * E2 + ee:
                                    (tg * 4 + tc) * E2 + ee + 1],
                                acc[:, tc * 1024 + fb * 512: tc * 1024 + (fb + 1) * 512],
                                ALU.mult, ALU.add))
                        b2.v(vev)
                        return out["m"]
                    omarks = tm_proj(b2, wse[ee], DC,
                                     lambda kc, tc: xn[:, kc * T + (tg * 4 + tc) * 128:
                                                       kc * T + (tg * 4 + tc + 1) * 128],
                                     evac_e, pr2, 4)
                    mks["moe"] = omarks[-1]

                def vac(e, tg=tg):
                    wait(e, mks["moe"])
                    e.drain()
                    for tc in range(4):
                        ins = e.tensor_scalar_add(
                            acc[:, tc * 1024:(tc + 1) * 1024],
                            acc[:, tc * 1024:(tc + 1) * 1024],
                            acsb[:, tg * 4 + tc: tg * 4 + tc + 1])
                    mks["ac"] = b.vec.tag(ins)
                b2.v(vac)

                for tc in range(4):
                    for mc in range(DC):
                        def ttp(e, tc=tc, mc=mc):
                            wait(e, mks["ac"])
                            pidx, ps, ps_free = pr2.get()
                            wait(e, ps_free)
                            mks["tp"] = b.pe.tag(e.matmul(
                                ps[:, 0:128],
                                acc[:, tc * 1024 + mc * 128: tc * 1024 + (mc + 1) * 128],
                                K.ident_f[:], start=True, stop=True))
                            ttp.ps = (pidx, ps)
                        b2.t(ttp)
                        pidx, ps = ttp.ps

                        def vadd(e, tc=tc, mc=mc, ps=ps, pidx=pidx):
                            wait(e, mks["tp"])
                            e.drain()
                            m = b.vec.tag(e.tensor_tensor(
                                xtg[:, mc * TGS + tc * 128: mc * TGS + (tc + 1) * 128],
                                xtg[:, mc * TGS + tc * 128: mc * TGS + (tc + 1) * 128],
                                ps[:, 0:128], ALU.add))
                            pr2.set_free(pidx, m)
                            vadd.mark = m
                        b2.v(vadd)
                        res_marks.append(vadd.mark)

                def ystore(e, tg=tg, rm=list(res_marks)):
                    wait(e, rm[-1])
                    grp = b.yqq.begin(e)
                    for c in range(DC):
                        ystore.mark = grp.tag(e.dma_start(
                            out=xout[c * 128:(c + 1) * 128, tg * TGS:(tg + 1) * TGS],
                            in_=xtg[:, c * TGS:(c + 1) * TGS]), 16)
                b2.y(ystore)
                store_mark = ystore.mark

    return store_mark


def build_block2(upto="full"):
    nc = bass.Bass("TRN2", target_bir_lowering=False)
    dr = {}
    xin = nc.dram_tensor("x", [D, T], F32, kind="ExternalInput").ap()
    for nm, shape in B2_SHAPES:
        dr[nm] = nc.dram_tensor(nm, shape, F32, kind="ExternalInput").ap()
    cs = make_consts()
    for nm in CONST_NAMES:
        dr[nm] = nc.dram_tensor(nm, list(cs[nm].shape), F32, kind="ExternalInput").ap()
    xout = nc.dram_tensor("xout", [D, T], F32, kind="ExternalOutput").ap()
    with ExitStack() as ctx:
        blk = ctx.enter_context(nc.Block())
        b = B(nc, ctx, blk)
        K = ConstTiles(b, dr)
        with ExitStack() as bctx:
            fin = emit_block2(b.sub(bctx), bctx, dr, K, xin, xout, upto=upto)
        if isinstance(fin, tuple):
            b.y(lambda e: wait(e, fin))
    return nc


def prep_block2_inputs(inputs):
    m = dict(make_consts())
    wat = _t(inputs["L2_attn_qkv_w"])
    m["wq"] = np.ascontiguousarray(wat[:, 0:D])
    m["wk"] = np.ascontiguousarray(wat[:, D:2 * D])
    m["wv"] = np.ascontiguousarray(wat[:, 2 * D:3 * D])
    ba = np.asarray(inputs["L2_attn_qkv_b"])
    m["bqc"] = _cols(ba[0:D])
    m["bkc"] = _cols(ba[D:2 * D])
    m["bvrow"] = np.ascontiguousarray(ba[2 * D:3 * D].reshape(1, D))
    m["wproj"] = _t(inputs["L2_attn_out_w"])
    m["bprojc"] = _cols(inputs["L2_attn_out_b"])
    m["wrouter"] = _t(inputs["L2_ffn_router_w"])
    m["brrow"] = np.ascontiguousarray(np.asarray(inputs["L2_ffn_router_b"]).reshape(1, 16))
    m["wexp"] = np.ascontiguousarray(np.asarray(inputs["L2_ffn_expert_w"]).reshape(16 * D, D))
    return m


def run_block2(x_fm_percore, inputs, upto="full"):
    nc = build_block2(upto=upto)
    base = prep_block2_inputs(inputs)
    in_maps = []
    for core in range(8):
        m = dict(base)
        m["x"] = np.ascontiguousarray(x_fm_percore[core], dtype=np.float32)
        in_maps.append(m)
    res = run_bass_kernel_spmd(nc, in_maps, core_ids=list(range(8)))
    return [res.results[c]["xout"] for c in range(8)]



DI = 2048
DIC = DI // 128
NST = 16


B0_SHAPES = [
    ("inproj", [D, 2 * DI]), ("convw", [128, DIC * 4]), ("convbc", [128, DIC]),
    ("xproj", [DI, 96]), ("dtproj", [64, DI]), ("dtbc", [128, DIC]),
    ("alogc", [128, DIC * NST]), ("dcolc", [128, DIC]), ("outproj", [DI, D]),
    ("fgw", [D, D]), ("fgbc", [128, DC]),
    ("wq", [D, D]), ("wk", [D, D]), ("wv", [D, D]), ("wo", [D, D]),
    ("bqc", [128, 8]), ("bkc", [128, 8]), ("bvrow", [1, D]), ("borow", [1, D]),
    ("g1c", [128, DC]), ("be1c", [128, DC]), ("g2c", [128, DC]),
    ("be2c", [128, DC]),
    ("wg", [D, 8]), ("bgrow", [1, 8]),
    ("we", [8 * D, D]), ("bexprow", [1, 8 * D]),
    ("sel16", [16, 16 * 128])]


def emit_block0(b, ctx, dr, K, xin, xout, upto="full"):
    nc = b.nc
    u_d = nc.dram_tensor("u_d", [128, DIC * T], BF16, kind="Internal").ap()
    zs_d = nc.dram_tensor("zs_d", [128, DIC * T], BF16, kind="Internal").ap()
    dt_d = nc.dram_tensor("dt_d", [128, DIC * T], F32, kind="Internal").ap()
    ba_d = nc.dram_tensor("ba_d", [128, DC * T], BF16, kind="Internal").ap()
    SCALE = 1.0 / 8.0
    TP4 = TGS + 4
    if True:
        xn = ctx.enter_context(nc.sbuf_tensor("xn", [128, DC * T], BF16))
        gsb = ctx.enter_context(nc.sbuf_tensor("gsb", [128, 16 * 8], F32))
        wg_sb = ctx.enter_context(nc.sbuf_tensor("k_wg", [128, DC * 8], BF16))
        colt = {}
        cmarks = []
        for nm, ncol in [("bqc", 8), ("bkc", 8), ("g1c", DC), ("be1c", DC),
                         ("g2c", DC), ("be2c", DC)]:
            colt[nm], m = load_cols(b, nm, dr, ncol)
            cmarks.append(m)
        brow = {}
        brow["bgrow"] = ctx.enter_context(nc.sbuf_tensor("k_bgrow", [1, 8], BF16))

        def gsmall(e):
            grp = b.gqq.begin(e)
            grp.tag(e.dma_start(out=brow["bgrow"][:], in_=dr["bgrow"]), 16)
            cmarks.append(grp.tag(e.dma_start(
                out=wg_sb[:].rearrange("p (kc m) -> p kc m", kc=DC),
                in_=dr["wg"].rearrange("(kc p) m -> p kc m", p=128)), 16))
        b.g(gsmall)
        K.barrier(b, extra=cmarks)

        xn_mark = emit_ln(b, xin, xn, K)

        phA = ctx.enter_context(ExitStack())
        bA = b.sub(phA)
        hstate = phA.enter_context(nc.sbuf_tensor("hstate", [128, DIC * NST], F32))
        asb = phA.enter_context(nc.sbuf_tensor("k_A", [128, DIC * NST], F32))
        bsb = phA.enter_context(nc.sbuf_tensor("bsb", [16, T], BF16))
        csb = phA.enter_context(nc.sbuf_tensor("csb", [16, T], BF16))
        sel16 = phA.enter_context(nc.sbuf_tensor("k_sel16", [16, 16 * 128], BF16))
        amarks = []
        for nm, ncol in [("convw", DIC * 4), ("convbc", DIC), ("dtbc", DIC),
                         ("alogc", DIC * NST), ("dcolc", DIC), ("fgbc", DC)]:
            colt[nm], m = load_cols(bA, nm, dr, ncol)
            amarks.append(m)
        s16m = {}

        def gsel(e):
            grp = b.gqq.begin(e)
            s16m["m"] = grp.tag(e.dma_start(out=sel16[:], in_=dr["sel16"]), 16)
        b.g(gsel)
        amks = {}

        def sA(e, w=list(amarks)):
            wait(e, *w)
            amks["exp"] = b.sca.tag(e.activation(asb[:], colt["alogc"][:], AF.Exp))
        b.s(sA)

        def vA(e):
            wait(e, amks["exp"], s16m["m"])
            e.drain()
            e.tensor_scalar_mul(asb[:], asb[:], -1.0)
            e.drain()
            amks["A"] = b.vec.tag(e.memset(hstate[:], 0.0))
        b.v(vA)

        fe_done = []
        with ExitStack() as phf:
            bf_ = b.sub(phf)
            prf = PsumRot(bf_, "prM", 4)
            pool_m = WPool(bf_, "pM", 8 * 512, 2)
            wsip = WStream(bf_, pool_m, dr["inproj"], D, 2 * DI, 512)
            xip = phf.enter_context(nc.sbuf_tensor("xip", [128, DIC * TP4], BF16))
            cvt = phf.enter_context(nc.sbuf_tensor("cvt", [128, DIC * TGS], F32))
            l1s = phf.enter_context(nc.sbuf_tensor("l1s", [128, DIC * TGS // 2], F32))
            dtraw = phf.enter_context(nc.sbuf_tensor("dtraw", [128, DIC * TGS], F32))
            xig = phf.enter_context(nc.sbuf_tensor("xig", [128, DIC * TGS], BF16))
            zst = phf.enter_context(nc.sbuf_tensor("zst", [128, DIC * TGS], BF16))
            dtin = phf.enter_context(nc.sbuf_tensor("dtin", [64, TGS], BF16))
            xproj_sb = phf.enter_context(nc.sbuf_tensor("k_xproj", [128, DIC * 96], BF16))
            dtproj_sb = phf.enter_context(nc.sbuf_tensor("k_dtproj", [64, DI], BF16))
            p_dt = phf.enter_context(nc.psum_tensor("p_dt", [64, TGS], F32))
            p_bb = phf.enter_context(nc.psum_tensor("p_bb", [16, TGS], F32))
            p_cc = phf.enter_context(nc.psum_tensor("p_cc", [16, TGS], F32))
            pm = {}

            def gproj(e):
                grp = b.gqq.begin(e)
                grp.tag(e.dma_start(
                    out=xproj_sb[:].rearrange("p (kc m) -> p kc m", kc=DIC),
                    in_=dr["xproj"].rearrange("(kc p) m -> p kc m", p=128)), 16)
                pm["w"] = grp.tag(e.dma_start(out=dtproj_sb[:], in_=dr["dtproj"]), 16)
            b.g(gproj)

            def vinit(e):
                wait(e, xn_mark)
                e.drain()
                pm["hal"] = b.vec.tag(e.memset(xip[:], 0.0))
            b.v(vinit)
            pm["sp"] = None
            pm["dtst"] = None

            for tg in range(NTG):
                fmk = {}
                halo_prev = pm["hal"]

                def evac_xz(mc, rid, ps, waits, halo_prev=halo_prev):
                    if mc < DIC:
                        out = {}

                        def vev(e):
                            wait(e, *waits, halo_prev)
                            e.drain()
                            out["m"] = b.vec.tag(e.tensor_copy(
                                xip[:, mc * TP4 + 4:(mc + 1) * TP4], ps[:]))
                        b.v(vev)
                        return out["m"]
                    em = b.sca_reserve()

                    def sev(e, udp=fmk.get("ud_prev")):
                        wait(e, *waits, udp)
                        e.activation(
                            zst[:, (mc - DIC) * TGS:(mc - DIC + 1) * TGS],
                            ps[:], AF.Silu).then_inc(b.sca.h, 1)
                    b.s(sev)
                    return em
                fmk["ud_prev"] = pm.get("ud")
                b.s_defer()
                xz_marks = fm_proj(bf_, wsip, DC, [tg],
                                   lambda kc, rid: xn[:, kc * T + rid * TGS:
                                                      kc * T + (rid + 1) * TGS],
                                   evac_xz, prf, 2 * DIC, extra_waits=[xn_mark])
                b.s_flush()

                def vconv(e, w=list(xz_marks), sp_prev=pm["sp"]):
                    wait(e, *w, sp_prev)
                    e.drain()
                    ins = None
                    for dcc in range(DIC):
                        base = dcc * TP4
                        o = cvt[:, dcc * TGS:(dcc + 1) * TGS]
                        e.tensor_scalar_mul(o, xip[:, base + 1: base + 1 + TGS],
                                            colt["convw"][:, dcc * 4:dcc * 4 + 1])
                        for j in range(1, 4):
                            e.drain()
                            ins = e.scalar_tensor_tensor(
                                o, xip[:, base + 1 + j: base + 1 + j + TGS],
                                colt["convw"][:, dcc * 4 + j: dcc * 4 + j + 1],
                                o, ALU.mult, ALU.add)
                        e.drain()
                    fmk["cv"] = b.vec.tag(ins)
                b.v(vconv)

                def sxi(e, ud_prev=pm.get("ud")):
                    wait(e, fmk["cv"], ud_prev)
                    ins = None
                    for dcc in range(DIC):
                        ins = e.activation(
                            xig[:, dcc * TGS:(dcc + 1) * TGS],
                            cvt[:, dcc * TGS:(dcc + 1) * TGS],
                            AF.Silu, bias=colt["convbc"][:, dcc:dcc + 1])
                    fmk["xi"] = b.sca.tag(ins)
                b.s(sxi)

                def vhalo(e):
                    wait(e, fmk["cv"])
                    e.drain()
                    ins = None
                    for dcc in range(DIC):
                        base = dcc * TP4
                        ins = e.tensor_copy(xip[:, base + 1:base + 4],
                                            xip[:, base + TGS + 1:base + TGS + 4])
                    pm["hal"] = b.vec.tag(ins)
                b.v(vhalo)

                def gu(e, tg=tg):
                    wait(e, fmk["xi"])
                    grp = b.gqq.begin(e)
                    for dcc in range(DIC):
                        grp.tag(e.dma_start(
                            out=u_d[:, dcc * T + tg * TGS: dcc * T + (tg + 1) * TGS],
                            in_=xig[:, dcc * TGS:(dcc + 1) * TGS]), 16)
                        pm["ud"] = grp.tag(e.dma_start(
                            out=zs_d[:, dcc * T + tg * TGS: dcc * T + (tg + 1) * TGS],
                            in_=zst[:, dcc * TGS:(dcc + 1) * TGS]), 16)
                b.g(gu)

                def tdbl(e, bc_prev=fmk.get("bc_prev")):
                    wait(e, fmk["xi"], pm.get("bc_done"), pm.get("dtin_done"),
                         pm["w"])
                    for kc in range(DIC):
                        e.matmul(p_dt[:], xproj_sb[:, kc * 96:kc * 96 + 64],
                                 xig[:, kc * TGS:(kc + 1) * TGS],
                                 start=(kc == 0), stop=(kc == DIC - 1))
                    for kc in range(DIC):
                        e.matmul(p_bb[:], xproj_sb[:, kc * 96 + 64:kc * 96 + 80],
                                 xig[:, kc * TGS:(kc + 1) * TGS],
                                 start=(kc == 0), stop=(kc == DIC - 1))
                    for kc in range(DIC):
                        ins = e.matmul(p_cc[:],
                                       xproj_sb[:, kc * 96 + 80:kc * 96 + 96],
                                       xig[:, kc * TGS:(kc + 1) * TGS],
                                       start=(kc == 0), stop=(kc == DIC - 1))
                    fmk["dbl"] = b.pe.tag(ins)
                b.t(tdbl)

                def vbc(e, tg=tg):
                    wait(e, fmk["dbl"])
                    e.drain()
                    e.tensor_copy(bsb[:, tg * TGS:(tg + 1) * TGS], p_bb[:])
                    e.drain()
                    ins = e.tensor_copy(csb[:, tg * TGS:(tg + 1) * TGS], p_cc[:])
                    pm["bc_done"] = b.vec.tag(ins)
                b.v(vbc)

                def sdtin(e):
                    wait(e, fmk["dbl"])
                    pm["dtin_done"] = b.sca.tag(
                        e.activation(dtin[:], p_dt[:], AF.Identity))
                b.s(sdtin)
                fmk["dtin"] = pm["dtin_done"]

                sp_prev = pm["sp"]
                dtst_prev = pm["dtst"]
                dt_free0 = [prf.free[0], prf.free[1]]
                for mc in range(DIC):
                    def tdt(e, mc=mc):
                        w = [fmk["dtin"], pm["w"]]
                        if mc >= 2:
                            w.append(fmk[f"spv{mc - 2}"])
                        else:
                            w.append(dt_free0[mc])
                            w.append(sp_prev)
                        wait(e, *w)
                        fmk[f"dtmm{mc}"] = b.pe.tag(e.matmul(
                            prf.tiles[mc % 2][:],
                            dtproj_sb[:, mc * 128:(mc + 1) * 128],
                            dtin[:], start=True, stop=True))
                    b.t(tdt)

                    def sraw(e, mc=mc):
                        wait(e, fmk[f"dtmm{mc}"], sp_prev, dtst_prev)
                        fmk[f"spv{mc}"] = b.sca.tag(e.activation(
                            dtraw[:, mc * TGS:(mc + 1) * TGS],
                            prf.tiles[mc % 2][:], AF.Identity,
                            bias=colt["dtbc"][:, mc:mc + 1]))
                    b.s(sraw)
                prf.set_free(0, fmk[f"spv{DIC - 2}"])
                prf.set_free(1, fmk[f"spv{DIC - 1}"])

                HL = DIC * TGS // 2
                for hf in range(2):
                    slh = slice(hf * HL, (hf + 1) * HL)

                    def ssp(e, slh=slh, dtst_prev=pm["dtst"], sp_prev2=pm["sp"]):
                        wait(e, fmk[f"spv{DIC - 1}"], dtst_prev, sp_prev2)
                        e.activation(cvt[:, slh], dtraw[:, slh], AF.Abs)
                        e.drain()
                        e.activation(l1s[:], cvt[:, slh], AF.Exp, scale=-1.0)
                        e.drain()
                        fmk["lnv"] = b.sca.tag(e.activation(l1s[:], l1s[:], AF.Ln,
                                                            bias=1.0))
                    b.s(ssp)

                    def vsp2(e, slh=slh, dtst_prev=pm["dtst"]):
                        wait(e, fmk["lnv"], dtst_prev)
                        e.drain()
                        e.tensor_relu(cvt[:, slh], dtraw[:, slh])
                        e.drain()
                        ins = e.tensor_tensor(dtraw[:, slh], cvt[:, slh], l1s[:],
                                              ALU.add)
                        pm["sp"] = b.vec.tag(ins)
                    b.v(vsp2)

                def ydt(e, tg=tg):
                    wait(e, pm["sp"])
                    grp = b.yqq.begin(e)
                    for dcc in range(DIC):
                        pm["dtst"] = grp.tag(e.dma_start(
                            out=dt_d[:, dcc * T + tg * TGS: dcc * T + (tg + 1) * TGS],
                            in_=dtraw[:, dcc * TGS:(dcc + 1) * TGS]), 16)
                b.y(ydt)

            fe_done = [pm["dtst"], pm["ud"], pm["hal"], pm["sp"], pm["bc_done"],
                       pm["dtin_done"], amks["A"]]

        y_d = nc.dram_tensor("y_d", [128, DIC * T], BF16, kind="Internal").ap()
        brep_d = nc.dram_tensor("brep_d", [128, NST * T], BF16, kind="Internal").ap()
        crep_d = nc.dram_tensor("crep_d", [128, NST * T], BF16, kind="Internal").ap()
        with ExitStack() as phs:
            bs_ = b.sub(phs)
            with ExitStack() as phb:
                brs = [phb.enter_context(nc.sbuf_tensor(f"brs{i}", [128, T], BF16))
                       for i in range(2)]
                p_bc = [phb.enter_context(nc.psum_tensor(f"p_bc{i}", [128, TGS], F32))
                        for i in range(2)]
                sbm = {"ev0": None, "ev1": None, "st0": None, "st1": None}
                for n in range(NST):
                    for src, dst_d in ((bsb, brep_d), (csb, crep_d)):
                        si = 0 if src is bsb else 1
                        for g in range(NTG):
                            pi = g % 2

                            def tbc(e, n=n, g=g, pi=pi, src=src,
                                    w=(sbm[f"ev{pi}"], fe_done if n == 0 and g == 0
                                       and src is bsb else None)):
                                wait(e, w[0], w[1], s16m["m"])
                                sbm["mm"] = b.pe.tag(e.matmul(
                                    p_bc[pi][:], sel16[:, n * 128:(n + 1) * 128],
                                    src[:, g * TGS:(g + 1) * TGS],
                                    start=True, stop=True))
                            b.t(tbc)

                            def sev(e, g=g, pi=pi, si=si,
                                    w=(sbm["mm"], sbm[f"st{si}"])):
                                wait(e, *[m for m in w if m])
                                sbm[f"ev{pi}"] = b.sca.tag(e.activation(
                                    brs[si][:, g * TGS:(g + 1) * TGS], p_bc[pi][:],
                                    AF.Identity))
                            b.s(sev)
                        evm = (sbm["ev0"], sbm["ev1"])

                        def yst(e, n=n, si=si, dst_d=dst_d, w=evm):
                            wait(e, *[m for m in w if m])
                            grp = b.yqq.begin(e)
                            sbm[f"st{si}"] = grp.tag(e.dma_start(
                                out=dst_d[:, n * T:(n + 1) * T], in_=brs[si][:]), 16)
                        b.y(yst)
                sb_done = (sbm["st0"], sbm["st1"])

            QD = 2
            dtq = phs.enter_context(nc.sbuf_tensor("dtq", [128, QD * T], F32))
            uq = phs.enter_context(nc.sbuf_tensor("uq", [128, QD * T], BF16))
            zq = phs.enter_context(nc.sbuf_tensor("zq", [128, QD * T], BF16))
            dtu = [phs.enter_context(nc.sbuf_tensor(f"dtu{j}", [128, T], F32))
                   for j in range(QD)]
            dAs = [phs.enter_context(nc.sbuf_tensor(f"dA{i}", [128, T], F32))
                   for i in range(4)]
            d1s = phs.enter_context(nc.sbuf_tensor("d1s", [128, T], F32))
            scs = phs.enter_context(nc.sbuf_tensor("scs", [128, T], F32))
            hCs = [phs.enter_context(nc.sbuf_tensor(f"hC{i}", [128, T], F32))
                   for i in range(4)]
            bsl = [phs.enter_context(nc.sbuf_tensor(f"bsl{i}", [128, T], BF16))
                   for i in range(2)]
            csl = [phs.enter_context(nc.sbuf_tensor(f"csl{i}", [128, T], BF16))
                   for i in range(2)]
            y_bf = [phs.enter_context(nc.sbuf_tensor(f"y_bf{j}", [128, T], BF16))
                    for j in range(QD)]
            zcol = phs.enter_context(nc.sbuf_tensor("zcol", [128, 1], F32))
            p_y = [phs.enter_context(nc.psum_tensor(f"p_y{i}", [128, TGS], F32))
                   for i in range(8)]
            smk = {}

            def vz(e):
                e.drain()
                smk["zc"] = b.vec.tag(e.memset(zcol[:], 0.0))
            b.v(vz)

            scm = {}
            dam = {}
            yam = {}
            b.s_defer()
            NQ = DIC // QD
            for q in range(NQ):
                if q == 0:
                    w_prev = list(fe_done) + [sb_done[0], sb_done[1]]
                else:
                    w_prev = [m for m in
                              ([dam.get((q * QD - 1 - j, NST - 1)) for j in range(QD)]
                               + [smk.get("vyf"), smk.get("dtu")]) if m]

                def yql(e, q=q, w=tuple(w_prev)):
                    wait(e, *[m for m in w if m])
                    grp = b.yqq.begin(e)
                    for j in range(QD):
                        dcc = q * QD + j
                        smk["dtq"] = grp.tag(e.dma_start(
                            out=dtq[:, j * T:(j + 1) * T],
                            in_=dt_d[:, dcc * T:(dcc + 1) * T]), 16)
                b.y(yql)

                def gql(e, q=q, w=tuple(w_prev)):
                    wait(e, *[m for m in w if m])
                    grp = b.gqq.begin(e)
                    for j in range(QD):
                        dcc = q * QD + j
                        grp.tag(e.dma_start(
                            out=uq[:, j * T:(j + 1) * T],
                            in_=u_d[:, dcc * T:(dcc + 1) * T]), 16)
                        smk["uzq"] = grp.tag(e.dma_start(
                            out=zq[:, j * T:(j + 1) * T],
                            in_=zs_d[:, dcc * T:(dcc + 1) * T]), 16)
                b.g(gql)

                for j in range(QD):
                    def vdtu(e, j=j, w=(smk["dtq"], smk["uzq"])):
                        wait(e, *w)
                        e.drain()
                        smk["dtu"] = b.vec.tag(e.tensor_tensor(
                            dtu[j][:], dtq[:, j * T:(j + 1) * T],
                            uq[:, j * T:(j + 1) * T], ALU.mult))
                    b.v(vdtu)
                dtu_mark = smk["dtu"]

                for n in range(NST):
                    sl = n % 2
                    pc_j = q * QD + QD - 1
                    prev_cons = scm.get((pc_j, n - 2)) if n >= 2 else \
                        scm.get((pc_j - QD, NST - 2 + n))

                    def ybc(e, n=n, sl=sl,
                            w=(prev_cons, sb_done[0], sb_done[1])):
                        wait(e, *[m for m in w if m])
                        grp = b.yqq.begin(e)
                        grp.tag(e.dma_start(
                            out=bsl[sl][:], in_=brep_d[:, n * T:(n + 1) * T]), 16)
                        smk["bc_ld"] = grp.tag(e.dma_start(
                            out=csl[sl][:], in_=crep_d[:, n * T:(n + 1) * T]), 16)
                    b.y(ybc)

                    for j in range(QD):
                        dci = q * QD + j
                        col = dci * NST + n
                        dsl = 2 * (n % 2) + j
                        da_prev = scm.get((dci, n - 2)) if n >= 2 else \
                            scm.get((dci - QD, NST - 2 + n))
                        dam[(dci, n)] = b.sca_reserve()

                        def sda(e, j=j, dsl=dsl, col=col,
                                w=(da_prev, smk["dtq"])):
                            wait(e, *[m for m in w if m])
                            e.activation(
                                dAs[dsl][:], dtq[:, j * T:(j + 1) * T], AF.Exp,
                                scale=asb[:, col:col + 1]).then_inc(b.sca.h, 1)
                        b.s(sda)

                        hc_prev = yam.get((dci, n - 2)) if n >= 2 else \
                            yam.get((dci - QD, NST - 2 + n))

                        def vsc(e, j=j, sl=sl, dsl=dsl,
                                w=(smk["bc_ld"], dam[(dci, n)],
                                   dtu_mark, smk["zc"], hc_prev)):
                            wait(e, *[m for m in w if m])
                            e.drain()
                            e.tensor_tensor(d1s[:], dtu[j][:], bsl[sl][:], ALU.mult)
                            e.drain()
                            e.tensor_tensor_scan(scs[:], dAs[dsl][:], d1s[:],
                                                 zcol[:, 0:1], ALU.mult, ALU.add)
                            e.drain()
                            scm[(dci, n)] = b.vec.tag(e.tensor_tensor(
                                hCs[dsl][:], scs[:], csl[sl][:], ALU.mult))
                        b.v(vsc)

                        def tya(e, j=j, dsl=dsl, n=n,
                                w=(scm[(dci, n)],
                                   smk.get("vyf") if n == 0 else None)):
                            wait(e, *[m for m in w if m])
                            for g in range(NTG):
                                ins = e.matmul(
                                    p_y[j * 4 + g][:], K.ident_f[:],
                                    hCs[dsl][:, g * TGS:(g + 1) * TGS],
                                    start=(n == 0), stop=(n == NST - 1))
                            yam[(dci, n)] = b.pe.tag(ins)
                        b.t(tya)

                for j in range(QD):
                    dci = q * QD + j

                    def vyf(e, j=j, dci=dci, w=(yam[(dci, NST - 1)],
                                                smk.get(f"yst{j}"))):
                        wait(e, *[m for m in w if m])
                        e.drain()
                        ins = None
                        for g in range(NTG):
                            gs = slice(g * TGS, (g + 1) * TGS)
                            js = slice(j * T + g * TGS, j * T + (g + 1) * TGS)
                            e.scalar_tensor_tensor(
                                scs[:, gs], uq[:, js],
                                colt["dcolc"][:, dci:dci + 1],
                                p_y[j * 4 + g][:], ALU.mult, ALU.add)
                            e.drain()
                            ins = e.tensor_tensor(y_bf[j][:, gs], scs[:, gs],
                                                  zq[:, js], ALU.mult)
                            e.drain()
                        smk["vyf"] = b.vec.tag(ins)
                    b.v(vyf)

                    def yyst(e, j=j, dci=dci, w=(smk["vyf"],)):
                        wait(e, *w)
                        grp = b.yqq.begin(e)
                        smk[f"yst{j}"] = grp.tag(e.dma_start(
                            out=y_d[:, dci * T:(dci + 1) * T], in_=y_bf[j][:]), 16)
                    b.y(yyst)
            b.s_flush()
            scan_done = [smk["yst0"], smk["yst1"], smk["vyf"],
                         dam[(DIC - 1, NST - 1)], dam[(DIC - 2, NST - 1)],
                         smk["dtu"]]

        with ExitStack() as php:
            bs_ = b.sub(php)
            prs = PsumRot(bs_, "prS", 4, init_free=list(scan_done))
            pool_s = WPool(bs_, "pS", 8 * 512, 2, init_free=list(scan_done))
            wsop = WStream(bs_, pool_s, dr["outproj"], DI, D, 256)
            wsfg = WStream(bs_, pool_s, dr["fgw"], D, D, 256)
            y_tg = php.enter_context(nc.sbuf_tensor("y_tg", [128, DIC * TGS], BF16))
            fg_sb = php.enter_context(nc.sbuf_tensor("fg_sb", [128, DC * TGS], BF16))
            ba_sb = php.enter_context(nc.sbuf_tensor("ba_sb", [128, DC * TGS], BF16))
            scr = php.enter_context(nc.sbuf_tensor("scr", [128, TGS], F32))
            smk = {}

            for tg in range(NTG):
                def yyl(e, tg=tg, w=(scan_done[0], scan_done[1],
                                     smk.get("op_done"))):
                    wait(e, *[m for m in w if m])
                    grp = b.yqq.begin(e)
                    for dcc in range(DIC):
                        smk["y_ld"] = grp.tag(e.dma_start(
                            out=y_tg[:, dcc * TGS:(dcc + 1) * TGS],
                            in_=y_d[:, dcc * T + tg * TGS: dcc * T + (tg + 1) * TGS]), 16)
                b.y(yyl)

                def evac_fg(mc, rid, ps, waits):
                    em = b.sca_reserve()

                    def sev(e):
                        wait(e, *waits)
                        e.activation(
                            fg_sb[:, mc * TGS:(mc + 1) * TGS], ps[:], AF.Sigmoid,
                            bias=colt["fgbc"][:, mc:mc + 1]).then_inc(b.sca.h, 1)
                    b.s(sev)
                    return em
                b.s_defer()
                fgm = fm_proj(bs_, wsfg, DC, [tg],
                              lambda kc, rid: xn[:, kc * T + rid * TGS:
                                                 kc * T + (rid + 1) * TGS],
                              evac_fg, prs, DC,
                              extra_waits=[m for m in (smk.get("ba_st"),) if m])
                b.s_flush()

                def evac_op(mc, rid, ps, waits, tg=tg, fgm=fgm):
                    out = {}

                    def vev(e, yl=smk["y_ld"]):
                        wait(e, *waits, *fgm, yl)
                        e.drain()
                        e.tensor_tensor(scr[:],
                                        xn[:, mc * T + tg * TGS: mc * T + (tg + 1) * TGS],
                                        ps[:], ALU.subtract)
                        e.drain()
                        e.tensor_tensor(scr[:],
                                        fg_sb[:, mc * TGS:(mc + 1) * TGS],
                                        scr[:], ALU.mult)
                        e.drain()
                        out["m"] = b.vec.tag(e.tensor_tensor(
                            ba_sb[:, mc * TGS:(mc + 1) * TGS], ps[:],
                            scr[:], ALU.add))
                    b.v(vev)
                    smk["op_done"] = out["m"]
                    return out["m"]
                opm = fm_proj(bs_, wsop, DIC, [tg],
                              lambda kc, rid: y_tg[:, kc * TGS:(kc + 1) * TGS],
                              evac_op, prs, DC, extra_waits=[smk["y_ld"]])

                def gba(e, tg=tg, w=list(opm)):
                    wait(e, *w)
                    grp = b.gqq.begin(e)
                    for mc in range(DC):
                        smk["ba_st"] = grp.tag(e.dma_start(
                            out=ba_d[:, mc * T + tg * TGS: mc * T + (tg + 1) * TGS],
                            in_=ba_sb[:, mc * TGS:(mc + 1) * TGS]), 16)
                b.g(gba)
            ba_done = smk["ba_st"]

        if upto == "mamba":
            def gdbg(e):
                wait(e, ba_done)
                grp = b.gqq.begin(e)
                for c in range(DC):
                    m = grp.tag(e.dma_start(out=xout[c * 128:(c + 1) * 128, :],
                                            in_=ba_d[:, c * T:(c + 1) * T]), 16)
                wait(e, m)
            b.g(gdbg)
            return nc

        phA.close()

        x1_d = nc.dram_tensor("x1_d", [128, DC * T], BF16, kind="Internal").ap()
        phBB = ctx.enter_context(ExitStack())
        ctx_sb = phBB.enter_context(nc.sbuf_tensor("ctx_sb", [128, DC * T], BF16))
        with ExitStack() as ph:
            q_sb = ph.enter_context(nc.sbuf_tensor("q_sb", [128, DC * T], BF16))
            k_sb = ph.enter_context(nc.sbuf_tensor("k_sb", [128, DC * T], BF16))
            v_sb = ph.enter_context(nc.sbuf_tensor("v_sb", [128, 16 * VSTRIDE], BF16))
            bq = b.sub(ph)
            qkv_marks = []
            with ExitStack() as phw:
                bw = b.sub(phw)
                pr = PsumRot(bw, "prA", 4)
                pool_a = WPool(bw, "pA", 4 * 1024, 2, init_free=ba_done)
                brow["bvrow"] = phw.enter_context(
                    nc.sbuf_tensor("k_bvrow", [1, D], BF16))
                bvm = {}

                def gbv(e):
                    wait(e, ba_done)
                    grp = b.gqq.begin(e)
                    bvm["m"] = grp.tag(e.dma_start(out=brow["bvrow"][:],
                                                   in_=dr["bvrow"]), 16)
                b.g(gbv)

                for wnm, dst, bc in [("wq", q_sb, "bqc"), ("wk", k_sb, "bkc")]:
                    ws = WStream(bw, pool_a, dr[wnm], D, D, 512)

                    def evac(mc, rid, ps, waits, dst=dst, bc=bc):
                        out = {}

                        def sev(e):
                            wait(e, *waits)
                            out["m"] = b.sca.tag(e.activation(
                                dst[:, mc * T + rid * TGS: mc * T + (rid + 1) * TGS],
                                ps[:], AF.Identity, bias=colt[bc][:, mc:mc + 1]))
                        b.s(sev)
                        return out["m"]
                    qkv_marks += fm_proj(bw, ws, DC, list(range(NTG)),
                                         lambda kc, rid: xn[:, kc * T + rid * TGS:
                                                            kc * T + (rid + 1) * TGS],
                                         evac, pr, DC,
                                         extra_waits=[xn_mark, ba_done])

                vms = {}

                def vmemset(e):
                    wait(e, xn_mark, ba_done)
                    e.drain()
                    vms["m"] = b.vec.tag(e.memset(v_sb[:], 1.0))
                b.v(vmemset)

                wsv = WStream(bw, pool_a, dr["wv"], D, D, 512)

                def vevac(tc, fb, ps, waits):
                    out = {}

                    def vev(e):
                        wait(e, *waits, vms["m"])
                        e.drain()
                        for j in range(8):
                            h = fb * 8 + j
                            out["m"] = b.vec.tag(e.tensor_copy(
                                v_sb[:, tc * VSTRIDE + h * 65: tc * VSTRIDE + h * 65 + 64],
                                ps[:, j * 64:(j + 1) * 64]))
                    b.v(vev)
                    return out["m"]
                qkv_marks += tm_proj(bw, wsv, DC,
                                     lambda kc, tc: xn[:, kc * T + tc * 128:
                                                       kc * T + (tc + 1) * 128],
                                     vevac, pr, 16,
                                     extra_waits=[xn_mark, bvm["m"]],
                                     bias_row=lambda fb: brow["bvrow"][:, fb * 512:(fb + 1) * 512],
                                     ones_row=K.ones_row)

            ctx_mark = emit_attention_mh(bq, K, q_sb, k_sb, v_sb, ctx_sb,
                                         SCALE, False, qkv_marks, "mha")

        with ExitStack() as ph3:
            b3 = b.sub(ph3)
            pr3 = PsumRot(b3, "prC", 4)
            pool_c = WPool(b3, "pC", 8 * 512, 2, init_free=ctx_mark)
            ao_sb = ph3.enter_context(nc.sbuf_tensor("ao_sb", [128, 16 * D], BF16))
            lsc = ph3.enter_context(nc.sbuf_tensor("lsc", [128, D + 8], F32))
            x1st = ph3.enter_context(nc.sbuf_tensor("x1st", [128, DC * 128], BF16))
            brow["borow"] = ph3.enter_context(nc.sbuf_tensor("k_borow", [1, D], BF16))
            bom = {}

            def gbo(e):
                wait(e, ctx_mark)
                grp = b.gqq.begin(e)
                bom["m"] = grp.tag(e.dma_start(out=brow["borow"][:], in_=dr["borow"]), 16)
            b.g(gbo)

            wso = WStream(b3, pool_c, dr["wo"], D, D, 512)

            def evac_ao(tc, fb, ps, waits):
                out = {}

                def vev(e):
                    wait(e, *waits)
                    e.drain()
                    out["m"] = b.vec.tag(e.tensor_copy(
                        ao_sb[:, tc * D + fb * 512: tc * D + (fb + 1) * 512],
                        ps[:, 0:512]))
                b.v(vev)
                return out["m"]
            aom = tm_proj(b3, wso, DC,
                          lambda kc, tc: ctx_sb[:, kc * T + tc * 128:
                                                kc * T + (tc + 1) * 128],
                          evac_ao, pr3, 16, extra_waits=[ctx_mark, bom["m"]],
                          bias_row=lambda fb: brow["borow"][:, fb * 512:(fb + 1) * 512],
                          ones_row=K.ones_row)

            lmk = {"x1": None}
            mean = lsc[:, D:D + 1]
            s2c_ = lsc[:, D + 1:D + 2]
            rstd = lsc[:, D + 2:D + 3]

            def scrx():
                return lsc[:, 0:128]

            for tc in range(16):
                def vst1(e, tc=tc, w=list(aom)):
                    wait(e, *w, lmk["x1"])
                    e.drain()
                    e.reduce_sum(mean, ao_sb[:, tc * D:(tc + 1) * D], AX.X)
                    e.drain()
                    lmk["mean"] = b.vec.tag(e.tensor_scalar_mul(mean, mean, 1.0 / D))
                b.v(vst1)

                def sst1(e, tc=tc):
                    wait(e, lmk["mean"])
                    lmk["sq"] = b.sca.tag(e.activation(
                        lsc[:, 0:D], ao_sb[:, tc * D:(tc + 1) * D], AF.Square,
                        accum_out=s2c_))
                b.s(sst1)

                def vst2(e, tc=tc):
                    wait(e, lmk["sq"])
                    e.drain()
                    e.tensor_scalar_mul(s2c_, s2c_, 1.0 / D)
                    e.drain()
                    e.tensor_tensor(lsc[:, D + 3:D + 4], mean, mean, ALU.mult)
                    e.drain()
                    lmk["var"] = b.vec.tag(e.tensor_tensor(
                        s2c_, s2c_, lsc[:, D + 3:D + 4], ALU.subtract))
                b.v(vst2)

                def sst2(e):
                    wait(e, lmk["var"])
                    e.activation(rstd, s2c_, AF.Ln, bias=EPS_LN)
                    e.drain()
                    lmk["rstd"] = b.sca.tag(e.activation(rstd, rstd, AF.Exp,
                                                         scale=-0.5))
                b.s(sst2)

                def vnorm(e, tc=tc):
                    wait(e, lmk["rstd"])
                    e.drain()
                    lmk["nrm"] = b.vec.tag(e.tensor_scalar(
                        lsc[:, 0:D], ao_sb[:, tc * D:(tc + 1) * D], mean, rstd,
                        ALU.subtract, ALU.mult))
                b.v(vnorm)

                for mc in range(DC):
                    pidx, psx, ps_free = pr3.get()

                    def ttp(e, tc=tc, mc=mc, psx=psx, ps_free=ps_free):
                        wait(e, lmk["nrm"], ps_free)
                        lmk["tp"] = b.pe.tag(e.matmul(
                            psx[:, 0:128], lsc[:, mc * 128:(mc + 1) * 128],
                            K.ident_f[:], start=True, stop=True))
                    b3.t(ttp)

                    def vx1(e, tc=tc, mc=mc, psx=psx):
                        wait(e, lmk["tp"], lmk.get("x1dma") if mc == 0 else None)
                        e.drain()
                        e.tensor_scalar(scrx(), psx[:, 0:128],
                                        colt["g1c"][:, mc:mc + 1],
                                        colt["be1c"][:, mc:mc + 1],
                                        ALU.mult, ALU.add)
                        e.drain()
                        ins = e.tensor_tensor(
                            x1st[:, mc * 128:(mc + 1) * 128],
                            xn[:, mc * T + tc * 128: mc * T + (tc + 1) * 128],
                            scrx(), ALU.add)
                        m = b.vec.tag(ins)
                        pr3.set_free(pidx, m)
                        lmk["x1"] = m
                    b3.v(vx1)

                def gx1(e, tc=tc):
                    wait(e, lmk["x1"])
                    grp = b.gqq.begin(e)
                    for mc in range(DC):
                        lmk["x1dma"] = grp.tag(e.dma_start(
                            out=x1_d[:, mc * T + tc * 128: mc * T + (tc + 1) * 128],
                            in_=x1st[:, mc * 128:(mc + 1) * 128]), 16)
                b3.g(gx1)
            x1_mark = lmk["x1dma"]

        if upto == "x1":
            def gdbg(e):
                wait(e, x1_mark)
                grp = b.gqq.begin(e)
                for c in range(DC):
                    m = grp.tag(e.dma_start(out=xout[c * 128:(c + 1) * 128, :],
                                            in_=x1_d[:, c * T:(c + 1) * T]), 16)
                wait(e, m)
            b.g(gdbg)
            return nc

        phBB.close()

        with ExitStack() as ph2:
            b2 = b.sub(ph2)
            x1_sb = ph2.enter_context(nc.sbuf_tensor("x1_sb", [128, DC * T], BF16))
            x1m = {}

            def gx1l(e):
                wait(e, x1_mark)
                grp = b.gqq.begin(e)
                for c in range(DC):
                    x1m["m"] = grp.tag(e.dma_start(
                        out=x1_sb[:, c * T:(c + 1) * T],
                        in_=x1_d[:, c * T:(c + 1) * T]), 16)
            b2.g(gx1l)
            xtg = ph2.enter_context(nc.sbuf_tensor("xtg", [128, DC * TGS], F32))
            acc = ph2.enter_context(nc.sbuf_tensor("acc", [128, 4 * 1024], F32))
            bax = ph2.enter_context(nc.sbuf_tensor("bax", [128, DC * TGS], BF16))
            lsc2 = ph2.enter_context(nc.sbuf_tensor("lsc2", [128, D + 8], F32))
            pr2 = PsumRot(b2, "prB", 4)
            pool_f = WPool(b2, "pF", 8 * 1024, 2, init_free=x1_mark)
            brow["bexprow"] = ph2.enter_context(
                nc.sbuf_tensor("k_bexprow", [1, 8 * D], BF16))
            bxm = {}

            def gbx(e):
                wait(e, x1_mark)
                grp = b.gqq.begin(e)
                bxm["m"] = grp.tag(e.dma_start(out=brow["bexprow"][:],
                                               in_=dr["bexprow"]), 16)
            b.g(gbx)
            wse = [WStream(b2, pool_f, dr["we"][e * D:(e + 1) * D, :], D, D, 512)
                   for e in range(8)]

            gate_mark = emit_gates_top2(b2, K, x1_sb, wg_sb, brow["bgrow"], gsb,
                                        None, [x1_mark, x1m["m"]], pr2)

            store_mark = x1_mark
            mean2 = lsc2[:, D:D + 1]
            s2c2 = lsc2[:, D + 1:D + 2]
            rstd2 = lsc2[:, D + 2:D + 3]
            for tg in range(NTG):
                mks = {}

                def yload(e, tg=tg, prev=store_mark, gm=gate_mark):
                    wait(e, prev, gm)
                    grp = b.yqq.begin(e)
                    for c in range(DC):
                        mks["xl"] = grp.tag(e.dma_start(
                            out=xtg[:, c * TGS:(c + 1) * TGS],
                            in_=xin[c * 128:(c + 1) * 128,
                                        tg * TGS:(tg + 1) * TGS]), 16)
                b2.y(yload)

                def gba2(e, tg=tg, prev=store_mark):
                    wait(e, prev, ba_done)
                    grp = b.gqq.begin(e)
                    for c in range(DC):
                        mks["bal"] = grp.tag(e.dma_start(
                            out=bax[:, c * TGS:(c + 1) * TGS],
                            in_=ba_d[:, c * T + tg * TGS: c * T + (tg + 1) * TGS]), 16)
                b2.g(gba2)

                def vres0(e, tg=tg):
                    wait(e, mks["xl"], mks["bal"], x1m["m"])
                    e.drain()
                    for c in range(DC):
                        e.tensor_tensor(xtg[:, c * TGS:(c + 1) * TGS],
                                        xtg[:, c * TGS:(c + 1) * TGS],
                                        bax[:, c * TGS:(c + 1) * TGS], ALU.add)
                        e.drain()
                        ins = e.tensor_tensor(
                            xtg[:, c * TGS:(c + 1) * TGS],
                            xtg[:, c * TGS:(c + 1) * TGS],
                            x1_sb[:, c * T + tg * TGS: c * T + (tg + 1) * TGS],
                            ALU.add)
                        e.drain()
                    mks["res0"] = b.vec.tag(ins)
                b2.v(vres0)

                def vclr(e):
                    wait(e, mks["res0"])
                    e.drain()
                    mks["clr"] = b.vec.tag(e.memset(acc[:], 0.0))
                b2.v(vclr)

                for ee in range(8):
                    def evac_e(tc, fb, ps, waits, ee=ee, tg=tg):
                        out = {}

                        def vev(e):
                            wait(e, *waits, mks["clr"])
                            e.drain()
                            out["m"] = b.vec.tag(e.scalar_tensor_tensor(
                                acc[:, tc * 1024 + fb * 512: tc * 1024 + (fb + 1) * 512],
                                ps[:, 0:512],
                                gsb[:, (tg * 4 + tc) * 8 + ee:
                                    (tg * 4 + tc) * 8 + ee + 1],
                                acc[:, tc * 1024 + fb * 512: tc * 1024 + (fb + 1) * 512],
                                ALU.mult, ALU.add))
                        b2.v(vev)
                        return out["m"]
                    omarks = tm_proj(b2, wse[ee], DC,
                                     lambda kc, tc: x1_sb[:, kc * T + (tg * 4 + tc) * 128:
                                                          kc * T + (tg * 4 + tc + 1) * 128],
                                     evac_e, pr2, 4,
                                     extra_waits=[bxm["m"]],
                                     bias_row=lambda fb, ee=ee:
                                         brow["bexprow"][:, ee * D + fb * 512:
                                                         ee * D + (fb + 1) * 512],
                                     ones_row=K.ones_row)
                    mks["moe"] = omarks[-1]

                res_marks = []
                for tc in range(4):
                    def vl1(e, tc=tc, w=mks["moe"]):
                        wait(e, w)
                        e.drain()
                        e.reduce_sum(mean2, acc[:, tc * 1024:(tc + 1) * 1024], AX.X)
                        e.drain()
                        mks["mean2"] = b.vec.tag(e.tensor_scalar_mul(mean2, mean2,
                                                                     1.0 / D))
                    b2.v(vl1)

                    def sl1(e, tc=tc):
                        wait(e, mks["mean2"])
                        mks["sq2"] = b.sca.tag(e.activation(
                            lsc2[:, 0:D], acc[:, tc * 1024:(tc + 1) * 1024],
                            AF.Square, accum_out=s2c2))
                    b2.s(sl1)

                    def vl2(e):
                        wait(e, mks["sq2"])
                        e.drain()
                        e.tensor_scalar_mul(s2c2, s2c2, 1.0 / D)
                        e.drain()
                        e.tensor_tensor(lsc2[:, D + 3:D + 4], mean2, mean2, ALU.mult)
                        e.drain()
                        mks["var2"] = b.vec.tag(e.tensor_tensor(
                            s2c2, s2c2, lsc2[:, D + 3:D + 4], ALU.subtract))
                    b2.v(vl2)

                    def sl2(e):
                        wait(e, mks["var2"])
                        e.activation(rstd2, s2c2, AF.Ln, bias=EPS_LN)
                        e.drain()
                        mks["rstd2"] = b.sca.tag(e.activation(rstd2, rstd2, AF.Exp,
                                                              scale=-0.5))
                    b2.s(sl2)

                    def vn2(e, tc=tc):
                        wait(e, mks["rstd2"])
                        e.drain()
                        mks["nrm2"] = b.vec.tag(e.tensor_scalar(
                            lsc2[:, 0:D], acc[:, tc * 1024:(tc + 1) * 1024],
                            mean2, rstd2, ALU.subtract, ALU.mult))
                    b2.v(vn2)

                    for mc in range(DC):
                        pidx, psx, ps_free = pr2.get()

                        def ttp(e, mc=mc, psx=psx, ps_free=ps_free):
                            wait(e, mks["nrm2"], ps_free)
                            mks["tp2"] = b.pe.tag(e.matmul(
                                psx[:, 0:128], lsc2[:, mc * 128:(mc + 1) * 128],
                                K.ident_f[:], start=True, stop=True))
                        b2.t(ttp)

                        def vfin(e, tc=tc, mc=mc, psx=psx, pidx=pidx):
                            wait(e, mks["tp2"], mks["res0"])
                            e.drain()
                            e.scalar_tensor_tensor(
                                xtg[:, mc * TGS + tc * 128: mc * TGS + (tc + 1) * 128],
                                psx[:, 0:128], colt["g2c"][:, mc:mc + 1],
                                xtg[:, mc * TGS + tc * 128: mc * TGS + (tc + 1) * 128],
                                ALU.mult, ALU.add)
                            e.drain()
                            ins = e.tensor_scalar_add(
                                xtg[:, mc * TGS + tc * 128: mc * TGS + (tc + 1) * 128],
                                xtg[:, mc * TGS + tc * 128: mc * TGS + (tc + 1) * 128],
                                colt["be2c"][:, mc:mc + 1])
                            e.drain()
                            m = b.vec.tag(ins)
                            pr2.set_free(pidx, m)
                            vfin.mark = m
                        b2.v(vfin)
                        res_marks.append(vfin.mark)

                def ystore(e, tg=tg, rm=list(res_marks)):
                    wait(e, rm[-1])
                    grp = b.yqq.begin(e)
                    for c in range(DC):
                        ystore.mark = grp.tag(e.dma_start(
                            out=xout[c * 128:(c + 1) * 128, tg * TGS:(tg + 1) * TGS],
                            in_=xtg[:, c * TGS:(c + 1) * TGS]), 16)
                b2.y(ystore)
                store_mark = ystore.mark

    return store_mark


def build_block0(upto="full"):
    nc = bass.Bass("TRN2", target_bir_lowering=False)
    dr = {}
    xin = nc.dram_tensor("x", [D, T], F32, kind="ExternalInput").ap()
    for nm, shape in B0_SHAPES:
        dr[nm] = nc.dram_tensor(nm, shape, F32, kind="ExternalInput").ap()
    cs = make_consts()
    for nm in CONST_NAMES:
        dr[nm] = nc.dram_tensor(nm, list(cs[nm].shape), F32, kind="ExternalInput").ap()
    xout = nc.dram_tensor("xout", [D, T], F32, kind="ExternalOutput").ap()
    with ExitStack() as ctx:
        blk = ctx.enter_context(nc.Block())
        b = B(nc, ctx, blk)
        K = ConstTiles(b, dr)
        with ExitStack() as bctx:
            fin = emit_block0(b.sub(bctx), bctx, dr, K, xin, xout, upto=upto)
        if isinstance(fin, tuple):
            b.y(lambda e: wait(e, fin))
    return nc


def prep_block0_inputs(inputs):
    m = dict(make_consts())
    m["inproj"] = np.ascontiguousarray(inputs["L0_attn_in_proj_w"])
    cw = np.asarray(inputs["L0_attn_conv_w"]).reshape(DI, 4)
    m["convw"] = np.ascontiguousarray(
        cw.reshape(DIC, 128, 4).transpose(1, 0, 2).reshape(128, DIC * 4))
    m["convbc"] = _cols(inputs["L0_attn_conv_b"])
    m["xproj"] = np.ascontiguousarray(inputs["L0_attn_x_proj_w"])
    m["dtproj"] = np.ascontiguousarray(inputs["L0_attn_dt_proj_w"])
    m["dtbc"] = _cols(inputs["L0_attn_dt_proj_b"])
    al = np.asarray(inputs["L0_attn_A_log"])
    m["alogc"] = np.ascontiguousarray(
        al.reshape(DIC, 128, NST).transpose(1, 0, 2).reshape(128, DIC * NST))
    m["dcolc"] = _cols(inputs["L0_attn_D"])
    m["outproj"] = np.ascontiguousarray(inputs["L0_attn_out_proj_w"])
    m["fgw"] = np.ascontiguousarray(inputs["L0_attn_fg_w"])
    m["fgbc"] = _cols(inputs["L0_attn_fg_b"])
    m["wq"] = np.ascontiguousarray(inputs["L0_ffn_Wq"])
    m["wk"] = np.ascontiguousarray(inputs["L0_ffn_Wk"])
    m["wv"] = np.ascontiguousarray(inputs["L0_ffn_Wv"])
    m["wo"] = np.ascontiguousarray(inputs["L0_ffn_Wo"])
    m["bqc"] = _cols(inputs["L0_ffn_bq"])
    m["bkc"] = _cols(inputs["L0_ffn_bk"])
    m["bvrow"] = np.ascontiguousarray(np.asarray(inputs["L0_ffn_bv"]).reshape(1, D))
    m["borow"] = np.ascontiguousarray(np.asarray(inputs["L0_ffn_bo"]).reshape(1, D))
    m["g1c"] = _cols(inputs["L0_ffn_g1"])
    m["be1c"] = _cols(inputs["L0_ffn_be1"])
    m["g2c"] = _cols(inputs["L0_ffn_g2"])
    m["be2c"] = _cols(inputs["L0_ffn_be2"])
    m["wg"] = np.ascontiguousarray(inputs["L0_ffn_Wg"])
    m["bgrow"] = np.ascontiguousarray(np.asarray(inputs["L0_ffn_bg"]).reshape(1, 8))
    m["we"] = np.ascontiguousarray(np.asarray(inputs["L0_ffn_We"]).reshape(8 * D, D))
    m["bexprow"] = np.ascontiguousarray(
        np.asarray(inputs["L0_ffn_bexp"]).reshape(1, 8 * D))
    s = np.zeros((16, 16 * 128), dtype=np.float32)
    for n in range(16):
        s[n, n * 128:(n + 1) * 128] = 1.0
    m["sel16"] = s
    return m


def run_block0(x_fm_percore, inputs, upto="full"):
    nc = build_block0(upto=upto)
    base = prep_block0_inputs(inputs)
    in_maps = []
    for core in range(8):
        m = dict(base)
        m["x"] = np.ascontiguousarray(x_fm_percore[core], dtype=np.float32)
        in_maps.append(m)
    res = run_bass_kernel_spmd(nc, in_maps, core_ids=list(range(8)))
    return [res.results[c]["xout"] for c in range(8)]




def build_fused():
    nc = bass.Bass("TRN2", target_bir_lowering=False)
    cs = make_consts()
    drc = {}
    for nm in CONST_NAMES:
        drc[nm] = nc.dram_tensor(nm, list(cs[nm].shape), F32,
                                 kind="ExternalInput").ap()
    xin = nc.dram_tensor("x", [D, T], F32, kind="ExternalInput").ap()
    xout = nc.dram_tensor("xout", [D, T], F32, kind="ExternalOutput").ap()
    xc = [nc.dram_tensor(f"xc{i}", [D, T], F32, kind="Internal").ap()
          for i in range(3)]
    drs = []
    for i, shapes in enumerate([B0_SHAPES, B1_SHAPES, B2_SHAPES, B3_SHAPES]):
        d = dict(drc)
        for nm, shape in shapes:
            d[nm] = nc.dram_tensor(f"b{i}_{nm}", shape, F32,
                                   kind="ExternalInput").ap()
        drs.append(d)

    with ExitStack() as ctx:
        blk = ctx.enter_context(nc.Block())
        b = B(nc, ctx, blk)
        K = ConstTiles(b, drc)
        emits = [emit_block0, emit_block1, emit_block2, emit_block3]
        xi_list = [xin, xc[0], xc[1], xc[2]]
        xo_list = [xc[0], xc[1], xc[2], xout]
        fin = None
        for i in range(4):
            ncp = NCP(nc, f"b{i}_")
            with ExitStack() as bctx:
                fin = emits[i](b.sub(bctx, ncp), bctx, drs[i], K,
                               xi_list[i], xo_list[i])
            block_barrier(b, fin)
        b.y(lambda e: wait(e, fin))
    return nc


def prep_fused_inputs(inputs):
    m = dict(make_consts())
    for i, prep in enumerate([prep_block0_inputs, prep_block1_inputs,
                              prep_block2_inputs, prep_block3_inputs]):
        d = prep(inputs)
        for k, v in d.items():
            if not k.startswith("c_"):
                m[f"b{i}_{k}"] = v
    return m


_FUSED_CACHE = {}


def kernel_fused(**inputs):
    if "nc" not in _FUSED_CACHE:
        _FUSED_CACHE["nc"] = build_fused()
    nc = _FUSED_CACHE["nc"]
    base = prep_fused_inputs(inputs)
    x = np.asarray(inputs["L0_attn_x"])
    in_maps = []
    for core in range(8):
        m = dict(base)
        m["x"] = np.ascontiguousarray(x[core].T, dtype=np.float32)
        in_maps.append(m)
    res = run_bass_kernel_spmd(nc, in_maps, core_ids=list(range(8)))
    return np.stack([np.ascontiguousarray(res.results[c]["xout"].T)
                     for c in range(8)])


def kernel(**inputs):
    return kernel_fused(**inputs)

